# version 10
# baseline (speedup 1.0000x reference)
import numpy as np
import ml_dtypes
import concourse.bass as bass
import concourse.mybir as mybir
from concourse.bass_utils import run_bass_kernel_spmd
from contextlib import ExitStack

F32 = mybir.dt.float32
BF16 = mybir.dt.bfloat16
I32 = mybir.dt.int32
AF = mybir.ActivationFunctionType
ALU = mybir.AluOpType
AX = mybir.AxisListType

L, DM, H, DH, DHID = 4096, 1024, 16, 64, 4096
TILE, NT, C, NCH = 512, 8, 128, 4
BLK = [8 * w + 2 * (j % 4) + j // 4 for w in range(2) for j in range(8)]

ENGS = ("sync", "vector", "scalar", "tensor", "gpsimd")


class Plan:
    def __init__(self):
        self.ev = {e: [] for e in ENGS}
        self.cnt = {}
        self.semh = {}

    def sem(self, name, stack, nc):
        self.semh[name] = stack.enter_context(nc.semaphore(name))
        self.cnt[name] = 0

    def emit(self, eng, fn, inc=None, k=1):
        if inc is not None:
            self.cnt[inc] += k
            sem = self.semh[inc]
            self.ev[eng].append(lambda e, fn=fn, sem=sem, k=k: fn(e).then_inc(sem, k))
        else:
            self.ev[eng].append(fn)

    def dma(self, eng, fn, inc):
        self.emit(eng, fn, inc, 16)

    def vflush(self, fb, inc=None):
        self.emit("vector", lambda e, fb=fb: e.tensor_copy(out=fb[:, 1:2],
                                                           in_=fb[:, 0:1]), inc=inc)

    def sflush(self, fb, inc=None):
        self.emit("scalar", lambda e, fb=fb: e.copy(out=fb[:, 1:2],
                                                    in_=fb[:, 0:1]), inc=inc)

    def inc_last(self, eng, sname, k=1):
        self.cnt[sname] += k
        sem = self.semh[sname]
        prev = self.ev[eng].pop()
        self.ev[eng].append(lambda e, prev=prev, sem=sem, k=k: prev(e).then_inc(sem, k))

    def wait(self, eng, sname, val=None):
        v = self.cnt[sname] if val is None else val
        if v <= 0:
            return
        sem = self.semh[sname]
        self.ev[eng].append(lambda e, sem=sem, v=v: e.wait_ge(sem, v))

    def val(self, sname):
        return self.cnt[sname]


def _slotgrid(ap2d, w, rep, parts=128):
    v = ap2d[0:parts, 8 * w:8 * w + 8]
    v = v.rearrange("p (lo hi) -> p hi lo", hi=2)
    return v.unsqueeze(3).broadcast_to([parts, 2, 4, rep])


def build_nc(debug=False, nt=NT, do_b=True, stage_b=5):
    nc = bass.Bass("TRN2", target_bir_lowering=False, detect_race_conditions=False)
    x_d = nc.dram_tensor("x", [L, DM], F32, kind="ExternalInput").ap()
    y_d = nc.dram_tensor("y", [L, DM], F32, kind="ExternalOutput").ap()
    wqT_d = nc.dram_tensor("wqT", [DM, DM], F32, kind="ExternalInput").ap()
    wkT_d = nc.dram_tensor("wkT", [DM, DM], F32, kind="ExternalInput").ap()
    wvT_d = nc.dram_tensor("wvT", [DM, DM], F32, kind="ExternalInput").ap()
    woT_d = nc.dram_tensor("woT", [DM, DM], F32, kind="ExternalInput").ap()
    wgT_d = nc.dram_tensor("wgT", [DM, 32], F32, kind="ExternalInput").ap()
    bg_d = nc.dram_tensor("bg", [16, 2], F32, kind="ExternalInput").ap()
    w1T_d = nc.dram_tensor("w1T", [DM, DHID], F32, kind="ExternalInput").ap()
    w2T_d = nc.dram_tensor("w2T", [DHID, DM], F32, kind="ExternalInput").ap()
    cw_d = nc.dram_tensor("cw", [128, 96], F32, kind="ExternalInput").ap()
    cb_d = nc.dram_tensor("cb", [128, 32], F32, kind="ExternalInput").ap()
    idbf_d = nc.dram_tensor("idbf", [128, 128], BF16, kind="ExternalInput").ap()
    idf32_d = nc.dram_tensor("idf32", [128, 128], F32, kind="ExternalInput").ap()
    mask4_d = nc.dram_tensor("mask4", [128, 512], BF16, kind="ExternalInput").ap()
    xqs_d = nc.dram_tensor("xqs", [NT, 128, 4096], BF16, kind="Internal").ap()

    dbg = {}
    if debug:
        for nm, shp, dt in [("dbg_xnfm", [128, 4096], BF16), ("dbg_q", [128, 4096], BF16),
                            ("dbg_k", [128, 4096], BF16), ("dbg_v", [128, 1040], BF16),
                            ("dbg_gD", [16, 512], F32), ("dbg_gA", [16, 512], F32),
                            ("dbg_R", [128, 1024], BF16), ("dbg_S", [64, 2048], F32),
                            ("dbg_eff", [128, 1024], BF16), ("dbg_xq", [128, 4096], BF16),
                            ("dbg_w1q", [128, 4096], BF16), ("dbg_hq", [128, 512], BF16),
                            ("dbg_hs", [128, 514], F32)]:
            dbg[nm] = nc.dram_tensor(nm, shp, dt, kind="ExternalOutput").ap()

    P = Plan()
    root = ExitStack()
    with root:
        def psum(name, shape, dt=F32):
            return nc.alloc_psum_tensor(name, shape, dt)

        for s in ("dX", "dW", "dG", "dO", "dQ", "dF", "sv", "ss", "st", "sg", "sy"):
            P.sem(s, root, nc)

        pAcc = psum("pAcc", [128, 1024])
        pSc = psum("pSc", [128, 1024])
        pDk = psum("pDk", [128, 1024])
        pTrF = nc.alloc_psum_tensor("pTr", [128, 512], F32)
        pTr = pTrF[:].bitcast(BF16)
        pMi = psum("pMi", [128, 512])
        cpool = nc.alloc_sbuf_tensor("cpool", [128, 4], F32)
        flshb = nc.alloc_sbuf_tensor("flshb", [128, 2], F32)
        for _i, _v in enumerate((1e-5, 1e-8, 0.5, 0.0)):
            nc.const_aps.register(F32, _v, cpool[:, _i:_i + 1])
        idbf = nc.alloc_sbuf_tensor("idbf_s", [128, 128], BF16)
        idf32 = nc.alloc_sbuf_tensor("idf32_s", [128, 128], F32)
        ones1 = nc.alloc_sbuf_tensor("ones1", [1, 128], F32)
        P.dma("sync", lambda e: e.dma_start(out=idbf[:], in_=idbf_d), "dW")
        P.dma("sync", lambda e: e.dma_start(out=idf32[:], in_=idf32_d), "dW")

        phase_base = nc.sbuf_base

        def sbA(name, shape, dt=F32):
            return nc.alloc_sbuf_tensor(name, shape, dt)

        wq_bf = sbA("wq_bf", [128, 8192], BF16)
        wk_bf = sbA("wk_bf", [128, 8192], BF16)
        wv_bf = sbA("wv_bf", [128, 8192], BF16)
        wo_bf = sbA("wo_bf", [128, 8192], BF16)
        wg_bf = sbA("wg_bf", [128, 256], BF16)
        x_sb = sbA("x_sb", [128, 4096])
        xn_fm = sbA("xn_fm", [128, 4096], BF16)
        xq_fm = sbA("xq_fm", [128, 4096], BF16)
        qfm = sbA("qfm", [128, 4096], BF16)
        kfm = sbA("kfm", [128, 4096], BF16)
        vtm = [sbA(f"vtm{i}", [128, 1040], BF16) for i in range(4)]
        ktm = sbA("ktm", [128, 1024], BF16)
        khat = sbA("khat", [128, 1024], BF16)
        rtm = sbA("rtm", [128, 1024], BF16)
        rfm = sbA("rfm", [128, 1024], BF16)
        xnb = sbA("xnb", [128, 1024], BF16)
        qa = sbA("qa", [128, 512])
        qti = sbA("qti", [128, 512], I32)
        xdq = sbA("xdq", [128, 1024], BF16)
        ebuf = sbA("ebuf", [128, 512])
        stat6 = sbA("stat6", [128, 12])
        mv = sbA("mv", [128, 2])
        colsA = sbA("colsA", [128, 32])
        g_lg = sbA("g_lg", [16, 512])
        g_D = sbA("g_D", [16, 512])
        g_A = sbA("g_A", [16, 512])
        g_t1 = sbA("g_t1", [16, 512])
        g_t2 = sbA("g_t2", [16, 512])
        g_lam = sbA("g_lam", [16, 128])
        d3a = sbA("d3a", [16, 512], BF16)
        d3b = sbA("d3b", [16, 512], BF16)
        d3c = sbA("d3c", [16, 512], BF16)
        d3e = sbA("d3e", [16, 512])
        dfl = sbA("dfl", [3, 2048], BF16)
        atok = sbA("atok", [128, 16])
        editok = sbA("editok", [128, 16])
        wdtok = sbA("wdtok", [128, 16])
        lamsb = sbA("lamsb", [128, 16])
        gegs = sbA("gegs", [16, 128])
        gwgs = sbA("gwgs", [16, 128])
        ptb = sbA("ptb", [128, 1024])
        efb = [sbA(f"efb{i}", [128, 1024], BF16) for i in range(2)]
        tcb = sbA("tcb", [128, 520])
        acb = sbA("acb", [128, 520])
        denb = sbA("denb", [128, 8])
        S_f = sbA("S_f", [64, 2048])
        S_bf = sbA("S_bf", [128, 2048], BF16)
        out_sb = [sbA(f"outsb{i}", [128, 1024]) for i in range(2)]
        mask4 = sbA("mask4_s", [128, 512], BF16)
        ones3 = sbA("ones3", [3, 128], BF16)
        bgc = sbA("bgc_s", [16, 2])
        nbw = sbA("nbw", [16, 1])

        P.dma("sync", lambda e: e.dma_start(out=mask4[:], in_=mask4_d), "dW")
        P.dma("sync", lambda e: e.dma_start(out=bgc[:], in_=bg_d), "dW")
        for dc in range(8):
            for dst, src in ((wq_bf, wqT_d), (wk_bf, wkT_d), (wv_bf, wvT_d), (wo_bf, woT_d)):
                P.dma("gpsimd", lambda e, dst=dst, src=src, dc=dc: e.dma_start(
                    out=dst[:, 1024 * dc:1024 * dc + 1024],
                    in_=src[128 * dc:128 * dc + 128, :]), "dG")
            P.dma("gpsimd", lambda e, dc=dc: e.dma_start(
                out=wg_bf[:, 32 * dc:32 * dc + 32],
                in_=wgT_d[128 * dc:128 * dc + 128, :]), "dG")
        wA = P.val("dW")
        P.wait("vector", "dW", wA)
        P.wait("vector", "dG")
        for _i, _v in enumerate((1e-5, 1e-8, 0.5, 0.0)):
            P.emit("vector", lambda e, _i=_i, _v=_v: e.memset(cpool[:, _i:_i + 1], _v))
        P.emit("vector", lambda e: e.memset(flshb[:], 0.0))
        P.emit("vector", lambda e: e.memset(ones3[:], 1.0))
        P.emit("vector", lambda e: e.memset(ones1[:], 1.0))
        P.emit("vector", lambda e: e.memset(S_f[:], 0.0))
        P.emit("vector", lambda e: e.memset(S_bf[:], 0.0))
        for i in range(4):
            P.emit("vector", lambda e, i=i: e.memset(
                vtm[i][:].rearrange("p (h o) -> p h o", o=65)[:, :, 64:65], 1.0))
        P.emit("vector", lambda e: e.tensor_scalar_mul(out=nbw[:], in0=bgc[:, 0:1], scalar1=-1.0),
               inc="sv")
        initV = P.val("sv")
        laststore = {0: 0, 1: 0}
        vPrevTileDone = 0
        tPrevVcons = 0
        ssrfmLast = 0
        dqStoreLast = 0
        vsupPrev = 0
        vcomb1Prev = 0

        for t in range(nt):
            P.wait("sync", "sv", vPrevTileDone)
            P.dma("sync", lambda e, t=t: e.dma_start(
                out=x_sb[:].rearrange("p (c d) -> p c d", d=DM),
                in_=x_d[TILE * t:TILE * (t + 1), :].rearrange("(c p) d -> p c d", p=128)), "dX")
            xw = P.val("dX")

            P.wait("vector", "dX", xw)
            P.wait("vector", "st", tPrevVcons)
            P.wait("scalar", "dG", dqStoreLast)
            for ch in range(4):
                xc = x_sb[:, 1024 * ch:1024 * ch + 1024]
                cc = colsA[:, 8 * ch:8 * ch + 8]
                P.emit("vector", lambda e, xc=xc: e.tensor_reduce(
                    out=stat6[:, 0:1], in_=xc[:], axis=AX.X, op=ALU.add), inc="sv")
                P.vflush(flshb)
                P.emit("vector", lambda e, cc=cc: e.tensor_single_scalar(
                    out=cc[:, 0:1], in_=stat6[:, 0:1], scalar=1.0 / 1024.0, op=ALU.mult),
                    inc="sv")
                P.wait("scalar", "sv")
                P.emit("scalar", lambda e, xc=xc: e.activation(
                    out=pMi[:, 0:512], in_=xc[:, 0:512], func=AF.Square,
                    accum_out=stat6[:, 1:2]), inc="ss")
                P.wait("scalar", "ss")
                P.emit("scalar", lambda e, xc=xc: e.activation(
                    out=pMi[:, 0:512], in_=xc[:, 512:1024], func=AF.Square,
                    accum_out=stat6[:, 2:3]))
                P.sflush(flshb, inc="ss")
                P.wait("vector", "ss")
                P.emit("vector", lambda e: e.tensor_add(out=stat6[:, 1:2], in0=stat6[:, 1:2],
                                                        in1=stat6[:, 2:3]))
                P.emit("vector", lambda e, cc=cc: e.tensor_mul(out=stat6[:, 3:4],
                                                               in0=cc[:, 0:1], in1=cc[:, 0:1]))
                P.vflush(flshb)
                P.emit("vector", lambda e: e.scalar_tensor_tensor(
                    out=mv[:, 1:2], in0=stat6[:, 1:2], scalar=1.0 / 1024.0,
                    in1=stat6[:, 3:4], op0=ALU.mult, op1=ALU.subtract),
                       inc="sv")
                P.wait("scalar", "sv")
                P.emit("scalar", lambda e: e.activation(out=mv[:, 1:2], in_=mv[:, 1:2],
                                                        func=AF.Ln, bias=1e-5))
                P.emit("scalar", lambda e, cc=cc: e.activation(out=cc[:, 1:2], in_=mv[:, 1:2],
                                                               func=AF.Exp, scale=-0.5),
                       inc="ss")
                P.emit("vector", lambda e, xc=xc, cc=cc: e.tensor_scalar(
                    out=xc[:], in0=xc[:], scalar1=cc[:, 0:1], scalar2=0.0,
                    op0=ALU.subtract, op1=ALU.max, accum_out=cc[:, 2:3]), inc="sv")
                P.emit("vector", lambda e, xc=xc, cc=cc: e.tensor_reduce(
                    out=cc[:, 7:8], in_=xc[:], axis=AX.X, op=ALU.min), inc="sv")
                P.vflush(flshb)
                P.emit("vector", lambda e, cc=cc: e.tensor_scalar_mul(
                    out=cc[:, 7:8], in0=cc[:, 7:8], scalar1=-1.0))
                P.vflush(flshb)
                P.emit("vector", lambda e, cc=cc: e.tensor_max(
                    out=cc[:, 2:3], in0=cc[:, 2:3], in1=cc[:, 7:8]))
                P.wait("vector", "ss")
                P.emit("vector", lambda e, cc=cc: e.tensor_scalar_mul(
                    out=cc[:, 6:7], in0=cc[:, 1:2], scalar1=127.0))
                P.emit("vector", lambda e, cc=cc: e.tensor_mul(out=cc[:, 3:4], in0=cc[:, 2:3],
                                                               in1=cc[:, 1:2]))
                P.vflush(flshb)
                P.emit("vector", lambda e, cc=cc: e.tensor_scalar_max(
                    out=cc[:, 3:4], in0=cc[:, 3:4], scalar1=1e-5))
                P.vflush(flshb)
                P.emit("vector", lambda e, cc=cc: e.tensor_single_scalar(
                    out=cc[:, 4:5], in_=cc[:, 3:4], scalar=1.0 / 127.0, op=ALU.mult))
                P.emit("vector", lambda e, cc=cc: e.reciprocal(out=cc[:, 7:8], in_=cc[:, 3:4]))
                P.vflush(flshb)
                P.emit("vector", lambda e, cc=cc: e.tensor_mul(
                    out=cc[:, 5:6], in0=cc[:, 6:7], in1=cc[:, 7:8]))
                P.emit("vector", lambda e, xc=xc, cc=cc: e.tensor_scalar_mul(
                    out=xnb[:], in0=xc[:], scalar1=cc[:, 1:2]))
                for qh in range(2):
                    qsl = slice(512 * qh, 512 * qh + 512)
                    P.emit("vector", lambda e, xc=xc, cc=cc, qsl=qsl: e.tensor_scalar(
                        out=qa[:], in0=xc[:, qsl], scalar1=cc[:, 5:6], scalar2=0.5,
                        op0=ALU.mult, op1=ALU.add))
                    P.emit("vector", lambda e: e.tensor_copy(out=qti[:], in_=qa[:]))
                    P.emit("vector", lambda e: e.tensor_tensor(out=ebuf[:], in0=qti[:],
                                                               in1=qa[:], op=ALU.is_gt))
                    P.emit("vector", lambda e: e.tensor_tensor(out=qa[:], in0=qti[:],
                                                               in1=ebuf[:], op=ALU.subtract))
                    P.emit("vector", lambda e, cc=cc, qsl=qsl: e.tensor_scalar_mul(
                        out=xdq[:, qsl], in0=qa[:], scalar1=cc[:, 4:5]), inc="sv")
                P.wait("tensor", "sv")
                if t > 0 and ch == 0:
                    P.wait("tensor", "ss", ssrfmLast)
                for dc in range(8):
                    P.emit("tensor", lambda e, dc=dc: e.transpose(
                        out=pTr[:, 128 * dc:128 * dc + 128],
                        in_=xnb[:, 128 * dc:128 * dc + 128], identity=idbf[:]))
                P.inc_last("tensor", "st")
                stxnb = P.val("st")
                P.wait("scalar", "st", stxnb)
                P.emit("scalar", lambda e, ch=ch: e.copy(
                    out=xn_fm[:].rearrange("p (d c) -> p d c", c=512)[:, :, 128 * ch:128 * ch + 128],
                    in_=pTr[:].rearrange("p (d c) -> p d c", c=128)), inc="ss")
                P.wait("tensor", "ss", P.val("ss"))
                for dc in range(8):
                    P.emit("tensor", lambda e, dc=dc: e.transpose(
                        out=pTr[:, 128 * dc:128 * dc + 128],
                        in_=xdq[:, 128 * dc:128 * dc + 128], identity=idbf[:]))
                P.inc_last("tensor", "st")
                stxdq = P.val("st")
                P.wait("scalar", "st", stxdq)
                P.emit("scalar", lambda e, ch=ch: e.copy(
                    out=xq_fm[:].rearrange("p (d c) -> p d c", c=512)[:, :, 128 * ch:128 * ch + 128],
                    in_=pTr[:].rearrange("p (d c) -> p d c", c=128)), inc="ss")
            vxn_ss = P.val("ss")
            P.wait("gpsimd", "ss", vxn_ss)
            P.dma("gpsimd", lambda e, t=t: e.dma_start(out=xqs_d[t], in_=xq_fm[:]), "dG")
            dqStoreLast = P.val("dG")
            if debug and t == 0:
                P.dma("sync", lambda e: e.dma_start(out=dbg["dbg_xnfm"], in_=xn_fm[:]), "dF")
                P.dma("sync", lambda e: e.dma_start(out=dbg["dbg_xq"], in_=xq_fm[:]), "dF")

            P.wait("tensor", "ss", vxn_ss)
            for dc in range(8):
                P.emit("tensor", lambda e, dc=dc: e.matmul(
                    out=pMi[0:16, 0:512], lhsT=wg_bf[:, 32 * dc:32 * dc + 16],
                    rhs=xn_fm[:, 512 * dc:512 * dc + 512], start=(dc == 0), stop=(dc == 7)))
            for dc in range(8):
                P.emit("tensor", lambda e, dc=dc: e.matmul(
                    out=pAcc[0:16, 0:512], lhsT=wg_bf[:, 32 * dc + 16:32 * dc + 32],
                    rhs=xn_fm[:, 512 * dc:512 * dc + 512], start=(dc == 0), stop=(dc == 7)))
            P.inc_last("tensor", "st")
            P.wait("scalar", "st")
            P.wait("scalar", "sv", initV)
            P.emit("scalar", lambda e: e.activation(out=g_t1[:], in_=pMi[0:16, 0:512],
                                                    func=AF.Exp, scale=-1.0, bias=nbw[:, 0:1]))
            P.emit("scalar", lambda e: e.activation(out=g_t2[:], in_=pAcc[0:16, 0:512],
                                                    func=AF.Exp, scale=1.0, bias=bgc[:, 1:2]),
                   inc="ss")
            gatePsumFree = P.val("ss")
            P.wait("vector", "ss")
            P.emit("vector", lambda e: e.tensor_scalar_add(out=g_t1[:], in0=g_t1[:], scalar1=1.0))
            P.emit("vector", lambda e: e.tensor_scalar_add(out=g_t2[:], in0=g_t2[:], scalar1=1.0))
            P.vflush(flshb)
            P.emit("vector", lambda e: e.reciprocal(out=g_t2[:], in_=g_t2[:]))
            P.vflush(flshb, inc="sv")
            P.wait("scalar", "sv")
            P.emit("scalar", lambda e: e.activation(out=g_t1[:], in_=g_t1[:], func=AF.Ln))
            P.emit("scalar", lambda e: e.activation(out=g_lg[:], in_=g_t2[:], func=AF.Ln,
                                                    scale=0.95, bias=1e-8), inc="ss")
            P.wait("vector", "ss")
            for ch in range(4):
                P.emit("vector", lambda e, ch=ch: e.tensor_tensor_scan(
                    out=g_D[:, 128 * ch:128 * ch + 128], data0=g_lg[:, 128 * ch:128 * ch + 128],
                    data1=g_lg[:, 128 * ch:128 * ch + 128], initial=0.0,
                    op0=ALU.add, op1=ALU.bypass), inc="sv")
            P.vflush(flshb)
            P.emit("vector", lambda e: e.scalar_tensor_tensor(
                out=g_A[:], in0=g_t1[:], scalar=-1.0, in1=g_D[:],
                op0=ALU.mult, op1=ALU.subtract))
            P.emit("vector", lambda e: e.tensor_copy(out=d3a[:], in_=g_D[:]))
            P.emit("vector", lambda e: e.tensor_tensor(out=d3e[:], in0=g_D[:], in1=d3a[:],
                                                       op=ALU.subtract))
            P.emit("vector", lambda e: e.tensor_copy(out=d3b[:], in_=d3e[:]))
            P.emit("vector", lambda e: e.tensor_tensor(out=d3e[:], in0=d3e[:], in1=d3b[:],
                                                       op=ALU.subtract))
            P.emit("vector", lambda e: e.tensor_copy(out=d3c[:], in_=d3e[:]), inc="sv")
            vgate = P.val("sv")
            if debug and t == 0:
                P.wait("sync", "sv", vgate)
                P.dma("sync", lambda e: e.dma_start(out=dbg["dbg_gD"], in_=g_D[:]), "dF")
                P.dma("sync", lambda e: e.dma_start(out=dbg["dbg_gA"], in_=g_A[:]), "dF")

            P.wait("tensor", "ss", gatePsumFree)
            banklast = {id(pMi): 0, id(pSc): 0}
            qkit = 0
            for wmat, dst in ((wq_bf, qfm), (wk_bf, kfm)):
                for oc in range(8):
                    pb = pMi if (qkit % 2 == 0) else pSc
                    eb = ebuf if (qkit % 2 == 0) else qa
                    P.wait("tensor", "sv", banklast[id(pb)])
                    for dc in range(8):
                        P.emit("tensor", lambda e, oc=oc, dc=dc, wmat=wmat, pb=pb: e.matmul(
                            out=pb[:, 0:512],
                            lhsT=wmat[:, 1024 * dc + 128 * oc:1024 * dc + 128 * oc + 128],
                            rhs=xn_fm[:, 512 * dc:512 * dc + 512],
                            start=(dc == 0), stop=(dc == 7)))
                    P.inc_last("tensor", "st")
                    stqk = P.val("st")
                    P.wait("scalar", "st", stqk)
                    P.wait("scalar", "sv", banklast[id(pb)])
                    P.emit("scalar", lambda e, pb=pb, eb=eb: e.activation(
                        out=eb[:], in_=pb[:, 0:512], func=AF.Exp), inc="ss")
                    ssqk = P.val("ss")
                    P.wait("vector", "ss", ssqk)
                    P.emit("vector", lambda e, eb=eb: e.tensor_scalar_min(
                        out=eb[:], in0=eb[:], scalar1=1.0))
                    P.emit("vector", lambda e, oc=oc, dst=dst, pb=pb, eb=eb:
                           e.scalar_tensor_tensor(
                        out=dst[:, 512 * oc:512 * oc + 512], in0=pb[:, 0:512], scalar=0.0,
                        in1=eb[:], op0=ALU.max, op1=ALU.add), inc="sv")
                    banklast[id(pb)] = P.val("sv")
                    qkit += 1
            if debug and t == 0:
                P.wait("sync", "sv")
                P.dma("sync", lambda e: e.dma_start(out=dbg["dbg_q"], in_=qfm[:]), "dF")
                P.dma("sync", lambda e: e.dma_start(out=dbg["dbg_k"], in_=kfm[:]), "dF")

            vcop = {}
            for ch in range(4):
                if ch > 0:
                    P.wait("tensor", "ss", vcop[ch - 1])
                else:
                    P.wait("tensor", "sv", vPrevTileDone)
                for dc in range(8):
                    for hf in range(2):
                        P.emit("tensor", lambda e, dc=dc, ch=ch, hf=hf: e.matmul(
                            out=pAcc[:, 512 * hf:512 * hf + 512],
                            lhsT=xn_fm[:, 512 * dc + 128 * ch:512 * dc + 128 * ch + 128],
                            rhs=wv_bf[:, 1024 * dc + 512 * hf:1024 * dc + 512 * hf + 512],
                            start=(dc == 0), stop=(dc == 7)))
                P.inc_last("tensor", "st")
                stv = P.val("st")
                P.wait("scalar", "st", stv)
                P.emit("scalar", lambda e, ch=ch: e.copy(
                    out=vtm[ch][:].rearrange("p (h o) -> p h o", o=65)[:, :, 0:64],
                    in_=pAcc[:].rearrange("p (h o) -> p h o", o=64)), inc="ss")
                vcop[ch] = P.val("ss")
            if debug and t == 0:
                P.wait("sync", "sv")
                P.dma("sync", lambda e: e.dma_start(out=dbg["dbg_v"], in_=vtm[0][:]), "dF")

            for ch in range(4):
                gsl = slice(128 * ch, 128 * ch + 128)
                dend = g_D[:, 128 * ch + 127:128 * ch + 128]
                P.wait("scalar", "sv", vgate)
                P.wait("scalar", "st")
                for i3, s3 in enumerate((d3a, d3b, d3c)):
                    P.dma("scalar", lambda e, i3=i3, s3=s3, gsl=gsl: e.dma_start(
                        out=dfl[i3:i3 + 1, :], in_=s3[:, gsl]), "dF")
                fD = P.val("dF")
                P.emit("scalar", lambda e, gsl=gsl: e.activation(
                    out=gegs[:], in_=g_D[:, gsl], func=AF.Exp))
                P.emit("scalar", lambda e, gsl=gsl, dend=dend: e.activation(
                    out=gwgs[:], in_=g_A[:, gsl], func=AF.Exp, bias=dend), inc="ss")
                sgt = P.val("ss")
                P.wait("vector", "ss", sgt)
                P.emit("vector", lambda e: e.tensor_copy(
                    out=g_lam[:], in_=gegs[:, 127:128].broadcast_to([16, 128])), inc="sv")
                vlam = P.val("sv")
                P.wait("tensor", "sv", vlam)
                P.emit("tensor", lambda e, gsl=gsl: e.transpose(
                    out=pMi[:, 0:16], in_=g_A[:, gsl], identity=idf32[0:16, 0:16]))
                P.emit("tensor", lambda e: e.transpose(out=pMi[:, 16:32], in_=gegs[:],
                                                       identity=idf32[0:16, 0:16]))
                P.emit("tensor", lambda e: e.transpose(out=pMi[:, 32:48], in_=gwgs[:],
                                                       identity=idf32[0:16, 0:16]))
                P.emit("tensor", lambda e: e.transpose(out=pMi[:, 48:64], in_=g_lam[:],
                                                       identity=idf32[0:16, 0:16]), inc="st")
                P.wait("vector", "st")
                P.emit("vector", lambda e: e.tensor_copy(out=atok[:], in_=pMi[:, 0:16]))
                P.emit("vector", lambda e: e.tensor_copy(out=editok[:], in_=pMi[:, 16:32]))
                P.emit("vector", lambda e: e.tensor_copy(out=wdtok[:], in_=pMi[:, 32:48]))
                P.emit("vector", lambda e: e.tensor_copy(out=lamsb[:], in_=pMi[:, 48:64]),
                       inc="sv")
                vtok = P.val("sv")
                for c2 in range(8):
                    P.emit("tensor", lambda e, c2=c2, ch=ch: e.transpose(
                        out=pTr[:, 128 * c2:128 * c2 + 128],
                        in_=kfm[:, 512 * c2 + 128 * ch:512 * c2 + 128 * ch + 128],
                        identity=idbf[:]))
                P.inc_last("tensor", "st")
                stktm = P.val("st")
                P.wait("scalar", "st", stktm)
                P.emit("scalar", lambda e: e.copy(out=ktm[:], in_=pTr[:]), inc="ss")
                ssktm = P.val("ss")
                P.wait("vector", "ss", ssktm)
                P.emit("vector", lambda e: e.tensor_tensor(
                    out=khat[:].rearrange("p (h o) -> p h o", o=64),
                    in0=ktm[:].rearrange("p (h o) -> p h o", o=64),
                    in1=wdtok[:].unsqueeze(2).broadcast_to([128, 16, 64]),
                    op=ALU.mult), inc="sv")
                vkhat = P.val("sv")

                def sc_blk(w, jj, width):
                    if w == 0:
                        return pSc[:, 128 * jj:128 * jj + width]
                    pb_ = pMi if jj < 4 else pTrF
                    return pb_[:, 128 * (jj % 4):128 * (jj % 4) + width]

                veff_w = {}
                vcomb_w = {}
                for w in range(2):
                    dkb = pDk if w == 0 else pAcc
                    if w == 0:
                        P.wait("tensor", "sv", vsupPrev)
                    else:
                        P.wait("tensor", "sv", max(vtok, vcomb1Prev))
                        P.wait("tensor", "ss", ssktm)
                    P.wait("tensor", "dF", fD)
                    for cp in range(4):
                        c = 4 * w + cp
                        for half in range(2):
                            b0, j = 64 * half, cp + 4 * half
                            P.emit("tensor", lambda e, c=c, b0=b0, j=j, ch=ch, w=w: e.matmul(
                                out=sc_blk(w, j, 128),
                                lhsT=kfm[b0:b0 + 64, 512 * c + 128 * ch:512 * c + 128 * ch + 128],
                                rhs=qfm[b0:b0 + 64, 512 * c + 128 * ch:512 * c + 128 * ch + 128],
                                start=True, stop=True))
                    dflv = dfl[:, :].rearrange("k (hw hm par i) -> k hw par hm i",
                                               hw=2, hm=4, par=2, i=128)
                    for half in range(2):
                        P.emit("tensor", lambda e, half=half, dkb=dkb: e.matmul(
                            out=dkb[:, 512 * half:512 * half + 512], lhsT=idbf[:],
                            rhs=mask4[:], start=True, stop=False))
                        P.emit("tensor", lambda e, w=w, half=half, dflv=dflv, dkb=dkb: e.matmul(
                            out=dkb[:, 512 * half:512 * half + 512], lhsT=ones3[:],
                            rhs=dflv[:, w, half, :, :],
                            start=False, stop=True))
                    P.inc_last("tensor", "st")
                    P.wait("scalar", "st")
                    P.wait("scalar", "sv", vtok)
                    if w == 1:
                        P.wait("scalar", "sv", veff_w[0])
                    for j in range(8):
                        sg_ = BLK[8 * w + j]
                        P.emit("scalar", lambda e, j=j, sg_=sg_, dkb=dkb: e.activation(
                            out=ptb[:, 128 * j:128 * j + 128], in_=dkb[:, 128 * j:128 * j + 128],
                            func=AF.Exp, bias=atok[:, sg_:sg_ + 1]))
                    P.emit("scalar", lambda e: e.copy(out=ptb[0:1, 0:1], in_=ptb[0:1, 0:1]),
                           inc="ss")
                    P.wait("vector", "ss")
                    if w == 0:
                        P.emit("vector", lambda e, w=w: e.tensor_tensor(
                            out=efb[w][:], in0=pSc[:], in1=ptb[:], op=ALU.mult), inc="sv")
                    else:
                        P.emit("vector", lambda e, w=w: e.tensor_tensor(
                            out=efb[w][:, 0:512], in0=pMi[:, 0:512], in1=ptb[:, 0:512],
                            op=ALU.mult))
                        P.emit("vector", lambda e, w=w: e.tensor_tensor(
                            out=efb[w][:, 512:1024], in0=pTrF[:, 0:512],
                            in1=ptb[:, 512:1024], op=ALU.mult), inc="sv")
                    veff = P.val("sv")
                    veff_w[w] = veff
                    if debug and t == 0 and ch == 0 and w == 0:
                        P.wait("sync", "sv", veff)
                        P.dma("sync", lambda e: e.dma_start(out=dbg["dbg_eff"], in_=efb[0][:]),
                              "dF")
                    P.wait("tensor", "sv", veff)
                    for j in range(8):
                        b = 8 * w + j
                        P.emit("tensor", lambda e, w=w, j=j, b=b, ch=ch: e.matmul(
                            out=sc_blk(w, j, 65),
                            lhsT=efb[w][:, 128 * j:128 * j + 128],
                            rhs=vtm[ch][:, 65 * b:65 * b + 65], start=True, stop=True))
                    for cp in range(4):
                        c = 4 * w + cp
                        for half in range(2):
                            b0, j = 64 * half, cp + 4 * half
                            b = 8 * w + j
                            P.emit("tensor", lambda e, c=c, b0=b0, j=j, b=b, ch=ch, dkb=dkb:
                                   e.matmul(
                                out=dkb[:, 128 * j:128 * j + 65],
                                lhsT=qfm[b0:b0 + 64, 512 * c + 128 * ch:512 * c + 128 * ch + 128],
                                rhs=S_bf[b0:b0 + 64, 128 * b:128 * b + 65],
                                start=True, stop=True))
                    P.inc_last("tensor", "st")
                    P.wait("vector", "st")
                    P.emit("vector", lambda e, w=w, dkb=dkb: e.tensor_tensor(
                        out=tcb[:].rearrange("p (hi lo o) -> p hi lo o", hi=2, lo=4),
                        in0=dkb[:].rearrange("p (hi lo o) -> p hi lo o",
                                             hi=2, lo=4)[:, :, :, 0:65],
                        in1=_slotgrid(editok, w, 65), op=ALU.mult))
                    if w == 0:
                        P.emit("vector", lambda e: e.tensor_add(
                            out=acb[:].rearrange("p (j o) -> p j o", o=65),
                            in0=tcb[:].rearrange("p (j o) -> p j o", o=65),
                            in1=pSc[:].rearrange("p (j o) -> p j o", o=128)[:, :, 0:65]))
                    else:
                        P.emit("vector", lambda e: e.tensor_add(
                            out=acb[:, 0:260].rearrange("p (j o) -> p j o", o=65),
                            in0=tcb[:, 0:260].rearrange("p (j o) -> p j o", o=65),
                            in1=pMi[:].rearrange("p (j o) -> p j o", o=128)[:, :, 0:65]))
                        P.emit("vector", lambda e: e.tensor_add(
                            out=acb[:, 260:520].rearrange("p (j o) -> p j o", o=65),
                            in0=tcb[:, 260:520].rearrange("p (j o) -> p j o", o=65),
                            in1=pTrF[:].rearrange("p (j o) -> p j o", o=128)[:, :, 0:65]))
                    P.vflush(flshb)
                    P.emit("vector", lambda e: e.tensor_single_scalar(
                        out=denb[:], in_=acb[:].rearrange("p (j o) -> p j o", o=65)[:, :, 64:65],
                        scalar=1e-6, op=ALU.add))
                    P.vflush(flshb)
                    P.emit("vector", lambda e: e.reciprocal(out=denb[:], in_=denb[:]))
                    P.vflush(flshb)
                    P.emit("vector", lambda e, w=w: e.tensor_tensor(
                        out=rtm[:, 512 * w:512 * w + 512].rearrange("p (j o) -> p j o", o=64),
                        in0=acb[:].rearrange("p (j o) -> p j o", o=65)[:, :, 0:64],
                        in1=denb[:].unsqueeze(2).broadcast_to([128, 8, 64]),
                        op=ALU.mult), inc="sv")
                    vcomb_w[w] = P.val("sv")
                vcomb1Prev = vcomb_w[1]
                P.wait("tensor", "sv", vcomb_w[0])
                for b in range(16):
                    hh = BLK[b]
                    pout, j = (pSc, b) if b < 8 else (pDk, b - 8)
                    P.emit("tensor", lambda e, hh=hh, pout=pout, j=j, b=b, ch=ch: e.matmul(
                        out=pout[0:64, 128 * j:128 * j + 65],
                        lhsT=khat[:, 64 * hh:64 * hh + 64],
                        rhs=vtm[ch][:, 65 * b:65 * b + 65], start=True, stop=True))
                P.inc_last("tensor", "st")
                P.wait("vector", "st")
                for w in range(2):
                    pin = pSc if w == 0 else pDk
                    sv_ = S_f[:, 1024 * w:1024 * w + 1024].rearrange(
                        "p (hi lo o) -> p hi lo o", hi=2, lo=4)[:, :, :, 0:65]
                    P.emit("vector", lambda e, sv_=sv_, w=w: e.tensor_tensor(
                        out=sv_, in0=sv_, in1=_slotgrid(lamsb, w, 65, 64), op=ALU.mult))
                    P.emit("vector", lambda e, sv_=sv_, pin=pin: e.tensor_add(
                        out=sv_, in0=sv_,
                        in1=pin[0:64, :].rearrange("p (hi lo o) -> p hi lo o",
                                                   hi=2, lo=4)[:, :, :, 0:65]))
                P.emit("vector", lambda e: e.tensor_copy(out=S_bf[0:64, :], in_=S_f[:]),
                       inc="sv")
                vsup = P.val("sv")
                vsupPrev = vsup
                P.wait("scalar", "sv", vsup)
                P.dma("scalar", lambda e: e.dma_start(out=S_bf[64:128, :], in_=S_bf[0:64, :]),
                      "dF")
                fS = P.val("dF")
                P.wait("tensor", "dF", fS)
                if debug and t == 0 and ch == 0:
                    P.wait("sync", "sv", vsup)
                    P.dma("sync", lambda e: e.dma_start(out=dbg["dbg_S"], in_=S_f[:]), "dF")
                    P.dma("sync", lambda e: e.dma_start(out=dbg["dbg_R"], in_=rtm[:]), "dF")

                P.wait("tensor", "sv", vcomb_w[1])
                for fc in range(8):
                    P.emit("tensor", lambda e, fc=fc: e.transpose(
                        out=pTr[:, 128 * fc:128 * fc + 128], in_=rtm[:, 128 * fc:128 * fc + 128],
                        identity=idbf[:]))
                P.inc_last("tensor", "st")
                strt = P.val("st")
                P.wait("scalar", "st", strt)
                P.emit("scalar", lambda e: e.copy(out=rfm[:], in_=pTr[:]), inc="ss")
                ssrfmLast = P.val("ss")
                P.wait("tensor", "ss", ssrfmLast)
                for fc in range(8):
                    for hf in range(2):
                        P.emit("tensor", lambda e, fc=fc, hf=hf: e.matmul(
                            out=pAcc[:, 512 * hf:512 * hf + 512],
                            lhsT=rfm[:, 128 * fc:128 * fc + 128],
                            rhs=wo_bf[:, 1024 * fc + 512 * hf:1024 * fc + 512 * hf + 512],
                            start=(fc == 0), stop=(fc == 7)))
                P.inc_last("tensor", "st")
                P.wait("vector", "st")
                ob = out_sb[ch % 2]
                P.wait("vector", "dO", laststore[ch % 2])
                P.emit("vector", lambda e, ob=ob, ch=ch: e.scalar_tensor_tensor(
                    out=ob[:], in0=x_sb[:, 1024 * ch:1024 * ch + 1024],
                    scalar=colsA[:, 8 * ch:8 * ch + 1], op0=ALU.add,
                    in1=pAcc[:], op1=ALU.add), inc="sv")
                vout = P.val("sv")
                P.wait("sync", "sv", vout)
                P.dma("sync", lambda e, t=t, ch=ch, ob=ob: e.dma_start(
                    out=y_d[TILE * t + 128 * ch:TILE * t + 128 * ch + 128, :], in_=ob[:]), "dO")
                laststore[ch % 2] = P.val("dO")
            vPrevTileDone = P.val("sv")
            tPrevVcons = P.val("st")

        endA_sv, endA_ss, endA_st = P.val("sv"), P.val("ss"), P.val("st")

        _bcur = [phase_base]
        _boff = {}

        def sbB(name, shape, dt=F32, at=None):
            esz = {F32: 4, BF16: 2, I32: 4}[dt]
            nb = shape[1] * esz
            off = _bcur[0] if at is None else at
            h = nc.alloc_sbuf_tensor_at(name, shape, dt, offset=off)
            _boff[name] = off
            if at is None:
                _bcur[0] = off + ((nb + 31) // 32) * 32
            return h

        w1q = sbB("w1q", [128, 32768], BF16)
        w2q = sbB("w2q", [128, 32768], BF16)
        xqB = [sbB(f"xqB{i}", [128, 4096], BF16) for i in range(2)]
        opB2 = [sbB(f"opB{i}", [128, 1024]) for i in range(2)]
        hsq = sbB("hsq", [128, 16384], BF16)
        hb = [sbB(f"hb{i}", [128, 514]) for i in range(3)]
        hsave = sbB("hsave", [128, 64])
        maccD = sbB("maccD", [128, 512])
        maccN = sbB("maccN", [128, 512])
        s2b = sbB("s2b", [128, 512], F32, at=_boff["maccD"])
        aB = [sbB("aB0", [128, 512])] * 2
        tiB = [sbB("tiB0", [128, 512], I32)] * 2
        convW = sbB("convW", [128, 96])
        convB = sbB("convB", [128, 32])
        colsB = sbB("colsB", [128, 16])
        rowB = sbB("rowB", [1, 512])
        yrow = sbB("yrow", [1, 1024])
        tfB = [sbB("tfB0", [128, 512], F32, at=_boff["rowB"])]
        cB = [sbB("cB0", [128, 512], F32, at=_boff["yrow"])]
        scl = sbB("scl", [1, 8])
        gfb = sbB("gfb", [128, 2])
        epf = sbB("epf", [128, 128], F32, at=_boff["aB0"])
        epb = sbB("epb", [128, 32], BF16, at=_boff["aB0"] + 512)
        epi32 = sbB("epi32", [128, 32], I32, at=_boff["aB0"] + 576)
        wstr = [sbB("wstr0", [128, 4096], F32, at=_boff["hsq"]),
                sbB("wstr1", [128, 4096], F32, at=_boff["hsq"] + 16384)]

        for e in ENGS:
            P.wait(e, "sv", endA_sv)
            P.wait(e, "ss", endA_ss)
            P.wait(e, "st", endA_st)
            P.wait(e, "dO")
            P.wait(e, "dQ")
            P.wait(e, "dF")
            P.wait(e, "dG")
        P.dma("sync", lambda e: e.dma_start(out=convW[:], in_=cw_d), "dW")
        P.dma("sync", lambda e: e.dma_start(out=convB[:], in_=cb_d), "dW")
        wB0 = P.val("dW")
        P.wait("vector", "dW", wB0)
        P.emit("vector", lambda e: e.memset(colsB[:, 3:4], 1.0))
        P.emit("vector", lambda e: e.memset(opB2[0][0:1, :], 0.0))
        P.emit("vector", lambda e: e.memset(hsave[:], 0.0), inc="sv")
        P.emit("gpsimd", lambda e: e.memset(gfb[:], 0.0))

        WQS = (("sync", "dW"), ("scalar", "dQ"), ("gpsimd", "dG"))

        def wquant(src_d, nch, chw, dstq, colix, bufs):
            nb = len(bufs)
            use_done = [0] * nb
            for i in range(nch):
                eng, sem = WQS[i % len(WQS)]
                P.wait(eng, "sv", use_done[i % nb])
                P.dma(eng, lambda e, i=i, src_d=src_d, chw=chw, b=bufs[i % nb]:
                      e.dma_start(out=b[:, 0:chw],
                                  in_=src_d[128 * i:128 * i + 128, :]), sem)
                P.wait("vector", sem)
                P.emit("vector", lambda e, i=i, chw=chw, b=bufs[i % nb]: e.tensor_reduce(
                    out=maccD[:, i % 4:i % 4 + 1], in_=b[:, 0:chw],
                    axis=AX.X, op=ALU.add, apply_absolute_value=True), inc="sv")
                use_done[i % nb] = P.val("sv")
                P.vflush(flshb)
                if i == 0:
                    P.emit("vector", lambda e: e.tensor_copy(
                        out=maccD[:, 4:5], in_=maccD[:, 0:1]), inc="sv")
                else:
                    P.emit("vector", lambda e, i=i: e.tensor_add(
                        out=maccD[:, 4:5], in0=maccD[:, 4:5],
                        in1=maccD[:, i % 4:i % 4 + 1]), inc="sv")
            P.wait("tensor", "sv")
            P.emit("tensor", lambda e: e.matmul(out=pMi[0:1, 0:1], lhsT=maccD[:, 4:5],
                                                rhs=colsB[:, 3:4], start=True, stop=True),
                   inc="st")
            P.wait("vector", "st")
            P.emit("vector", lambda e: e.tensor_single_scalar(
                out=scl[0:1, 0:1], in_=pMi[0:1, 0:1], scalar=1.0 / (4096.0 * 1024.0),
                op=ALU.mult))
            P.vflush(flshb)
            P.emit("vector", lambda e: e.tensor_scalar_max(out=scl[0:1, 0:1],
                                                           in0=scl[0:1, 0:1], scalar1=1e-5))
            P.vflush(flshb)
            P.emit("vector", lambda e: e.reciprocal(out=scl[0:1, 1:2], in_=scl[0:1, 0:1]))
            P.vflush(flshb, inc="sv")
            P.wait("tensor", "sv")
            P.emit("tensor", lambda e: e.matmul(out=pMi[:, 4:5], lhsT=ones1[0:1, :],
                                                rhs=scl[0:1, 1:2], start=True, stop=True),
                   inc="st")
            P.wait("vector", "st")
            P.emit("vector", lambda e, colix=colix: e.tensor_copy(
                out=colsB[:, colix:colix + 1], in_=pMi[:, 4:5]), inc="sv")
            P.vflush(flshb)
            for i in range(nch):
                eng, sem = WQS[i % len(WQS)]
                P.wait(eng, "sv", use_done[i % nb])
                P.dma(eng, lambda e, i=i, src_d=src_d, chw=chw, b=bufs[i % nb]:
                      e.dma_start(out=b[:, 0:chw],
                                  in_=src_d[128 * i:128 * i + 128, :]), sem)
                P.wait("vector", sem)
                wb = bufs[i % nb]
                P.emit("vector", lambda e, wb=wb, chw=chw, colix=colix: e.tensor_scalar(
                    out=wb[:, 0:chw], in0=wb[:, 0:chw], scalar1=colsB[:, colix:colix + 1],
                    scalar2=4.5, op0=ALU.mult, op1=ALU.add))
                for s0 in range(0, chw, 512):
                    P.emit("vector", lambda e, wb=wb, s0=s0: e.tensor_copy(
                        out=tiB[0][:], in_=wb[:, s0:s0 + 512]))
                    P.emit("vector", lambda e, wb=wb, s0=s0: e.tensor_tensor(
                        out=cB[0][:], in0=tiB[0][:], in1=wb[:, s0:s0 + 512], op=ALU.is_gt))
                    P.emit("vector", lambda e, wb=wb, s0=s0: e.tensor_tensor(
                        out=wb[:, s0:s0 + 512], in0=tiB[0][:], in1=cB[0][:], op=ALU.subtract))
                P.emit("vector", lambda e, wb=wb, chw=chw: e.tensor_scalar(
                    out=wb[:, 0:chw], in0=wb[:, 0:chw], scalar1=-4.0, scalar2=1.0,
                    op0=ALU.add, op1=ALU.min))
                P.emit("vector", lambda e, wb=wb, chw=chw, i=i, dstq=dstq: e.tensor_single_scalar(
                    out=dstq[:, chw * i:chw * i + chw], in_=wb[:, 0:chw], scalar=-1.0,
                    op=ALU.max), inc="sv")
                use_done[i % nb] = P.val("sv")

        import os as _os0
        _qv = int(_os0.environ.get("QV", "2"))
        wstr4 = [sbB(f"wst4_{k}", [128, 4096], F32, at=_boff["w2q"] + 16384 * k)
                 for k in range(4)]
        wstr8 = [sbB(f"wst8_{k}", [128, 1024], F32, at=_boff["hsq"] + 4096 * k)
                 for k in range(8)]
        if do_b and _qv >= 1:
            wquant(w1T_d, 8, 4096, w1q, 0, wstr4)
        if do_b and _qv >= 2:
            wquant(w2T_d, 32, 1024, w2q, 1, wstr8)
        P.wait("vector", "sv")
        if not do_b:
            P.emit("vector", lambda e: e.memset(colsB[:], 1.0), inc="sv")
        P.vflush(flshb)
        P.emit("vector", lambda e: e.reciprocal(out=colsB[:, 2:3], in_=colsB[:, 1:2]))
        P.vflush(flshb)
        P.emit("vector", lambda e: e.reciprocal(out=scl[0:1, 3:4], in_=colsB[0:1, 0:1]))
        P.vflush(flshb, inc="sv")
        P.wait("tensor", "sv")
        P.emit("tensor", lambda e: e.matmul(out=pMi[:, 6:7], lhsT=ones1[0:1, :],
                                            rhs=scl[0:1, 3:4], start=True, stop=True), inc="st")
        P.wait("vector", "st")
        P.emit("vector", lambda e: e.tensor_copy(out=colsB[:, 15:16], in_=pMi[:, 6:7]))
        P.vflush(flshb)
        P.emit("vector", lambda e: e.tensor_scalar_mul(out=convW[:], in0=convW[:],
                                                       scalar1=colsB[:, 15:16]), inc="sv")
        if debug:
            P.wait("sync", "sv")
            P.dma("sync", lambda e: e.dma_start(out=dbg["dbg_w1q"], in_=w1q[:, 0:4096]), "dF")

        import os as _osB
        lastout = {0: 0, 1: 0}
        lastopcons = {0: 0, 1: 0}
        lastxqT = {0: 0, 1: 0}

        P.wait("tensor", "sv")
        ssevac = {}
        svconv = {}
        svround = {}
        svstt = {}
        stmacc = {}
        w2banks = [(pAcc, 0), (pAcc, 512), (pSc, 0), (pSc, 512),
                   (pDk, 0), (pDk, 512), (pTrF, 0), (pMi, 0)]
        for t in range(nt if do_b else 0):
            w0 = TILE * t - 1
            xb = xqB[t % 2]
            P.wait("gpsimd", "st", lastxqT[t % 2])
            P.dma("gpsimd", lambda e, t=t, xb=xb: e.dma_start(out=xb[:], in_=xqs_d[t]), "dG")
            ldx = P.val("dG")
            P.wait("tensor", "dG", ldx)
            if t > 0:
                P.wait("tensor", "sv", svstt[(t - 1, 3)])
                P.wait("tensor", "sv", svstt[(t - 1, 1)])

            for hc in range(32):
                slot = hc % 3
                pb = pMi if hc % 2 == 0 else pSc
                if hc >= 2:
                    P.wait("tensor", "ss", ssevac[(t, hc - 2)])
                elif t > 0:
                    P.wait("tensor", "ss", ssevac[(t - 1, 30 + hc)])
                for dc in range(8):
                    P.emit("tensor", lambda e, dc=dc, hc=hc, pb=pb, xb=xb: e.matmul(
                        out=pb[:, 0:512],
                        lhsT=w1q[:, 4096 * dc + 128 * hc:4096 * dc + 128 * hc + 128],
                        rhs=xb[:, 512 * dc:512 * dc + 512], start=(dc == 0), stop=(dc == 7)))
                P.inc_last("tensor", "st")
                stw1 = P.val("st")
                P.wait("scalar", "st", stw1)
                if hc >= 3:
                    P.wait("scalar", "sv", svconv[(t, hc - 3)])
                elif t > 0:
                    P.wait("scalar", "sv", svconv[(t - 1, 29 + hc)])
                P.emit("scalar", lambda e, slot=slot, pb=pb: e.copy(
                    out=hb[slot][:, 2:514], in_=pb[:, 0:512]), inc="ss")
                ssevac[(t, hc)] = P.val("ss")
                P.wait("vector", "ss", ssevac[(t, hc)])
                P.emit("vector", lambda e, slot=slot, hc=hc: e.tensor_copy(
                    out=hb[slot][:, 0:2], in_=hsave[:, 2 * hc:2 * hc + 2]))
                P.emit("vector", lambda e, slot=slot, hc=hc: e.tensor_copy(
                    out=hsave[:, 2 * hc:2 * hc + 2], in_=hb[slot][:, 512:514]))
                P.emit("vector", lambda e, slot=slot, hc=hc: e.tensor_scalar_mul(
                    out=aB[0][:], in0=hb[slot][:, 1:513],
                    scalar1=convW[:, 3 * hc + 1:3 * hc + 2]))
                P.emit("vector", lambda e, slot=slot, hc=hc: e.scalar_tensor_tensor(
                    out=aB[0][:], in0=hb[slot][:, 0:512], scalar=convW[:, 3 * hc:3 * hc + 1],
                    in1=aB[0][:], op0=ALU.mult, op1=ALU.add))
                P.emit("vector", lambda e, slot=slot, hc=hc: e.scalar_tensor_tensor(
                    out=aB[0][:], in0=hb[slot][:, 2:514],
                    scalar=convW[:, 3 * hc + 2:3 * hc + 3],
                    in1=aB[0][:], op0=ALU.mult, op1=ALU.add), inc="sv")
                svconv[(t, hc)] = P.val("sv")
                P.wait("scalar", "sv", svconv[(t, hc)])
                P.emit("scalar", lambda e, hc=hc: e.activation(
                    out=hsq[:, 512 * hc:512 * hc + 512], in_=aB[0][:], func=AF.Silu,
                    bias=convB[:, hc:hc + 1]), inc="ss")
                sssilu = P.val("ss")
                P.wait("vector", "ss", sssilu)
                if hc == 0:
                    if t > 0:
                        P.wait("vector", "st", stmacc[t - 1])
                    P.emit("vector", lambda e, hc=hc: e.tensor_copy(
                        out=maccD[:], in_=hsq[:, 512 * hc:512 * hc + 512]))
                    P.emit("vector", lambda e, hc=hc: e.tensor_copy(
                        out=maccN[:], in_=hsq[:, 512 * hc:512 * hc + 512]), inc="sv")
                else:
                    P.emit("vector", lambda e, hc=hc: e.tensor_tensor(
                        out=maccD[:], in0=maccD[:], in1=hsq[:, 512 * hc:512 * hc + 512],
                        op=ALU.max))
                    P.emit("vector", lambda e, hc=hc: e.tensor_tensor(
                        out=maccN[:], in0=maccN[:], in1=hsq[:, 512 * hc:512 * hc + 512],
                        op=ALU.min), inc="sv")
            lastxqT[t % 2] = P.val("st")
            P.emit("vector", lambda e: e.tensor_scalar_mul(out=maccN[:], in0=maccN[:],
                                                           scalar1=-1.0))
            P.vflush(flshb)
            P.emit("vector", lambda e: e.tensor_max(out=maccD[:], in0=maccD[:],
                                                    in1=maccN[:]), inc="sv")
            svmacc = P.val("sv")

            P.wait("tensor", "sv", svmacc)
            for su in range(4):
                P.emit("tensor", lambda e, su=su: e.transpose(
                    out=pMi[:, 128 * su:128 * su + 128],
                    in_=maccD[:, 128 * su:128 * su + 128], identity=idf32[:]))
            P.inc_last("tensor", "st")
            stmacc[t] = P.val("st")
            P.wait("vector", "st", stmacc[t])
            for su in range(4):
                P.emit("vector", lambda e, su=su: e.reduce_max(
                    out=colsB[:, 4 + su:5 + su], in_=pMi[:, 128 * su:128 * su + 128],
                    axis=AX.X), inc="sv")
            P.vflush(flshb)
            P.emit("vector", lambda e: e.tensor_scalar_max(
                out=colsB[:, 4:8], in0=colsB[:, 4:8], scalar1=1e-5))
            for su in range(4):
                P.vflush(flshb)
                P.emit("vector", lambda e, su=su: e.reciprocal(
                    out=colsB[:, 12 + su:13 + su], in_=colsB[:, 4 + su:5 + su]))
                P.vflush(flshb)
            P.emit("vector", lambda e: e.tensor_scalar_mul(
                out=colsB[:, 12:16], in0=colsB[:, 12:16], scalar1=127.0), inc="sv")
            P.wait("tensor", "sv")
            for su in range(4):
                P.emit("tensor", lambda e, su=su: e.transpose(
                    out=pSc[0:1, 128 * su:128 * su + 128], in_=colsB[:, 12 + su:13 + su],
                    identity=idf32[:]))
            P.inc_last("tensor", "st")
            P.wait("vector", "st")
            P.emit("vector", lambda e: e.tensor_copy(out=rowB[0:1, :], in_=pSc[0:1, 0:512]),
                   inc="sv")
            P.wait("tensor", "sv")
            P.emit("tensor", lambda e: e.matmul(out=pSc[:, 512:1024], lhsT=ones1[0:1, :],
                                                rhs=rowB[0:1, :], start=True, stop=True),
                   inc="st")
            P.wait("vector", "st")
            P.emit("vector", lambda e: e.tensor_copy(out=s2b[:], in_=pSc[:, 512:1024]))
            P.emit("vector", lambda e: e.tensor_scalar(
                out=colsB[:, 8:12], in0=colsB[:, 4:8], scalar1=colsB[:, 2:3],
                scalar2=1.0 / 127.0, op0=ALU.mult, op1=ALU.mult), inc="sv")
            vs2b = P.val("sv")

            P.wait("vector", "sv", vs2b)
            for hc in range(32):
                hsl = hsq[:, 512 * hc:512 * hc + 512]
                P.emit("vector", lambda e, hsl=hsl: e.tensor_tensor(
                    out=tfB[0][:], in0=hsl, in1=s2b[:], op=ALU.mult))
                P.emit("vector", lambda e: e.tensor_single_scalar(
                    out=cB[0][:], in_=tfB[0][:], scalar=0.5, op=ALU.add))
                P.emit("vector", lambda e: e.tensor_copy(
                    out=tiB[0][:], in_=cB[0][:]))
                P.emit("vector", lambda e: e.tensor_tensor(
                    out=tfB[0][:], in0=tiB[0][:], in1=cB[0][:], op=ALU.is_gt))
                P.emit("vector", lambda e, hsl=hsl: e.tensor_tensor(
                    out=hsl, in0=tiB[0][:], in1=tfB[0][:], op=ALU.subtract), inc="sv")
                svround[(t, hc)] = P.val("sv")
                P.wait("tensor", "sv", svround[(t, hc)])
                for su in range(4):
                    for hf in range(2):
                        pbank, off = w2banks[2 * su + hf]
                        P.emit("tensor", lambda e, su=su, hc=hc, hf=hf, pbank=pbank,
                               off=off: e.matmul(
                            out=pbank[:, off:off + 512],
                            lhsT=hsq[:, 512 * hc + 128 * su:512 * hc + 128 * su + 128],
                            rhs=w2q[:, 1024 * hc + 512 * hf:1024 * hc + 512 * hf + 512],
                            start=(hc == 0), stop=(hc == 31)))
            P.inc_last("tensor", "st")
            stw2 = P.val("st")
            for su in range(4):
                obi = su % 2
                r0 = w0 + 128 * su
                P.wait("sync", "sv", lastopcons[obi])
                if r0 < 0:
                    P.dma("sync", lambda e, obi=obi: e.dma_start(
                        out=opB2[obi][1:128, :], in_=y_d[0:127, :]), "dO")
                else:
                    P.dma("sync", lambda e, r0=r0, obi=obi: e.dma_start(
                        out=opB2[obi][:], in_=y_d[r0:r0 + 128, :]), "dO")
                ldop = P.val("dO")
                P.wait("vector", "st", stw2)
                P.wait("vector", "dO", max(ldop, lastout[obi]))
                b0, o0 = w2banks[2 * su]
                b1, o1 = w2banks[2 * su + 1]
                P.emit("vector", lambda e, su=su, obi=obi, b0=b0, o0=o0: e.scalar_tensor_tensor(
                    out=opB2[obi][:, 0:512], in0=b0[:, o0:o0 + 512],
                    scalar=colsB[:, 8 + su:9 + su],
                    op0=ALU.mult, in1=opB2[obi][:, 0:512], op1=ALU.add))
                P.emit("vector", lambda e, su=su, obi=obi, b1=b1, o1=o1: e.scalar_tensor_tensor(
                    out=opB2[obi][:, 512:1024], in0=b1[:, o1:o1 + 512],
                    scalar=colsB[:, 8 + su:9 + su],
                    op0=ALU.mult, in1=opB2[obi][:, 512:1024], op1=ALU.add),
                    inc="sv")
                svstt[(t, su)] = P.val("sv")
                lastopcons[obi] = P.val("sv")
                vob = P.val("sv")
                P.wait("sync", "sv", vob)
                if r0 < 0:
                    P.dma("sync", lambda e, obi=obi: e.dma_start(
                        out=y_d[0:127, :], in_=opB2[obi][1:128, :]), "dO")
                else:
                    P.dma("sync", lambda e, r0=r0, obi=obi: e.dma_start(
                        out=y_d[r0:r0 + 128, :], in_=opB2[obi][:]), "dO")
                lastout[obi] = P.val("dO")

        for _pn in range(int(_osB.environ.get("PADN", "0"))):
            P.emit("tensor", lambda e: e.nop())
        import os as _os2
        do_epi = _os2.environ.get("EPI", "1") == "1" and do_b
        cw3 = convW[:].rearrange("p (c k) -> p c k", k=3)
        if do_epi:
            hs2 = hsave[:].rearrange("p (c k) -> p c k", k=2)
            P.wait("vector", "sv")
            P.emit("vector", lambda e: e.tensor_tensor(
                out=epf[:, 0:32].unsqueeze(2), in0=hs2[:, :, 0:1],
                in1=cw3[:, :, 0:1], op=ALU.mult))
            P.emit("vector", lambda e: e.tensor_tensor(
                out=epf[:, 32:64].unsqueeze(2), in0=hs2[:, :, 1:2],
                in1=cw3[:, :, 1:2], op=ALU.mult))
            P.vflush(flshb)
            P.emit("vector", lambda e: e.tensor_add(out=epf[:, 0:32], in0=epf[:, 0:32],
                                                    in1=epf[:, 32:64]))
            P.vflush(flshb)
            P.emit("vector", lambda e: e.tensor_add(out=epf[:, 0:32], in0=epf[:, 0:32],
                                                    in1=convB[:]), inc="sv")
            P.wait("scalar", "sv")
            P.emit("scalar", lambda e: e.activation(out=epf[:, 32:64], in_=epf[:, 0:32],
                                                    func=AF.Silu), inc="ss")
            P.wait("vector", "ss")
            P.emit("vector", lambda e: e.reduce_max(out=epf[:, 64:65], in_=epf[:, 32:64],
                                                    axis=AX.X, apply_absolute_value=True),
                   inc="sv")
            P.vflush(flshb)
            P.wait("tensor", "sv")
            P.emit("tensor", lambda e: e.transpose(out=pMi[0:1, 0:128], in_=epf[:, 64:65],
                                                   identity=idf32[:]), inc="st")
            P.wait("vector", "st")
            P.emit("vector", lambda e: e.reduce_max(out=scl[0:1, 4:5], in_=pMi[0:1, 0:128],
                                                    axis=AX.X), inc="sv")
            P.vflush(flshb)
            P.emit("vector", lambda e: e.tensor_scalar_max(out=scl[0:1, 4:5], in0=scl[0:1, 4:5],
                                                           scalar1=1e-5))
            P.vflush(flshb)
            P.emit("vector", lambda e: e.reciprocal(out=scl[0:1, 5:6], in_=scl[0:1, 4:5]))
            P.vflush(flshb)
            P.emit("vector", lambda e: e.tensor_single_scalar(out=scl[0:1, 5:6], in_=scl[0:1, 5:6],
                                                              scalar=127.0, op=ALU.mult))
            P.emit("vector", lambda e: e.tensor_single_scalar(out=scl[0:1, 6:7], in_=scl[0:1, 4:5],
                                                              scalar=1.0 / 127.0, op=ALU.mult))
            P.vflush(flshb)
            P.emit("vector", lambda e: e.tensor_mul(out=scl[0:1, 6:7], in0=scl[0:1, 6:7],
                                                    in1=colsB[0:1, 2:3]), inc="sv")
            P.wait("tensor", "sv")
            P.emit("tensor", lambda e: e.matmul(out=pMi[:, 2:3], lhsT=ones1[0:1, :],
                                                rhs=scl[0:1, 5:6], start=True, stop=True),
                   inc="st")
            P.wait("vector", "st")
            P.emit("vector", lambda e: e.tensor_copy(out=colsB[:, 12:13], in_=pMi[:, 2:3]))
            P.vflush(flshb)
            P.emit("vector", lambda e: e.tensor_scalar(
                out=epf[:, 64:96], in0=epf[:, 32:64], scalar1=colsB[:, 12:13], scalar2=0.5,
                op0=ALU.mult, op1=ALU.add))
            P.vflush(flshb)
            P.emit("vector", lambda e: e.tensor_copy(out=epi32[:], in_=epf[:, 64:96]))
            P.vflush(flshb)
            P.emit("vector", lambda e: e.tensor_tensor(out=epf[:, 96:128], in0=epi32[:],
                                                       in1=epf[:, 64:96], op=ALU.is_gt))
            P.vflush(flshb)
            P.emit("vector", lambda e: e.tensor_tensor(out=epb[:], in0=epi32[:],
                                                       in1=epf[:, 96:128], op=ALU.subtract),
                   inc="sv")
            vepq = P.val("sv")
            P.dma("sync", lambda e: e.dma_start(out=yrow[0:1, :], in_=y_d[4095:4096, :]), "dO")
            ldyr = P.val("dO")
            P.wait("tensor", "sv", vepq)
            for oc in range(8):
                for hc in range(32):
                    P.emit("tensor", lambda e, hc=hc, oc=oc: e.matmul(
                        out=pAcc[:, oc:oc + 1],
                        lhsT=w2q[:, 1024 * hc + 128 * oc:1024 * hc + 128 * oc + 128],
                        rhs=epb[:, hc:hc + 1], start=(hc == 0), stop=(hc == 31)))
            P.inc_last("tensor", "st")
            P.wait("vector", "st")
            P.emit("vector", lambda e: e.tensor_copy(out=epf[:, 0:8], in_=pAcc[:, 0:8]),
                   inc="sv")
            P.wait("tensor", "sv")
            P.emit("tensor", lambda e: e.transpose(out=pMi[0:8, 0:128], in_=epf[:, 0:8],
                                                   identity=idf32[:]), inc="st")
            P.wait("vector", "st")
            P.emit("vector", lambda e: e.tensor_copy(out=epf[0:8, 0:128],
                                                     in_=pMi[0:8, 0:128]), inc="sv")
            P.wait("scalar", "sv")
            P.dma("scalar", lambda e: e.dma_start(out=opB2[1][0:1, 0:1024],
                                                  in_=epf[0:8, 0:128]), "dF")
            ldfl = P.val("dF")
            P.wait("vector", "dF", ldfl)
            P.wait("vector", "dO", ldyr)
            P.emit("vector", lambda e: e.scalar_tensor_tensor(
                out=yrow[0:1, :], in0=opB2[1][0:1, 0:1024], scalar=scl[0:1, 6:7],
                op0=ALU.mult, in1=yrow[0:1, :], op1=ALU.add), inc="sv")
            P.wait("sync", "sv")
            P.dma("sync", lambda e: e.dma_start(out=y_d[4095:4096, :], in_=yrow[0:1, :]), "dO")
        P.wait("sync", "dO")
        P.wait("sync", "dX")
        P.wait("sync", "dW")
        P.wait("sync", "dQ")
        P.wait("sync", "dF")

        with nc.Block() as block:
            @block.sync
            def _(e):
                for fn in P.ev["sync"]:
                    fn(e)

            @block.vector
            def _(e):
                for fn in P.ev["vector"]:
                    fn(e)

            @block.scalar
            def _(e):
                for fn in P.ev["scalar"]:
                    fn(e)

            @block.tensor
            def _(e):
                for fn in P.ev["tensor"]:
                    fn(e)

            @block.gpsimd
            def _(e):
                for fn in P.ev["gpsimd"]:
                    fn(e)

    return nc


def _host_inputs(inputs):
    f32 = np.float32
    x = np.ascontiguousarray(inputs["L0_attn_x"], dtype=f32)
    perm = np.concatenate([np.arange(64 * b, 64 * b + 64) for b in BLK])
    base = {
        "wqT": np.ascontiguousarray(inputs["L0_attn_Wq"].T, dtype=f32),
        "wkT": np.ascontiguousarray(inputs["L0_attn_Wk"].T, dtype=f32),
        "wvT": np.ascontiguousarray(inputs["L0_attn_Wv"].T[:, perm], dtype=f32),
        "woT": np.ascontiguousarray(inputs["L0_attn_Wo"].T[perm, :], dtype=f32),
        "wgT": np.ascontiguousarray(
            np.concatenate([inputs["L0_attn_Ww"].T, inputs["L0_attn_We"].T], axis=1), dtype=f32),
        "bg": np.ascontiguousarray(
            np.stack([inputs["L0_attn_bw"], inputs["L0_attn_be"]], axis=1), dtype=f32),
        "w1T": np.ascontiguousarray(inputs["L0_ffn_w1"].T, dtype=f32),
        "w2T": np.ascontiguousarray(inputs["L0_ffn_w2"].T, dtype=f32),
        "cw": np.ascontiguousarray(
            inputs["L0_ffn_conv_w"].reshape(32, 128, 3).transpose(1, 0, 2).reshape(128, 96),
            dtype=f32),
        "cb": np.ascontiguousarray(
            inputs["L0_ffn_conv_b"].reshape(32, 128).T, dtype=f32),
        "idbf": np.eye(128, dtype=ml_dtypes.bfloat16),
        "idf32": np.eye(128, dtype=f32),
        "mask4": np.ascontiguousarray(
            np.tile(np.tril(np.full((128, 128), -1.0e30, dtype=f32), -1), (1, 4))
            .astype(ml_dtypes.bfloat16)),
    }
    return x, base


def kernel(**inputs):
    x, base = _host_inputs(inputs)
    nc = build_nc(debug=False)
    in_maps = [dict(base, x=np.ascontiguousarray(x[c])) for c in range(8)]
    res = run_bass_kernel_spmd(nc, in_maps, core_ids=list(range(8)))
    out = np.stack([np.asarray(r["y"]) for r in res.results], axis=0)
    return out.astype(np.float32)


# version 11
# speedup vs baseline: 1.0272x; 1.0272x over previous
import numpy as np
import ml_dtypes
import concourse.bass as bass
import concourse.mybir as mybir
from concourse.bass_utils import run_bass_kernel_spmd
from contextlib import ExitStack

F32 = mybir.dt.float32
BF16 = mybir.dt.bfloat16
I32 = mybir.dt.int32
AF = mybir.ActivationFunctionType
ALU = mybir.AluOpType
AX = mybir.AxisListType

L, DM, H, DH, DHID = 4096, 1024, 16, 64, 4096
TILE, NT, C, NCH = 512, 8, 128, 4
BLK = [8 * w + 2 * (j % 4) + j // 4 for w in range(2) for j in range(8)]

ENGS = ("sync", "vector", "scalar", "tensor", "gpsimd")


class Plan:
    def __init__(self):
        self.ev = {e: [] for e in ENGS}
        self.cnt = {}
        self.semh = {}

    def sem(self, name, stack, nc):
        self.semh[name] = stack.enter_context(nc.semaphore(name))
        self.cnt[name] = 0

    def emit(self, eng, fn, inc=None, k=1):
        if inc is not None:
            self.cnt[inc] += k
            sem = self.semh[inc]
            self.ev[eng].append(lambda e, fn=fn, sem=sem, k=k: fn(e).then_inc(sem, k))
        else:
            self.ev[eng].append(fn)

    def dma(self, eng, fn, inc):
        self.emit(eng, fn, inc, 16)

    def vflush(self, fb, inc=None):
        self.emit("vector", lambda e, fb=fb: e.tensor_copy(out=fb[:, 1:2],
                                                           in_=fb[:, 0:1]), inc=inc)

    def sflush(self, fb, inc=None):
        self.emit("scalar", lambda e, fb=fb: e.copy(out=fb[:, 1:2],
                                                    in_=fb[:, 0:1]), inc=inc)

    def inc_last(self, eng, sname, k=1):
        self.cnt[sname] += k
        sem = self.semh[sname]
        prev = self.ev[eng].pop()
        self.ev[eng].append(lambda e, prev=prev, sem=sem, k=k: prev(e).then_inc(sem, k))

    def wait(self, eng, sname, val=None):
        v = self.cnt[sname] if val is None else val
        if v <= 0:
            return
        sem = self.semh[sname]
        self.ev[eng].append(lambda e, sem=sem, v=v: e.wait_ge(sem, v))

    def val(self, sname):
        return self.cnt[sname]


def _slotgrid(ap2d, w, rep, parts=128):
    v = ap2d[0:parts, 8 * w:8 * w + 8]
    v = v.rearrange("p (lo hi) -> p hi lo", hi=2)
    return v.unsqueeze(3).broadcast_to([parts, 2, 4, rep])


def build_nc(debug=False, nt=NT, do_b=True, stage_b=5):
    nc = bass.Bass("TRN2", target_bir_lowering=False, detect_race_conditions=False)
    x_d = nc.dram_tensor("x", [L, DM], F32, kind="ExternalInput").ap()
    y_d = nc.dram_tensor("y", [L, DM], F32, kind="ExternalOutput").ap()
    wqT_d = nc.dram_tensor("wqT", [DM, DM], F32, kind="ExternalInput").ap()
    wkT_d = nc.dram_tensor("wkT", [DM, DM], F32, kind="ExternalInput").ap()
    wvT_d = nc.dram_tensor("wvT", [DM, DM], F32, kind="ExternalInput").ap()
    woT_d = nc.dram_tensor("woT", [DM, DM], F32, kind="ExternalInput").ap()
    wgT_d = nc.dram_tensor("wgT", [DM, 32], F32, kind="ExternalInput").ap()
    bg_d = nc.dram_tensor("bg", [16, 2], F32, kind="ExternalInput").ap()
    w1T_d = nc.dram_tensor("w1T", [DM, DHID], F32, kind="ExternalInput").ap()
    w2T_d = nc.dram_tensor("w2T", [DHID, DM], F32, kind="ExternalInput").ap()
    cw_d = nc.dram_tensor("cw", [128, 96], F32, kind="ExternalInput").ap()
    cb_d = nc.dram_tensor("cb", [128, 32], F32, kind="ExternalInput").ap()
    idbf_d = nc.dram_tensor("idbf", [128, 128], BF16, kind="ExternalInput").ap()
    idf32_d = nc.dram_tensor("idf32", [128, 128], F32, kind="ExternalInput").ap()
    mask4_d = nc.dram_tensor("mask4", [128, 512], BF16, kind="ExternalInput").ap()
    xqs_d = nc.dram_tensor("xqs", [NT, 128, 4096], BF16, kind="Internal").ap()

    dbg = {}
    if debug:
        for nm, shp, dt in [("dbg_xnfm", [128, 4096], BF16), ("dbg_q", [128, 4096], BF16),
                            ("dbg_k", [128, 4096], BF16), ("dbg_v", [128, 1040], BF16),
                            ("dbg_gD", [16, 512], F32), ("dbg_gA", [16, 512], F32),
                            ("dbg_R", [128, 1024], BF16), ("dbg_S", [64, 2048], F32),
                            ("dbg_eff", [128, 1024], BF16), ("dbg_xq", [128, 4096], BF16),
                            ("dbg_w1q", [128, 4096], BF16), ("dbg_hq", [128, 512], BF16),
                            ("dbg_hs", [128, 514], F32)]:
            dbg[nm] = nc.dram_tensor(nm, shp, dt, kind="ExternalOutput").ap()

    P = Plan()
    root = ExitStack()
    with root:
        def psum(name, shape, dt=F32):
            return nc.alloc_psum_tensor(name, shape, dt)

        for s in ("dX", "dW", "dG", "dO", "dQ", "dF", "sv", "ss", "st", "sg", "sy"):
            P.sem(s, root, nc)

        pAcc = psum("pAcc", [128, 1024])
        pSc = psum("pSc", [128, 1024])
        pDk = psum("pDk", [128, 1024])
        pTrF = nc.alloc_psum_tensor("pTr", [128, 512], F32)
        pTr = pTrF[:].bitcast(BF16)
        pMi = psum("pMi", [128, 512])
        cpool = nc.alloc_sbuf_tensor("cpool", [128, 4], F32)
        flshb = nc.alloc_sbuf_tensor("flshb", [128, 2], F32)
        for _i, _v in enumerate((1e-5, 1e-8, 0.5, 0.0)):
            nc.const_aps.register(F32, _v, cpool[:, _i:_i + 1])
        idbf = nc.alloc_sbuf_tensor("idbf_s", [128, 128], BF16)
        idf32 = nc.alloc_sbuf_tensor("idf32_s", [128, 128], F32)
        ones1 = nc.alloc_sbuf_tensor("ones1", [1, 128], F32)
        P.dma("sync", lambda e: e.dma_start(out=idbf[:], in_=idbf_d), "dW")
        P.dma("sync", lambda e: e.dma_start(out=idf32[:], in_=idf32_d), "dW")

        phase_base = nc.sbuf_base

        def sbA(name, shape, dt=F32):
            return nc.alloc_sbuf_tensor(name, shape, dt)

        wq_bf = sbA("wq_bf", [128, 8192], BF16)
        wk_bf = sbA("wk_bf", [128, 8192], BF16)
        wv_bf = sbA("wv_bf", [128, 8192], BF16)
        wo_bf = sbA("wo_bf", [128, 8192], BF16)
        wg_bf = sbA("wg_bf", [128, 256], BF16)
        x_sb = sbA("x_sb", [128, 4096])
        xn_fm = sbA("xn_fm", [128, 4096], BF16)
        xq_fm = sbA("xq_fm", [128, 4096], BF16)
        qfm = sbA("qfm", [128, 4096], BF16)
        kfm = sbA("kfm", [128, 4096], BF16)
        vtm = [sbA(f"vtm{i}", [128, 1040], BF16) for i in range(4)]
        ktm = sbA("ktm", [128, 1024], BF16)
        khat = sbA("khat", [128, 1024], BF16)
        rtm = sbA("rtm", [128, 1024], BF16)
        rfm = sbA("rfm", [128, 1024], BF16)
        xnb = sbA("xnb", [128, 1024], BF16)
        qa = sbA("qa", [128, 512])
        qti = sbA("qti", [128, 512], I32)
        xdq = sbA("xdq", [128, 1024], BF16)
        ebuf = sbA("ebuf", [128, 512])
        stat6 = sbA("stat6", [128, 12])
        mv = sbA("mv", [128, 2])
        colsA = sbA("colsA", [128, 32])
        g_lg = sbA("g_lg", [16, 512])
        g_D = sbA("g_D", [16, 512])
        g_A = sbA("g_A", [16, 512])
        g_t1 = sbA("g_t1", [16, 512])
        g_t2 = sbA("g_t2", [16, 512])
        g_lam = sbA("g_lam", [16, 128])
        d3a = sbA("d3a", [16, 512], BF16)
        d3b = sbA("d3b", [16, 512], BF16)
        d3c = sbA("d3c", [16, 512], BF16)
        d3e = sbA("d3e", [16, 512])
        dfl = sbA("dfl", [3, 2048], BF16)
        atok = sbA("atok", [128, 16])
        editok = sbA("editok", [128, 16])
        wdtok = sbA("wdtok", [128, 16])
        lamsb = sbA("lamsb", [128, 16])
        gegs = sbA("gegs", [16, 128])
        gwgs = sbA("gwgs", [16, 128])
        ptb = sbA("ptb", [128, 1024])
        efb = [sbA(f"efb{i}", [128, 1024], BF16) for i in range(2)]
        tcb = sbA("tcb", [128, 520])
        acb = sbA("acb", [128, 520])
        denb = sbA("denb", [128, 8])
        S_f = sbA("S_f", [64, 2048])
        S_bf = sbA("S_bf", [128, 2048], BF16)
        out_sb = [sbA(f"outsb{i}", [128, 1024]) for i in range(2)]
        mask4 = sbA("mask4_s", [128, 512], BF16)
        ones3 = sbA("ones3", [3, 128], BF16)
        bgc = sbA("bgc_s", [16, 2])
        nbw = sbA("nbw", [16, 1])

        P.dma("sync", lambda e: e.dma_start(out=mask4[:], in_=mask4_d), "dW")
        P.dma("sync", lambda e: e.dma_start(out=bgc[:], in_=bg_d), "dW")
        for dc in range(8):
            for dst, src in ((wq_bf, wqT_d), (wk_bf, wkT_d), (wv_bf, wvT_d), (wo_bf, woT_d)):
                P.dma("gpsimd", lambda e, dst=dst, src=src, dc=dc: e.dma_start(
                    out=dst[:, 1024 * dc:1024 * dc + 1024],
                    in_=src[128 * dc:128 * dc + 128, :]), "dG")
            P.dma("gpsimd", lambda e, dc=dc: e.dma_start(
                out=wg_bf[:, 32 * dc:32 * dc + 32],
                in_=wgT_d[128 * dc:128 * dc + 128, :]), "dG")
        wA = P.val("dW")
        P.wait("vector", "dW", wA)
        P.wait("vector", "dG")
        for _i, _v in enumerate((1e-5, 1e-8, 0.5, 0.0)):
            P.emit("vector", lambda e, _i=_i, _v=_v: e.memset(cpool[:, _i:_i + 1], _v))
        P.emit("vector", lambda e: e.memset(flshb[:], 0.0))
        P.emit("vector", lambda e: e.memset(ones3[:], 1.0))
        P.emit("vector", lambda e: e.memset(ones1[:], 1.0))
        P.emit("vector", lambda e: e.memset(S_f[:], 0.0))
        P.emit("vector", lambda e: e.memset(S_bf[:], 0.0))
        for i in range(4):
            P.emit("vector", lambda e, i=i: e.memset(
                vtm[i][:].rearrange("p (h o) -> p h o", o=65)[:, :, 64:65], 1.0))
        P.emit("vector", lambda e: e.tensor_scalar_mul(out=nbw[:], in0=bgc[:, 0:1], scalar1=-1.0),
               inc="sv")
        initV = P.val("sv")
        laststore = {0: 0, 1: 0}
        vPrevTileDone = 0
        tPrevVcons = 0
        ssrfmLast = 0
        dqStoreLast = 0
        vsupPrev = 0
        vcomb1Prev = 0

        for t in range(nt):
            P.wait("sync", "sv", vPrevTileDone)
            P.dma("sync", lambda e, t=t: e.dma_start(
                out=x_sb[:].rearrange("p (c d) -> p c d", d=DM),
                in_=x_d[TILE * t:TILE * (t + 1), :].rearrange("(c p) d -> p c d", p=128)), "dX")
            xw = P.val("dX")

            P.wait("vector", "dX", xw)
            P.wait("vector", "st", tPrevVcons)
            P.wait("scalar", "dG", dqStoreLast)
            for ch in range(4):
                xc = x_sb[:, 1024 * ch:1024 * ch + 1024]
                cc = colsA[:, 8 * ch:8 * ch + 8]
                P.emit("vector", lambda e, xc=xc: e.tensor_reduce(
                    out=stat6[:, 0:1], in_=xc[:], axis=AX.X, op=ALU.add), inc="sv")
                P.vflush(flshb)
                P.emit("vector", lambda e, cc=cc: e.tensor_single_scalar(
                    out=cc[:, 0:1], in_=stat6[:, 0:1], scalar=1.0 / 1024.0, op=ALU.mult),
                    inc="sv")
                P.wait("scalar", "sv")
                P.emit("scalar", lambda e, xc=xc: e.activation(
                    out=pMi[:, 0:512], in_=xc[:, 0:512], func=AF.Square,
                    accum_out=stat6[:, 1:2]), inc="ss")
                P.wait("scalar", "ss")
                P.emit("scalar", lambda e, xc=xc: e.activation(
                    out=pMi[:, 0:512], in_=xc[:, 512:1024], func=AF.Square,
                    accum_out=stat6[:, 2:3]))
                P.sflush(flshb, inc="ss")
                P.wait("vector", "ss")
                P.emit("vector", lambda e: e.tensor_add(out=stat6[:, 1:2], in0=stat6[:, 1:2],
                                                        in1=stat6[:, 2:3]))
                P.emit("vector", lambda e, cc=cc: e.tensor_mul(out=stat6[:, 3:4],
                                                               in0=cc[:, 0:1], in1=cc[:, 0:1]))
                P.vflush(flshb)
                P.emit("vector", lambda e: e.scalar_tensor_tensor(
                    out=mv[:, 1:2], in0=stat6[:, 1:2], scalar=1.0 / 1024.0,
                    in1=stat6[:, 3:4], op0=ALU.mult, op1=ALU.subtract),
                       inc="sv")
                P.wait("scalar", "sv")
                P.emit("scalar", lambda e: e.activation(out=mv[:, 1:2], in_=mv[:, 1:2],
                                                        func=AF.Ln, bias=1e-5))
                P.emit("scalar", lambda e, cc=cc: e.activation(out=cc[:, 1:2], in_=mv[:, 1:2],
                                                               func=AF.Exp, scale=-0.5),
                       inc="ss")
                P.emit("vector", lambda e, xc=xc, cc=cc: e.tensor_scalar(
                    out=xc[:], in0=xc[:], scalar1=cc[:, 0:1], scalar2=0.0,
                    op0=ALU.subtract, op1=ALU.max, accum_out=cc[:, 2:3]), inc="sv")
                P.emit("vector", lambda e, xc=xc, cc=cc: e.tensor_reduce(
                    out=cc[:, 7:8], in_=xc[:], axis=AX.X, op=ALU.min), inc="sv")
                P.vflush(flshb)
                P.emit("vector", lambda e, cc=cc: e.tensor_scalar_mul(
                    out=cc[:, 7:8], in0=cc[:, 7:8], scalar1=-1.0))
                P.vflush(flshb)
                P.emit("vector", lambda e, cc=cc: e.tensor_max(
                    out=cc[:, 2:3], in0=cc[:, 2:3], in1=cc[:, 7:8]))
                P.wait("vector", "ss")
                P.emit("vector", lambda e, cc=cc: e.tensor_scalar_mul(
                    out=cc[:, 6:7], in0=cc[:, 1:2], scalar1=127.0))
                P.emit("vector", lambda e, cc=cc: e.tensor_mul(out=cc[:, 3:4], in0=cc[:, 2:3],
                                                               in1=cc[:, 1:2]))
                P.vflush(flshb)
                P.emit("vector", lambda e, cc=cc: e.tensor_scalar_max(
                    out=cc[:, 3:4], in0=cc[:, 3:4], scalar1=1e-5))
                P.vflush(flshb)
                P.emit("vector", lambda e, cc=cc: e.tensor_single_scalar(
                    out=cc[:, 4:5], in_=cc[:, 3:4], scalar=1.0 / 127.0, op=ALU.mult))
                P.emit("vector", lambda e, cc=cc: e.reciprocal(out=cc[:, 7:8], in_=cc[:, 3:4]))
                P.vflush(flshb)
                P.emit("vector", lambda e, cc=cc: e.tensor_mul(
                    out=cc[:, 5:6], in0=cc[:, 6:7], in1=cc[:, 7:8]))
                P.emit("vector", lambda e, xc=xc, cc=cc: e.tensor_scalar_mul(
                    out=xnb[:], in0=xc[:], scalar1=cc[:, 1:2]))
                for qh in range(2):
                    qsl = slice(512 * qh, 512 * qh + 512)
                    P.emit("vector", lambda e, xc=xc, cc=cc, qsl=qsl: e.tensor_scalar(
                        out=qa[:], in0=xc[:, qsl], scalar1=cc[:, 5:6], scalar2=0.5,
                        op0=ALU.mult, op1=ALU.add))
                    P.emit("vector", lambda e: e.tensor_copy(out=qti[:], in_=qa[:]))
                    P.emit("vector", lambda e: e.tensor_tensor(out=ebuf[:], in0=qti[:],
                                                               in1=qa[:], op=ALU.is_gt))
                    P.emit("vector", lambda e: e.tensor_tensor(out=qa[:], in0=qti[:],
                                                               in1=ebuf[:], op=ALU.subtract))
                    P.emit("vector", lambda e, cc=cc, qsl=qsl: e.tensor_scalar_mul(
                        out=xdq[:, qsl], in0=qa[:], scalar1=cc[:, 4:5]), inc="sv")
                P.wait("tensor", "sv")
                if t > 0 and ch == 0:
                    P.wait("tensor", "ss", ssrfmLast)
                for dc in range(8):
                    P.emit("tensor", lambda e, dc=dc: e.transpose(
                        out=pTr[:, 128 * dc:128 * dc + 128],
                        in_=xnb[:, 128 * dc:128 * dc + 128], identity=idbf[:]))
                P.inc_last("tensor", "st")
                stxnb = P.val("st")
                P.wait("scalar", "st", stxnb)
                P.emit("scalar", lambda e, ch=ch: e.copy(
                    out=xn_fm[:].rearrange("p (d c) -> p d c", c=512)[:, :, 128 * ch:128 * ch + 128],
                    in_=pTr[:].rearrange("p (d c) -> p d c", c=128)), inc="ss")
                P.wait("tensor", "ss", P.val("ss"))
                for dc in range(8):
                    P.emit("tensor", lambda e, dc=dc: e.transpose(
                        out=pTr[:, 128 * dc:128 * dc + 128],
                        in_=xdq[:, 128 * dc:128 * dc + 128], identity=idbf[:]))
                P.inc_last("tensor", "st")
                stxdq = P.val("st")
                P.wait("scalar", "st", stxdq)
                P.emit("scalar", lambda e, ch=ch: e.copy(
                    out=xq_fm[:].rearrange("p (d c) -> p d c", c=512)[:, :, 128 * ch:128 * ch + 128],
                    in_=pTr[:].rearrange("p (d c) -> p d c", c=128)), inc="ss")
            vxn_ss = P.val("ss")
            P.wait("gpsimd", "ss", vxn_ss)
            P.dma("gpsimd", lambda e, t=t: e.dma_start(out=xqs_d[t], in_=xq_fm[:]), "dG")
            dqStoreLast = P.val("dG")
            if debug and t == 0:
                P.dma("sync", lambda e: e.dma_start(out=dbg["dbg_xnfm"], in_=xn_fm[:]), "dF")
                P.dma("sync", lambda e: e.dma_start(out=dbg["dbg_xq"], in_=xq_fm[:]), "dF")

            P.wait("tensor", "ss", vxn_ss)
            for dc in range(8):
                P.emit("tensor", lambda e, dc=dc: e.matmul(
                    out=pMi[0:16, 0:512], lhsT=wg_bf[:, 32 * dc:32 * dc + 16],
                    rhs=xn_fm[:, 512 * dc:512 * dc + 512], start=(dc == 0), stop=(dc == 7)))
            for dc in range(8):
                P.emit("tensor", lambda e, dc=dc: e.matmul(
                    out=pAcc[0:16, 0:512], lhsT=wg_bf[:, 32 * dc + 16:32 * dc + 32],
                    rhs=xn_fm[:, 512 * dc:512 * dc + 512], start=(dc == 0), stop=(dc == 7)))
            P.inc_last("tensor", "st")
            P.wait("scalar", "st")
            P.wait("scalar", "sv", initV)
            P.emit("scalar", lambda e: e.activation(out=g_t1[:], in_=pMi[0:16, 0:512],
                                                    func=AF.Exp, scale=-1.0, bias=nbw[:, 0:1]))
            P.emit("scalar", lambda e: e.activation(out=g_t2[:], in_=pAcc[0:16, 0:512],
                                                    func=AF.Exp, scale=1.0, bias=bgc[:, 1:2]),
                   inc="ss")
            gatePsumFree = P.val("ss")
            P.wait("vector", "ss")
            P.emit("vector", lambda e: e.tensor_scalar_add(out=g_t1[:], in0=g_t1[:], scalar1=1.0))
            P.emit("vector", lambda e: e.tensor_scalar_add(out=g_t2[:], in0=g_t2[:], scalar1=1.0))
            P.vflush(flshb)
            P.emit("vector", lambda e: e.reciprocal(out=g_t2[:], in_=g_t2[:]))
            P.vflush(flshb, inc="sv")
            P.wait("scalar", "sv")
            P.emit("scalar", lambda e: e.activation(out=g_t1[:], in_=g_t1[:], func=AF.Ln))
            P.emit("scalar", lambda e: e.activation(out=g_lg[:], in_=g_t2[:], func=AF.Ln,
                                                    scale=0.95, bias=1e-8), inc="ss")
            P.wait("vector", "ss")
            for ch in range(4):
                P.emit("vector", lambda e, ch=ch: e.tensor_tensor_scan(
                    out=g_D[:, 128 * ch:128 * ch + 128], data0=g_lg[:, 128 * ch:128 * ch + 128],
                    data1=g_lg[:, 128 * ch:128 * ch + 128], initial=0.0,
                    op0=ALU.add, op1=ALU.bypass), inc="sv")
            P.vflush(flshb)
            P.emit("vector", lambda e: e.scalar_tensor_tensor(
                out=g_A[:], in0=g_t1[:], scalar=-1.0, in1=g_D[:],
                op0=ALU.mult, op1=ALU.subtract))
            P.emit("vector", lambda e: e.tensor_copy(out=d3a[:], in_=g_D[:]))
            P.emit("vector", lambda e: e.tensor_tensor(out=d3e[:], in0=g_D[:], in1=d3a[:],
                                                       op=ALU.subtract))
            P.emit("vector", lambda e: e.tensor_copy(out=d3b[:], in_=d3e[:]))
            P.emit("vector", lambda e: e.tensor_tensor(out=d3e[:], in0=d3e[:], in1=d3b[:],
                                                       op=ALU.subtract))
            P.emit("vector", lambda e: e.tensor_copy(out=d3c[:], in_=d3e[:]), inc="sv")
            vgate = P.val("sv")
            if debug and t == 0:
                P.wait("sync", "sv", vgate)
                P.dma("sync", lambda e: e.dma_start(out=dbg["dbg_gD"], in_=g_D[:]), "dF")
                P.dma("sync", lambda e: e.dma_start(out=dbg["dbg_gA"], in_=g_A[:]), "dF")

            P.wait("tensor", "ss", gatePsumFree)
            banklast = {id(pMi): 0, id(pSc): 0}
            qkit = 0
            for wmat, dst in ((wq_bf, qfm), (wk_bf, kfm)):
                for oc in range(8):
                    pb = pMi if (qkit % 2 == 0) else pSc
                    eb = ebuf if (qkit % 2 == 0) else qa
                    P.wait("tensor", "sv", banklast[id(pb)])
                    for dc in range(8):
                        P.emit("tensor", lambda e, oc=oc, dc=dc, wmat=wmat, pb=pb: e.matmul(
                            out=pb[:, 0:512],
                            lhsT=wmat[:, 1024 * dc + 128 * oc:1024 * dc + 128 * oc + 128],
                            rhs=xn_fm[:, 512 * dc:512 * dc + 512],
                            start=(dc == 0), stop=(dc == 7)))
                    P.inc_last("tensor", "st")
                    stqk = P.val("st")
                    P.wait("scalar", "st", stqk)
                    P.wait("scalar", "sv", banklast[id(pb)])
                    P.emit("scalar", lambda e, pb=pb, eb=eb: e.activation(
                        out=eb[:], in_=pb[:, 0:512], func=AF.Exp), inc="ss")
                    ssqk = P.val("ss")
                    P.wait("vector", "ss", ssqk)
                    P.emit("vector", lambda e, eb=eb: e.tensor_scalar_min(
                        out=eb[:], in0=eb[:], scalar1=1.0))
                    P.emit("vector", lambda e, oc=oc, dst=dst, pb=pb, eb=eb:
                           e.scalar_tensor_tensor(
                        out=dst[:, 512 * oc:512 * oc + 512], in0=pb[:, 0:512], scalar=0.0,
                        in1=eb[:], op0=ALU.max, op1=ALU.add), inc="sv")
                    banklast[id(pb)] = P.val("sv")
                    qkit += 1
            if debug and t == 0:
                P.wait("sync", "sv")
                P.dma("sync", lambda e: e.dma_start(out=dbg["dbg_q"], in_=qfm[:]), "dF")
                P.dma("sync", lambda e: e.dma_start(out=dbg["dbg_k"], in_=kfm[:]), "dF")

            vcop = {}
            for ch in range(4):
                if ch > 0:
                    P.wait("tensor", "ss", vcop[ch - 1])
                else:
                    P.wait("tensor", "sv", vPrevTileDone)
                for dc in range(8):
                    for hf in range(2):
                        P.emit("tensor", lambda e, dc=dc, ch=ch, hf=hf: e.matmul(
                            out=pAcc[:, 512 * hf:512 * hf + 512],
                            lhsT=xn_fm[:, 512 * dc + 128 * ch:512 * dc + 128 * ch + 128],
                            rhs=wv_bf[:, 1024 * dc + 512 * hf:1024 * dc + 512 * hf + 512],
                            start=(dc == 0), stop=(dc == 7)))
                P.inc_last("tensor", "st")
                stv = P.val("st")
                P.wait("scalar", "st", stv)
                P.emit("scalar", lambda e, ch=ch: e.copy(
                    out=vtm[ch][:].rearrange("p (h o) -> p h o", o=65)[:, :, 0:64],
                    in_=pAcc[:].rearrange("p (h o) -> p h o", o=64)), inc="ss")
                vcop[ch] = P.val("ss")
            if debug and t == 0:
                P.wait("sync", "sv")
                P.dma("sync", lambda e: e.dma_start(out=dbg["dbg_v"], in_=vtm[0][:]), "dF")

            for ch in range(4):
                gsl = slice(128 * ch, 128 * ch + 128)
                dend = g_D[:, 128 * ch + 127:128 * ch + 128]
                P.wait("scalar", "sv", vgate)
                P.wait("scalar", "st")
                for i3, s3 in enumerate((d3a, d3b, d3c)):
                    P.dma("scalar", lambda e, i3=i3, s3=s3, gsl=gsl: e.dma_start(
                        out=dfl[i3:i3 + 1, :], in_=s3[:, gsl]), "dF")
                fD = P.val("dF")
                P.emit("scalar", lambda e, gsl=gsl: e.activation(
                    out=gegs[:], in_=g_D[:, gsl], func=AF.Exp))
                P.emit("scalar", lambda e, gsl=gsl, dend=dend: e.activation(
                    out=gwgs[:], in_=g_A[:, gsl], func=AF.Exp, bias=dend), inc="ss")
                sgt = P.val("ss")
                P.wait("vector", "ss", sgt)
                P.emit("vector", lambda e: e.tensor_copy(
                    out=g_lam[:], in_=gegs[:, 127:128].broadcast_to([16, 128])), inc="sv")
                vlam = P.val("sv")
                P.wait("tensor", "sv", vlam)
                P.emit("tensor", lambda e, gsl=gsl: e.transpose(
                    out=pMi[:, 0:16], in_=g_A[:, gsl], identity=idf32[0:16, 0:16]))
                P.emit("tensor", lambda e: e.transpose(out=pMi[:, 16:32], in_=gegs[:],
                                                       identity=idf32[0:16, 0:16]))
                P.emit("tensor", lambda e: e.transpose(out=pMi[:, 32:48], in_=gwgs[:],
                                                       identity=idf32[0:16, 0:16]))
                P.emit("tensor", lambda e: e.transpose(out=pMi[:, 48:64], in_=g_lam[:],
                                                       identity=idf32[0:16, 0:16]), inc="st")
                P.wait("vector", "st")
                P.emit("vector", lambda e: e.tensor_copy(out=atok[:], in_=pMi[:, 0:16]))
                P.emit("vector", lambda e: e.tensor_copy(out=editok[:], in_=pMi[:, 16:32]))
                P.emit("vector", lambda e: e.tensor_copy(out=wdtok[:], in_=pMi[:, 32:48]))
                P.emit("vector", lambda e: e.tensor_copy(out=lamsb[:], in_=pMi[:, 48:64]),
                       inc="sv")
                vtok = P.val("sv")
                for c2 in range(8):
                    P.emit("tensor", lambda e, c2=c2, ch=ch: e.transpose(
                        out=pTr[:, 128 * c2:128 * c2 + 128],
                        in_=kfm[:, 512 * c2 + 128 * ch:512 * c2 + 128 * ch + 128],
                        identity=idbf[:]))
                P.inc_last("tensor", "st")
                stktm = P.val("st")
                P.wait("scalar", "st", stktm)
                P.emit("scalar", lambda e: e.copy(out=ktm[:], in_=pTr[:]), inc="ss")
                ssktm = P.val("ss")
                P.wait("vector", "ss", ssktm)
                P.emit("vector", lambda e: e.tensor_tensor(
                    out=khat[:].rearrange("p (h o) -> p h o", o=64),
                    in0=ktm[:].rearrange("p (h o) -> p h o", o=64),
                    in1=wdtok[:].unsqueeze(2).broadcast_to([128, 16, 64]),
                    op=ALU.mult), inc="sv")
                vkhat = P.val("sv")

                def sc_blk(w, jj, width):
                    if w == 0:
                        return pSc[:, 128 * jj:128 * jj + width]
                    pb_ = pMi if jj < 4 else pTrF
                    return pb_[:, 128 * (jj % 4):128 * (jj % 4) + width]

                veff_w = {}
                vcomb_w = {}
                for w in range(2):
                    dkb = pDk if w == 0 else pAcc
                    if w == 0:
                        P.wait("tensor", "sv", vsupPrev)
                    else:
                        P.wait("tensor", "sv", max(vtok, vcomb1Prev))
                        P.wait("tensor", "ss", ssktm)
                    P.wait("tensor", "dF", fD)
                    for cp in range(4):
                        c = 4 * w + cp
                        for half in range(2):
                            b0, j = 64 * half, cp + 4 * half
                            P.emit("tensor", lambda e, c=c, b0=b0, j=j, ch=ch, w=w: e.matmul(
                                out=sc_blk(w, j, 128),
                                lhsT=kfm[b0:b0 + 64, 512 * c + 128 * ch:512 * c + 128 * ch + 128],
                                rhs=qfm[b0:b0 + 64, 512 * c + 128 * ch:512 * c + 128 * ch + 128],
                                start=True, stop=True))
                    dflv = dfl[:, :].rearrange("k (hw hm par i) -> k hw par hm i",
                                               hw=2, hm=4, par=2, i=128)
                    for half in range(2):
                        P.emit("tensor", lambda e, half=half, dkb=dkb: e.matmul(
                            out=dkb[:, 512 * half:512 * half + 512], lhsT=idbf[:],
                            rhs=mask4[:], start=True, stop=False))
                        P.emit("tensor", lambda e, w=w, half=half, dflv=dflv, dkb=dkb: e.matmul(
                            out=dkb[:, 512 * half:512 * half + 512], lhsT=ones3[:],
                            rhs=dflv[:, w, half, :, :],
                            start=False, stop=True))
                    P.inc_last("tensor", "st")
                    P.wait("scalar", "st")
                    P.wait("scalar", "sv", vtok)
                    if w == 1:
                        P.wait("scalar", "sv", veff_w[0])
                    for j in range(8):
                        sg_ = BLK[8 * w + j]
                        P.emit("scalar", lambda e, j=j, sg_=sg_, dkb=dkb: e.activation(
                            out=ptb[:, 128 * j:128 * j + 128], in_=dkb[:, 128 * j:128 * j + 128],
                            func=AF.Exp, bias=atok[:, sg_:sg_ + 1]))
                    P.emit("scalar", lambda e: e.copy(out=ptb[0:1, 0:1], in_=ptb[0:1, 0:1]),
                           inc="ss")
                    P.wait("vector", "ss")
                    if w == 0:
                        P.emit("vector", lambda e, w=w: e.tensor_tensor(
                            out=efb[w][:], in0=pSc[:], in1=ptb[:], op=ALU.mult), inc="sv")
                    else:
                        P.emit("vector", lambda e, w=w: e.tensor_tensor(
                            out=efb[w][:, 0:512], in0=pMi[:, 0:512], in1=ptb[:, 0:512],
                            op=ALU.mult))
                        P.emit("vector", lambda e, w=w: e.tensor_tensor(
                            out=efb[w][:, 512:1024], in0=pTrF[:, 0:512],
                            in1=ptb[:, 512:1024], op=ALU.mult), inc="sv")
                    veff = P.val("sv")
                    veff_w[w] = veff
                    if debug and t == 0 and ch == 0 and w == 0:
                        P.wait("sync", "sv", veff)
                        P.dma("sync", lambda e: e.dma_start(out=dbg["dbg_eff"], in_=efb[0][:]),
                              "dF")
                    P.wait("tensor", "sv", veff)
                    for j in range(8):
                        b = 8 * w + j
                        P.emit("tensor", lambda e, w=w, j=j, b=b, ch=ch: e.matmul(
                            out=sc_blk(w, j, 65),
                            lhsT=efb[w][:, 128 * j:128 * j + 128],
                            rhs=vtm[ch][:, 65 * b:65 * b + 65], start=True, stop=True))
                    for cp in range(4):
                        c = 4 * w + cp
                        for half in range(2):
                            b0, j = 64 * half, cp + 4 * half
                            b = 8 * w + j
                            P.emit("tensor", lambda e, c=c, b0=b0, j=j, b=b, ch=ch, dkb=dkb:
                                   e.matmul(
                                out=dkb[:, 128 * j:128 * j + 65],
                                lhsT=qfm[b0:b0 + 64, 512 * c + 128 * ch:512 * c + 128 * ch + 128],
                                rhs=S_bf[b0:b0 + 64, 128 * b:128 * b + 65],
                                start=True, stop=True))
                    P.inc_last("tensor", "st")
                    P.wait("vector", "st")
                    P.emit("vector", lambda e, w=w, dkb=dkb: e.tensor_tensor(
                        out=tcb[:].rearrange("p (hi lo o) -> p hi lo o", hi=2, lo=4),
                        in0=dkb[:].rearrange("p (hi lo o) -> p hi lo o",
                                             hi=2, lo=4)[:, :, :, 0:65],
                        in1=_slotgrid(editok, w, 65), op=ALU.mult))
                    if w == 0:
                        P.emit("vector", lambda e: e.tensor_add(
                            out=acb[:].rearrange("p (j o) -> p j o", o=65),
                            in0=tcb[:].rearrange("p (j o) -> p j o", o=65),
                            in1=pSc[:].rearrange("p (j o) -> p j o", o=128)[:, :, 0:65]))
                    else:
                        P.emit("vector", lambda e: e.tensor_add(
                            out=acb[:, 0:260].rearrange("p (j o) -> p j o", o=65),
                            in0=tcb[:, 0:260].rearrange("p (j o) -> p j o", o=65),
                            in1=pMi[:].rearrange("p (j o) -> p j o", o=128)[:, :, 0:65]))
                        P.emit("vector", lambda e: e.tensor_add(
                            out=acb[:, 260:520].rearrange("p (j o) -> p j o", o=65),
                            in0=tcb[:, 260:520].rearrange("p (j o) -> p j o", o=65),
                            in1=pTrF[:].rearrange("p (j o) -> p j o", o=128)[:, :, 0:65]))
                    P.vflush(flshb)
                    P.emit("vector", lambda e: e.tensor_single_scalar(
                        out=denb[:], in_=acb[:].rearrange("p (j o) -> p j o", o=65)[:, :, 64:65],
                        scalar=1e-6, op=ALU.add))
                    P.vflush(flshb)
                    P.emit("vector", lambda e: e.reciprocal(out=denb[:], in_=denb[:]))
                    P.vflush(flshb)
                    P.emit("vector", lambda e, w=w: e.tensor_tensor(
                        out=rtm[:, 512 * w:512 * w + 512].rearrange("p (j o) -> p j o", o=64),
                        in0=acb[:].rearrange("p (j o) -> p j o", o=65)[:, :, 0:64],
                        in1=denb[:].unsqueeze(2).broadcast_to([128, 8, 64]),
                        op=ALU.mult), inc="sv")
                    vcomb_w[w] = P.val("sv")
                vcomb1Prev = vcomb_w[1]
                P.wait("tensor", "sv", vcomb_w[0])
                for b in range(16):
                    hh = BLK[b]
                    pout, j = (pSc, b) if b < 8 else (pDk, b - 8)
                    P.emit("tensor", lambda e, hh=hh, pout=pout, j=j, b=b, ch=ch: e.matmul(
                        out=pout[0:64, 128 * j:128 * j + 65],
                        lhsT=khat[:, 64 * hh:64 * hh + 64],
                        rhs=vtm[ch][:, 65 * b:65 * b + 65], start=True, stop=True))
                P.inc_last("tensor", "st")
                P.wait("vector", "st")
                for w in range(2):
                    pin = pSc if w == 0 else pDk
                    sv_ = S_f[:, 1024 * w:1024 * w + 1024].rearrange(
                        "p (hi lo o) -> p hi lo o", hi=2, lo=4)[:, :, :, 0:65]
                    P.emit("vector", lambda e, sv_=sv_, w=w: e.tensor_tensor(
                        out=sv_, in0=sv_, in1=_slotgrid(lamsb, w, 65, 64), op=ALU.mult))
                    P.emit("vector", lambda e, sv_=sv_, pin=pin: e.tensor_add(
                        out=sv_, in0=sv_,
                        in1=pin[0:64, :].rearrange("p (hi lo o) -> p hi lo o",
                                                   hi=2, lo=4)[:, :, :, 0:65]))
                P.emit("vector", lambda e: e.tensor_copy(out=S_bf[0:64, :], in_=S_f[:]),
                       inc="sv")
                vsup = P.val("sv")
                vsupPrev = vsup
                P.wait("scalar", "sv", vsup)
                P.dma("scalar", lambda e: e.dma_start(out=S_bf[64:128, :], in_=S_bf[0:64, :]),
                      "dF")
                fS = P.val("dF")
                if debug and t == 0 and ch == 0:
                    P.wait("sync", "sv", vsup)
                    P.dma("sync", lambda e: e.dma_start(out=dbg["dbg_S"], in_=S_f[:]), "dF")
                    P.dma("sync", lambda e: e.dma_start(out=dbg["dbg_R"], in_=rtm[:]), "dF")

                P.wait("tensor", "sv", vcomb_w[1])
                for fc in range(8):
                    P.emit("tensor", lambda e, fc=fc: e.transpose(
                        out=pTr[:, 128 * fc:128 * fc + 128], in_=rtm[:, 128 * fc:128 * fc + 128],
                        identity=idbf[:]))
                P.inc_last("tensor", "st")
                strt = P.val("st")
                P.wait("scalar", "st", strt)
                P.emit("scalar", lambda e: e.copy(out=rfm[:], in_=pTr[:]), inc="ss")
                ssrfmLast = P.val("ss")
                P.wait("tensor", "ss", ssrfmLast)
                for fc in range(8):
                    for hf in range(2):
                        P.emit("tensor", lambda e, fc=fc, hf=hf: e.matmul(
                            out=pAcc[:, 512 * hf:512 * hf + 512],
                            lhsT=rfm[:, 128 * fc:128 * fc + 128],
                            rhs=wo_bf[:, 1024 * fc + 512 * hf:1024 * fc + 512 * hf + 512],
                            start=(fc == 0), stop=(fc == 7)))
                P.inc_last("tensor", "st")
                P.wait("vector", "st")
                ob = out_sb[ch % 2]
                P.wait("vector", "dO", laststore[ch % 2])
                P.emit("vector", lambda e, ob=ob, ch=ch: e.scalar_tensor_tensor(
                    out=ob[:], in0=x_sb[:, 1024 * ch:1024 * ch + 1024],
                    scalar=colsA[:, 8 * ch:8 * ch + 1], op0=ALU.add,
                    in1=pAcc[:], op1=ALU.add), inc="sv")
                vout = P.val("sv")
                P.wait("sync", "sv", vout)
                P.dma("sync", lambda e, t=t, ch=ch, ob=ob: e.dma_start(
                    out=y_d[TILE * t + 128 * ch:TILE * t + 128 * ch + 128, :], in_=ob[:]), "dO")
                laststore[ch % 2] = P.val("dO")
            vPrevTileDone = P.val("sv")
            tPrevVcons = P.val("st")

        endA_sv, endA_ss, endA_st = P.val("sv"), P.val("ss"), P.val("st")

        _bcur = [phase_base]
        _boff = {}

        def sbB(name, shape, dt=F32, at=None):
            esz = {F32: 4, BF16: 2, I32: 4}[dt]
            nb = shape[1] * esz
            off = _bcur[0] if at is None else at
            h = nc.alloc_sbuf_tensor_at(name, shape, dt, offset=off)
            _boff[name] = off
            if at is None:
                _bcur[0] = off + ((nb + 31) // 32) * 32
            return h

        w1q = sbB("w1q", [128, 32768], BF16)
        w2q = sbB("w2q", [128, 32768], BF16)
        xqB = [sbB(f"xqB{i}", [128, 4096], BF16) for i in range(2)]
        opB2 = [sbB(f"opB{i}", [128, 1024]) for i in range(2)]
        hsq = sbB("hsq", [128, 16384], BF16)
        hb = [sbB(f"hb{i}", [128, 514]) for i in range(3)]
        hsave = sbB("hsave", [128, 64])
        maccD = sbB("maccD", [128, 512])
        maccN = sbB("maccN", [128, 512])
        s2b = sbB("s2b", [128, 512], F32, at=_boff["maccD"])
        aB = [sbB("aB0", [128, 512])] * 2
        tiB = [sbB("tiB0", [128, 512], I32)] * 2
        convW = sbB("convW", [128, 96])
        convB = sbB("convB", [128, 32])
        colsB = sbB("colsB", [128, 16])
        rowB = sbB("rowB", [1, 512])
        yrow = sbB("yrow", [1, 1024])
        tfB = [sbB("tfB0", [128, 512], F32, at=_boff["rowB"])]
        cB = [sbB("cB0", [128, 512], F32, at=_boff["yrow"])]
        scl = sbB("scl", [1, 8])
        gfb = sbB("gfb", [128, 2])
        epf = sbB("epf", [128, 128], F32, at=_boff["aB0"])
        epb = sbB("epb", [128, 32], BF16, at=_boff["aB0"] + 512)
        epi32 = sbB("epi32", [128, 32], I32, at=_boff["aB0"] + 576)
        wstr = [sbB("wstr0", [128, 4096], F32, at=_boff["hsq"]),
                sbB("wstr1", [128, 4096], F32, at=_boff["hsq"] + 16384)]

        for e in ENGS:
            P.wait(e, "sv", endA_sv)
            P.wait(e, "ss", endA_ss)
            P.wait(e, "st", endA_st)
            P.wait(e, "dO")
            P.wait(e, "dQ")
            P.wait(e, "dF")
            P.wait(e, "dG")
        P.dma("sync", lambda e: e.dma_start(out=convW[:], in_=cw_d), "dW")
        P.dma("sync", lambda e: e.dma_start(out=convB[:], in_=cb_d), "dW")
        wB0 = P.val("dW")
        P.wait("vector", "dW", wB0)
        P.emit("vector", lambda e: e.memset(colsB[:, 3:4], 1.0))
        P.emit("vector", lambda e: e.memset(opB2[0][0:1, :], 0.0))
        P.emit("vector", lambda e: e.memset(hsave[:], 0.0), inc="sv")
        P.emit("gpsimd", lambda e: e.memset(gfb[:], 0.0))

        WQS = (("sync", "dW"), ("scalar", "dQ"), ("gpsimd", "dG"))

        def wquant(src_d, nch, chw, dstq, colix, bufs):
            nb = len(bufs)
            use_done = [0] * nb
            for i in range(nch):
                eng, sem = WQS[i % len(WQS)]
                P.wait(eng, "sv", use_done[i % nb])
                P.dma(eng, lambda e, i=i, src_d=src_d, chw=chw, b=bufs[i % nb]:
                      e.dma_start(out=b[:, 0:chw],
                                  in_=src_d[128 * i:128 * i + 128, :]), sem)
                P.wait("vector", sem)
                P.emit("vector", lambda e, i=i, chw=chw, b=bufs[i % nb]: e.tensor_reduce(
                    out=maccD[:, i % 4:i % 4 + 1], in_=b[:, 0:chw],
                    axis=AX.X, op=ALU.add, apply_absolute_value=True), inc="sv")
                use_done[i % nb] = P.val("sv")
                P.vflush(flshb)
                if i == 0:
                    P.emit("vector", lambda e: e.tensor_copy(
                        out=maccD[:, 4:5], in_=maccD[:, 0:1]), inc="sv")
                else:
                    P.emit("vector", lambda e, i=i: e.tensor_add(
                        out=maccD[:, 4:5], in0=maccD[:, 4:5],
                        in1=maccD[:, i % 4:i % 4 + 1]), inc="sv")
            P.wait("tensor", "sv")
            P.emit("tensor", lambda e: e.matmul(out=pMi[0:1, 0:1], lhsT=maccD[:, 4:5],
                                                rhs=colsB[:, 3:4], start=True, stop=True),
                   inc="st")
            P.wait("vector", "st")
            P.emit("vector", lambda e: e.tensor_single_scalar(
                out=scl[0:1, 0:1], in_=pMi[0:1, 0:1], scalar=1.0 / (4096.0 * 1024.0),
                op=ALU.mult))
            P.vflush(flshb)
            P.emit("vector", lambda e: e.tensor_scalar_max(out=scl[0:1, 0:1],
                                                           in0=scl[0:1, 0:1], scalar1=1e-5))
            P.vflush(flshb)
            P.emit("vector", lambda e: e.reciprocal(out=scl[0:1, 1:2], in_=scl[0:1, 0:1]))
            P.vflush(flshb, inc="sv")
            P.wait("tensor", "sv")
            P.emit("tensor", lambda e: e.matmul(out=pMi[:, 4:5], lhsT=ones1[0:1, :],
                                                rhs=scl[0:1, 1:2], start=True, stop=True),
                   inc="st")
            P.wait("vector", "st")
            P.emit("vector", lambda e, colix=colix: e.tensor_copy(
                out=colsB[:, colix:colix + 1], in_=pMi[:, 4:5]), inc="sv")
            P.vflush(flshb)
            for i in range(nch):
                eng, sem = WQS[i % len(WQS)]
                P.wait(eng, "sv", use_done[i % nb])
                P.dma(eng, lambda e, i=i, src_d=src_d, chw=chw, b=bufs[i % nb]:
                      e.dma_start(out=b[:, 0:chw],
                                  in_=src_d[128 * i:128 * i + 128, :]), sem)
                P.wait("vector", sem)
                wb = bufs[i % nb]
                P.emit("vector", lambda e, wb=wb, chw=chw, colix=colix: e.tensor_scalar(
                    out=wb[:, 0:chw], in0=wb[:, 0:chw], scalar1=colsB[:, colix:colix + 1],
                    scalar2=4.5, op0=ALU.mult, op1=ALU.add))
                for s0 in range(0, chw, 512):
                    P.emit("vector", lambda e, wb=wb, s0=s0: e.tensor_copy(
                        out=tiB[0][:], in_=wb[:, s0:s0 + 512]))
                    P.emit("vector", lambda e, wb=wb, s0=s0: e.tensor_tensor(
                        out=cB[0][:], in0=tiB[0][:], in1=wb[:, s0:s0 + 512], op=ALU.is_gt))
                    P.emit("vector", lambda e, wb=wb, s0=s0: e.tensor_tensor(
                        out=wb[:, s0:s0 + 512], in0=tiB[0][:], in1=cB[0][:], op=ALU.subtract))
                P.emit("vector", lambda e, wb=wb, chw=chw: e.tensor_scalar(
                    out=wb[:, 0:chw], in0=wb[:, 0:chw], scalar1=-4.0, scalar2=1.0,
                    op0=ALU.add, op1=ALU.min))
                P.emit("vector", lambda e, wb=wb, chw=chw, i=i, dstq=dstq: e.tensor_single_scalar(
                    out=dstq[:, chw * i:chw * i + chw], in_=wb[:, 0:chw], scalar=-1.0,
                    op=ALU.max), inc="sv")
                use_done[i % nb] = P.val("sv")

        import os as _os0
        _qv = int(_os0.environ.get("QV", "2"))
        wstr4 = [sbB(f"wst4_{k}", [128, 4096], F32, at=_boff["w2q"] + 16384 * k)
                 for k in range(4)]
        wstr8 = [sbB(f"wst8_{k}", [128, 1024], F32, at=_boff["hsq"] + 4096 * k)
                 for k in range(8)]
        if do_b and _qv >= 1:
            wquant(w1T_d, 8, 4096, w1q, 0, wstr4)
        if do_b and _qv >= 2:
            wquant(w2T_d, 32, 1024, w2q, 1, wstr8)
        P.wait("vector", "sv")
        if not do_b:
            P.emit("vector", lambda e: e.memset(colsB[:], 1.0), inc="sv")
        P.vflush(flshb)
        P.emit("vector", lambda e: e.reciprocal(out=colsB[:, 2:3], in_=colsB[:, 1:2]))
        P.vflush(flshb)
        P.emit("vector", lambda e: e.reciprocal(out=scl[0:1, 3:4], in_=colsB[0:1, 0:1]))
        P.vflush(flshb, inc="sv")
        P.wait("tensor", "sv")
        P.emit("tensor", lambda e: e.matmul(out=pMi[:, 6:7], lhsT=ones1[0:1, :],
                                            rhs=scl[0:1, 3:4], start=True, stop=True), inc="st")
        P.wait("vector", "st")
        P.emit("vector", lambda e: e.tensor_copy(out=colsB[:, 15:16], in_=pMi[:, 6:7]))
        P.vflush(flshb)
        P.emit("vector", lambda e: e.tensor_scalar_mul(out=convW[:], in0=convW[:],
                                                       scalar1=colsB[:, 15:16]), inc="sv")
        if debug:
            P.wait("sync", "sv")
            P.dma("sync", lambda e: e.dma_start(out=dbg["dbg_w1q"], in_=w1q[:, 0:4096]), "dF")

        import os as _osB
        lastout = {0: 0, 1: 0}
        lastopcons = {0: 0, 1: 0}
        lastxqT = {0: 0, 1: 0}

        P.wait("tensor", "sv")
        ssevac = {}
        svconv = {}
        svround = {}
        svstt = {}
        stmacc = {}
        w2banks = [(pAcc, 0), (pAcc, 512), (pSc, 0), (pSc, 512),
                   (pDk, 0), (pDk, 512), (pTrF, 0), (pMi, 0)]
        for t in range(nt if do_b else 0):
            w0 = TILE * t - 1
            xb = xqB[t % 2]
            P.wait("gpsimd", "st", lastxqT[t % 2])
            P.dma("gpsimd", lambda e, t=t, xb=xb: e.dma_start(out=xb[:], in_=xqs_d[t]), "dG")
            ldx = P.val("dG")
            P.wait("tensor", "dG", ldx)
            if t > 0:
                P.wait("tensor", "sv", svstt[(t - 1, 3)])
                P.wait("tensor", "sv", svstt[(t - 1, 1)])

            for hc in range(32):
                slot = hc % 3
                pb = pMi if hc % 2 == 0 else pSc
                if hc >= 2:
                    P.wait("tensor", "ss", ssevac[(t, hc - 2)])
                elif t > 0:
                    P.wait("tensor", "ss", ssevac[(t - 1, 30 + hc)])
                for dc in range(8):
                    P.emit("tensor", lambda e, dc=dc, hc=hc, pb=pb, xb=xb: e.matmul(
                        out=pb[:, 0:512],
                        lhsT=w1q[:, 4096 * dc + 128 * hc:4096 * dc + 128 * hc + 128],
                        rhs=xb[:, 512 * dc:512 * dc + 512], start=(dc == 0), stop=(dc == 7)))
                P.inc_last("tensor", "st")
                stw1 = P.val("st")
                P.wait("scalar", "st", stw1)
                if hc >= 3:
                    P.wait("scalar", "sv", svconv[(t, hc - 3)])
                elif t > 0:
                    P.wait("scalar", "sv", svconv[(t - 1, 29 + hc)])
                P.emit("scalar", lambda e, slot=slot, pb=pb: e.copy(
                    out=hb[slot][:, 2:514], in_=pb[:, 0:512]), inc="ss")
                ssevac[(t, hc)] = P.val("ss")
                P.wait("vector", "ss", ssevac[(t, hc)])
                P.emit("vector", lambda e, slot=slot, hc=hc: e.tensor_copy(
                    out=hb[slot][:, 0:2], in_=hsave[:, 2 * hc:2 * hc + 2]))
                P.emit("vector", lambda e, slot=slot, hc=hc: e.tensor_copy(
                    out=hsave[:, 2 * hc:2 * hc + 2], in_=hb[slot][:, 512:514]))
                P.emit("vector", lambda e, slot=slot, hc=hc: e.tensor_scalar_mul(
                    out=aB[0][:], in0=hb[slot][:, 1:513],
                    scalar1=convW[:, 3 * hc + 1:3 * hc + 2]))
                P.emit("vector", lambda e, slot=slot, hc=hc: e.scalar_tensor_tensor(
                    out=aB[0][:], in0=hb[slot][:, 0:512], scalar=convW[:, 3 * hc:3 * hc + 1],
                    in1=aB[0][:], op0=ALU.mult, op1=ALU.add))
                P.emit("vector", lambda e, slot=slot, hc=hc: e.scalar_tensor_tensor(
                    out=aB[0][:], in0=hb[slot][:, 2:514],
                    scalar=convW[:, 3 * hc + 2:3 * hc + 3],
                    in1=aB[0][:], op0=ALU.mult, op1=ALU.add), inc="sv")
                svconv[(t, hc)] = P.val("sv")
                P.wait("scalar", "sv", svconv[(t, hc)])
                P.emit("scalar", lambda e, hc=hc: e.activation(
                    out=hsq[:, 512 * hc:512 * hc + 512], in_=aB[0][:], func=AF.Silu,
                    bias=convB[:, hc:hc + 1]), inc="ss")
                sssilu = P.val("ss")
                P.wait("vector", "ss", sssilu)
                if hc == 0:
                    if t > 0:
                        P.wait("vector", "st", stmacc[t - 1])
                    P.emit("vector", lambda e, hc=hc: e.tensor_copy(
                        out=maccD[:], in_=hsq[:, 512 * hc:512 * hc + 512]))
                    P.emit("vector", lambda e, hc=hc: e.tensor_copy(
                        out=maccN[:], in_=hsq[:, 512 * hc:512 * hc + 512]), inc="sv")
                else:
                    P.emit("vector", lambda e, hc=hc: e.tensor_tensor(
                        out=maccD[:], in0=maccD[:], in1=hsq[:, 512 * hc:512 * hc + 512],
                        op=ALU.max))
                    P.emit("vector", lambda e, hc=hc: e.tensor_tensor(
                        out=maccN[:], in0=maccN[:], in1=hsq[:, 512 * hc:512 * hc + 512],
                        op=ALU.min), inc="sv")
            lastxqT[t % 2] = P.val("st")
            P.emit("vector", lambda e: e.tensor_scalar_mul(out=maccN[:], in0=maccN[:],
                                                           scalar1=-1.0))
            P.vflush(flshb)
            P.emit("vector", lambda e: e.tensor_max(out=maccD[:], in0=maccD[:],
                                                    in1=maccN[:]), inc="sv")
            svmacc = P.val("sv")

            P.wait("tensor", "sv", svmacc)
            for su in range(4):
                P.emit("tensor", lambda e, su=su: e.transpose(
                    out=pMi[:, 128 * su:128 * su + 128],
                    in_=maccD[:, 128 * su:128 * su + 128], identity=idf32[:]))
            P.inc_last("tensor", "st")
            stmacc[t] = P.val("st")
            P.wait("vector", "st", stmacc[t])
            for su in range(4):
                P.emit("vector", lambda e, su=su: e.reduce_max(
                    out=colsB[:, 4 + su:5 + su], in_=pMi[:, 128 * su:128 * su + 128],
                    axis=AX.X), inc="sv")
            P.vflush(flshb)
            P.emit("vector", lambda e: e.tensor_scalar_max(
                out=colsB[:, 4:8], in0=colsB[:, 4:8], scalar1=1e-5))
            for su in range(4):
                P.vflush(flshb)
                P.emit("vector", lambda e, su=su: e.reciprocal(
                    out=colsB[:, 12 + su:13 + su], in_=colsB[:, 4 + su:5 + su]))
                P.vflush(flshb)
            P.emit("vector", lambda e: e.tensor_scalar_mul(
                out=colsB[:, 12:16], in0=colsB[:, 12:16], scalar1=127.0), inc="sv")
            P.wait("tensor", "sv")
            for su in range(4):
                P.emit("tensor", lambda e, su=su: e.transpose(
                    out=pSc[0:1, 128 * su:128 * su + 128], in_=colsB[:, 12 + su:13 + su],
                    identity=idf32[:]))
            P.inc_last("tensor", "st")
            P.wait("vector", "st")
            P.emit("vector", lambda e: e.tensor_copy(out=rowB[0:1, :], in_=pSc[0:1, 0:512]),
                   inc="sv")
            P.wait("tensor", "sv")
            P.emit("tensor", lambda e: e.matmul(out=pSc[:, 512:1024], lhsT=ones1[0:1, :],
                                                rhs=rowB[0:1, :], start=True, stop=True),
                   inc="st")
            P.wait("vector", "st")
            P.emit("vector", lambda e: e.tensor_copy(out=s2b[:], in_=pSc[:, 512:1024]))
            P.emit("vector", lambda e: e.tensor_scalar(
                out=colsB[:, 8:12], in0=colsB[:, 4:8], scalar1=colsB[:, 2:3],
                scalar2=1.0 / 127.0, op0=ALU.mult, op1=ALU.mult), inc="sv")
            vs2b = P.val("sv")

            P.wait("vector", "sv", vs2b)
            for hc in range(32):
                hsl = hsq[:, 512 * hc:512 * hc + 512]
                P.emit("vector", lambda e, hsl=hsl: e.tensor_tensor(
                    out=tfB[0][:], in0=hsl, in1=s2b[:], op=ALU.mult))
                P.emit("vector", lambda e: e.tensor_single_scalar(
                    out=cB[0][:], in_=tfB[0][:], scalar=0.5, op=ALU.add))
                P.emit("vector", lambda e: e.tensor_copy(
                    out=tiB[0][:], in_=cB[0][:]))
                P.emit("vector", lambda e: e.tensor_tensor(
                    out=tfB[0][:], in0=tiB[0][:], in1=cB[0][:], op=ALU.is_gt))
                P.emit("vector", lambda e, hsl=hsl: e.tensor_tensor(
                    out=hsl, in0=tiB[0][:], in1=tfB[0][:], op=ALU.subtract), inc="sv")
                svround[(t, hc)] = P.val("sv")
                P.wait("tensor", "sv", svround[(t, hc)])
                for su in range(4):
                    for hf in range(2):
                        pbank, off = w2banks[2 * su + hf]
                        P.emit("tensor", lambda e, su=su, hc=hc, hf=hf, pbank=pbank,
                               off=off: e.matmul(
                            out=pbank[:, off:off + 512],
                            lhsT=hsq[:, 512 * hc + 128 * su:512 * hc + 128 * su + 128],
                            rhs=w2q[:, 1024 * hc + 512 * hf:1024 * hc + 512 * hf + 512],
                            start=(hc == 0), stop=(hc == 31)))
            P.inc_last("tensor", "st")
            stw2 = P.val("st")
            for su in range(4):
                obi = su % 2
                r0 = w0 + 128 * su
                P.wait("sync", "sv", lastopcons[obi])
                if r0 < 0:
                    P.dma("sync", lambda e, obi=obi: e.dma_start(
                        out=opB2[obi][1:128, :], in_=y_d[0:127, :]), "dO")
                else:
                    P.dma("sync", lambda e, r0=r0, obi=obi: e.dma_start(
                        out=opB2[obi][:], in_=y_d[r0:r0 + 128, :]), "dO")
                ldop = P.val("dO")
                P.wait("vector", "st", stw2)
                P.wait("vector", "dO", max(ldop, lastout[obi]))
                b0, o0 = w2banks[2 * su]
                b1, o1 = w2banks[2 * su + 1]
                P.emit("vector", lambda e, su=su, obi=obi, b0=b0, o0=o0: e.scalar_tensor_tensor(
                    out=opB2[obi][:, 0:512], in0=b0[:, o0:o0 + 512],
                    scalar=colsB[:, 8 + su:9 + su],
                    op0=ALU.mult, in1=opB2[obi][:, 0:512], op1=ALU.add))
                P.emit("vector", lambda e, su=su, obi=obi, b1=b1, o1=o1: e.scalar_tensor_tensor(
                    out=opB2[obi][:, 512:1024], in0=b1[:, o1:o1 + 512],
                    scalar=colsB[:, 8 + su:9 + su],
                    op0=ALU.mult, in1=opB2[obi][:, 512:1024], op1=ALU.add),
                    inc="sv")
                svstt[(t, su)] = P.val("sv")
                lastopcons[obi] = P.val("sv")
                vob = P.val("sv")
                P.wait("sync", "sv", vob)
                if r0 < 0:
                    P.dma("sync", lambda e, obi=obi: e.dma_start(
                        out=y_d[0:127, :], in_=opB2[obi][1:128, :]), "dO")
                else:
                    P.dma("sync", lambda e, r0=r0, obi=obi: e.dma_start(
                        out=y_d[r0:r0 + 128, :], in_=opB2[obi][:]), "dO")
                lastout[obi] = P.val("dO")

        for _pn in range(int(_osB.environ.get("PADN", "0"))):
            P.emit("tensor", lambda e: e.nop())
        import os as _os2
        do_epi = _os2.environ.get("EPI", "1") == "1" and do_b
        cw3 = convW[:].rearrange("p (c k) -> p c k", k=3)
        if do_epi:
            hs2 = hsave[:].rearrange("p (c k) -> p c k", k=2)
            P.wait("vector", "sv")
            P.emit("vector", lambda e: e.tensor_tensor(
                out=epf[:, 0:32].unsqueeze(2), in0=hs2[:, :, 0:1],
                in1=cw3[:, :, 0:1], op=ALU.mult))
            P.emit("vector", lambda e: e.tensor_tensor(
                out=epf[:, 32:64].unsqueeze(2), in0=hs2[:, :, 1:2],
                in1=cw3[:, :, 1:2], op=ALU.mult))
            P.vflush(flshb)
            P.emit("vector", lambda e: e.tensor_add(out=epf[:, 0:32], in0=epf[:, 0:32],
                                                    in1=epf[:, 32:64]))
            P.vflush(flshb)
            P.emit("vector", lambda e: e.tensor_add(out=epf[:, 0:32], in0=epf[:, 0:32],
                                                    in1=convB[:]), inc="sv")
            P.wait("scalar", "sv")
            P.emit("scalar", lambda e: e.activation(out=epf[:, 32:64], in_=epf[:, 0:32],
                                                    func=AF.Silu), inc="ss")
            P.wait("vector", "ss")
            P.emit("vector", lambda e: e.reduce_max(out=epf[:, 64:65], in_=epf[:, 32:64],
                                                    axis=AX.X, apply_absolute_value=True),
                   inc="sv")
            P.vflush(flshb)
            P.wait("tensor", "sv")
            P.emit("tensor", lambda e: e.transpose(out=pMi[0:1, 0:128], in_=epf[:, 64:65],
                                                   identity=idf32[:]), inc="st")
            P.wait("vector", "st")
            P.emit("vector", lambda e: e.reduce_max(out=scl[0:1, 4:5], in_=pMi[0:1, 0:128],
                                                    axis=AX.X), inc="sv")
            P.vflush(flshb)
            P.emit("vector", lambda e: e.tensor_scalar_max(out=scl[0:1, 4:5], in0=scl[0:1, 4:5],
                                                           scalar1=1e-5))
            P.vflush(flshb)
            P.emit("vector", lambda e: e.reciprocal(out=scl[0:1, 5:6], in_=scl[0:1, 4:5]))
            P.vflush(flshb)
            P.emit("vector", lambda e: e.tensor_single_scalar(out=scl[0:1, 5:6], in_=scl[0:1, 5:6],
                                                              scalar=127.0, op=ALU.mult))
            P.emit("vector", lambda e: e.tensor_single_scalar(out=scl[0:1, 6:7], in_=scl[0:1, 4:5],
                                                              scalar=1.0 / 127.0, op=ALU.mult))
            P.vflush(flshb)
            P.emit("vector", lambda e: e.tensor_mul(out=scl[0:1, 6:7], in0=scl[0:1, 6:7],
                                                    in1=colsB[0:1, 2:3]), inc="sv")
            P.wait("tensor", "sv")
            P.emit("tensor", lambda e: e.matmul(out=pMi[:, 2:3], lhsT=ones1[0:1, :],
                                                rhs=scl[0:1, 5:6], start=True, stop=True),
                   inc="st")
            P.wait("vector", "st")
            P.emit("vector", lambda e: e.tensor_copy(out=colsB[:, 12:13], in_=pMi[:, 2:3]))
            P.vflush(flshb)
            P.emit("vector", lambda e: e.tensor_scalar(
                out=epf[:, 64:96], in0=epf[:, 32:64], scalar1=colsB[:, 12:13], scalar2=0.5,
                op0=ALU.mult, op1=ALU.add))
            P.vflush(flshb)
            P.emit("vector", lambda e: e.tensor_copy(out=epi32[:], in_=epf[:, 64:96]))
            P.vflush(flshb)
            P.emit("vector", lambda e: e.tensor_tensor(out=epf[:, 96:128], in0=epi32[:],
                                                       in1=epf[:, 64:96], op=ALU.is_gt))
            P.vflush(flshb)
            P.emit("vector", lambda e: e.tensor_tensor(out=epb[:], in0=epi32[:],
                                                       in1=epf[:, 96:128], op=ALU.subtract),
                   inc="sv")
            vepq = P.val("sv")
            P.dma("sync", lambda e: e.dma_start(out=yrow[0:1, :], in_=y_d[4095:4096, :]), "dO")
            ldyr = P.val("dO")
            P.wait("tensor", "sv", vepq)
            for oc in range(8):
                for hc in range(32):
                    P.emit("tensor", lambda e, hc=hc, oc=oc: e.matmul(
                        out=pAcc[:, oc:oc + 1],
                        lhsT=w2q[:, 1024 * hc + 128 * oc:1024 * hc + 128 * oc + 128],
                        rhs=epb[:, hc:hc + 1], start=(hc == 0), stop=(hc == 31)))
            P.inc_last("tensor", "st")
            P.wait("vector", "st")
            P.emit("vector", lambda e: e.tensor_copy(out=epf[:, 0:8], in_=pAcc[:, 0:8]),
                   inc="sv")
            P.wait("tensor", "sv")
            P.emit("tensor", lambda e: e.transpose(out=pMi[0:8, 0:128], in_=epf[:, 0:8],
                                                   identity=idf32[:]), inc="st")
            P.wait("vector", "st")
            P.emit("vector", lambda e: e.tensor_copy(out=epf[0:8, 0:128],
                                                     in_=pMi[0:8, 0:128]), inc="sv")
            P.wait("scalar", "sv")
            P.dma("scalar", lambda e: e.dma_start(out=opB2[1][0:1, 0:1024],
                                                  in_=epf[0:8, 0:128]), "dF")
            ldfl = P.val("dF")
            P.wait("vector", "dF", ldfl)
            P.wait("vector", "dO", ldyr)
            P.emit("vector", lambda e: e.scalar_tensor_tensor(
                out=yrow[0:1, :], in0=opB2[1][0:1, 0:1024], scalar=scl[0:1, 6:7],
                op0=ALU.mult, in1=yrow[0:1, :], op1=ALU.add), inc="sv")
            P.wait("sync", "sv")
            P.dma("sync", lambda e: e.dma_start(out=y_d[4095:4096, :], in_=yrow[0:1, :]), "dO")
        P.wait("sync", "dO")
        P.wait("sync", "dX")
        P.wait("sync", "dW")
        P.wait("sync", "dQ")
        P.wait("sync", "dF")

        with nc.Block() as block:
            @block.sync
            def _(e):
                for fn in P.ev["sync"]:
                    fn(e)

            @block.vector
            def _(e):
                for fn in P.ev["vector"]:
                    fn(e)

            @block.scalar
            def _(e):
                for fn in P.ev["scalar"]:
                    fn(e)

            @block.tensor
            def _(e):
                for fn in P.ev["tensor"]:
                    fn(e)

            @block.gpsimd
            def _(e):
                for fn in P.ev["gpsimd"]:
                    fn(e)

    return nc


def _host_inputs(inputs):
    f32 = np.float32
    x = np.ascontiguousarray(inputs["L0_attn_x"], dtype=f32)
    perm = np.concatenate([np.arange(64 * b, 64 * b + 64) for b in BLK])
    base = {
        "wqT": np.ascontiguousarray(inputs["L0_attn_Wq"].T, dtype=f32),
        "wkT": np.ascontiguousarray(inputs["L0_attn_Wk"].T, dtype=f32),
        "wvT": np.ascontiguousarray(inputs["L0_attn_Wv"].T[:, perm], dtype=f32),
        "woT": np.ascontiguousarray(inputs["L0_attn_Wo"].T[perm, :], dtype=f32),
        "wgT": np.ascontiguousarray(
            np.concatenate([inputs["L0_attn_Ww"].T, inputs["L0_attn_We"].T], axis=1), dtype=f32),
        "bg": np.ascontiguousarray(
            np.stack([inputs["L0_attn_bw"], inputs["L0_attn_be"]], axis=1), dtype=f32),
        "w1T": np.ascontiguousarray(inputs["L0_ffn_w1"].T, dtype=f32),
        "w2T": np.ascontiguousarray(inputs["L0_ffn_w2"].T, dtype=f32),
        "cw": np.ascontiguousarray(
            inputs["L0_ffn_conv_w"].reshape(32, 128, 3).transpose(1, 0, 2).reshape(128, 96),
            dtype=f32),
        "cb": np.ascontiguousarray(
            inputs["L0_ffn_conv_b"].reshape(32, 128).T, dtype=f32),
        "idbf": np.eye(128, dtype=ml_dtypes.bfloat16),
        "idf32": np.eye(128, dtype=f32),
        "mask4": np.ascontiguousarray(
            np.tile(np.tril(np.full((128, 128), -1.0e30, dtype=f32), -1), (1, 4))
            .astype(ml_dtypes.bfloat16)),
    }
    return x, base


def kernel(**inputs):
    x, base = _host_inputs(inputs)
    nc = build_nc(debug=False)
    in_maps = [dict(base, x=np.ascontiguousarray(x[c])) for c in range(8)]
    res = run_bass_kernel_spmd(nc, in_maps, core_ids=list(range(8)))
    out = np.stack([np.asarray(r["y"]) for r in res.results], axis=0)
    return out.astype(np.float32)


# version 12
# speedup vs baseline: 1.0413x; 1.0137x over previous
import numpy as np
import ml_dtypes
import concourse.bass as bass
import concourse.mybir as mybir
from concourse.bass_utils import run_bass_kernel_spmd
from contextlib import ExitStack

F32 = mybir.dt.float32
BF16 = mybir.dt.bfloat16
I32 = mybir.dt.int32
AF = mybir.ActivationFunctionType
ALU = mybir.AluOpType
AX = mybir.AxisListType

L, DM, H, DH, DHID = 4096, 1024, 16, 64, 4096
TILE, NT, C, NCH = 512, 8, 128, 4
BLK = [8 * w + 2 * (j % 4) + j // 4 for w in range(2) for j in range(8)]

ENGS = ("sync", "vector", "scalar", "tensor", "gpsimd")


class Plan:
    def __init__(self):
        self.ev = {e: [] for e in ENGS}
        self.cnt = {}
        self.semh = {}

    def sem(self, name, stack, nc):
        self.semh[name] = stack.enter_context(nc.semaphore(name))
        self.cnt[name] = 0

    def emit(self, eng, fn, inc=None, k=1):
        if inc is not None:
            self.cnt[inc] += k
            sem = self.semh[inc]
            self.ev[eng].append(lambda e, fn=fn, sem=sem, k=k: fn(e).then_inc(sem, k))
        else:
            self.ev[eng].append(fn)

    def dma(self, eng, fn, inc):
        self.emit(eng, fn, inc, 16)

    def vflush(self, fb, inc=None):
        self.emit("vector", lambda e, fb=fb: e.tensor_copy(out=fb[:, 1:2],
                                                           in_=fb[:, 0:1]), inc=inc)

    def sflush(self, fb, inc=None):
        self.emit("scalar", lambda e, fb=fb: e.copy(out=fb[:, 1:2],
                                                    in_=fb[:, 0:1]), inc=inc)

    def inc_last(self, eng, sname, k=1):
        self.cnt[sname] += k
        sem = self.semh[sname]
        prev = self.ev[eng].pop()
        self.ev[eng].append(lambda e, prev=prev, sem=sem, k=k: prev(e).then_inc(sem, k))

    def wait(self, eng, sname, val=None):
        v = self.cnt[sname] if val is None else val
        if v <= 0:
            return
        sem = self.semh[sname]
        self.ev[eng].append(lambda e, sem=sem, v=v: e.wait_ge(sem, v))

    def val(self, sname):
        return self.cnt[sname]


def _slotgrid(ap2d, w, rep, parts=128):
    v = ap2d[0:parts, 8 * w:8 * w + 8]
    v = v.rearrange("p (lo hi) -> p hi lo", hi=2)
    return v.unsqueeze(3).broadcast_to([parts, 2, 4, rep])


def build_nc(debug=False, nt=NT, do_b=True, stage_b=5):
    nc = bass.Bass("TRN2", target_bir_lowering=False, detect_race_conditions=False)
    x_d = nc.dram_tensor("x", [L, DM], F32, kind="ExternalInput").ap()
    y_d = nc.dram_tensor("y", [L, DM], F32, kind="ExternalOutput").ap()
    wqT_d = nc.dram_tensor("wqT", [DM, DM], F32, kind="ExternalInput").ap()
    wkT_d = nc.dram_tensor("wkT", [DM, DM], F32, kind="ExternalInput").ap()
    wvT_d = nc.dram_tensor("wvT", [DM, DM], F32, kind="ExternalInput").ap()
    woT_d = nc.dram_tensor("woT", [DM, DM], F32, kind="ExternalInput").ap()
    wgT_d = nc.dram_tensor("wgT", [DM, 32], F32, kind="ExternalInput").ap()
    bg_d = nc.dram_tensor("bg", [16, 2], F32, kind="ExternalInput").ap()
    w1T_d = nc.dram_tensor("w1T", [DM, DHID], F32, kind="ExternalInput").ap()
    w2T_d = nc.dram_tensor("w2T", [DHID, DM], F32, kind="ExternalInput").ap()
    cw_d = nc.dram_tensor("cw", [128, 96], F32, kind="ExternalInput").ap()
    cb_d = nc.dram_tensor("cb", [128, 32], F32, kind="ExternalInput").ap()
    idbf_d = nc.dram_tensor("idbf", [128, 128], BF16, kind="ExternalInput").ap()
    idf32_d = nc.dram_tensor("idf32", [128, 128], F32, kind="ExternalInput").ap()
    mask4_d = nc.dram_tensor("mask4", [128, 512], BF16, kind="ExternalInput").ap()
    xqs_d = nc.dram_tensor("xqs", [NT, 128, 4096], BF16, kind="Internal").ap()

    dbg = {}
    if debug:
        for nm, shp, dt in [("dbg_xnfm", [128, 4096], BF16), ("dbg_q", [128, 4096], BF16),
                            ("dbg_k", [128, 4096], BF16), ("dbg_v", [128, 1040], BF16),
                            ("dbg_gD", [16, 512], F32), ("dbg_gA", [16, 512], F32),
                            ("dbg_R", [128, 1024], BF16), ("dbg_S", [64, 2048], F32),
                            ("dbg_eff", [128, 1024], BF16), ("dbg_xq", [128, 4096], BF16),
                            ("dbg_w1q", [128, 4096], BF16), ("dbg_hq", [128, 512], BF16),
                            ("dbg_hs", [128, 514], F32)]:
            dbg[nm] = nc.dram_tensor(nm, shp, dt, kind="ExternalOutput").ap()

    P = Plan()
    root = ExitStack()
    with root:
        def psum(name, shape, dt=F32):
            return nc.alloc_psum_tensor(name, shape, dt)

        for s in ("dX", "dW", "dG", "dO", "dQ", "dF", "sv", "ss", "st", "sg", "sy"):
            P.sem(s, root, nc)

        pAcc = psum("pAcc", [128, 1024])
        pSc = psum("pSc", [128, 1024])
        pDk = psum("pDk", [128, 1024])
        pTrF = nc.alloc_psum_tensor("pTr", [128, 512], F32)
        pTr = pTrF[:].bitcast(BF16)
        pMi = psum("pMi", [128, 512])
        cpool = nc.alloc_sbuf_tensor("cpool", [128, 4], F32)
        flshb = nc.alloc_sbuf_tensor("flshb", [128, 2], F32)
        for _i, _v in enumerate((1e-5, 1e-8, 0.5, 0.0)):
            nc.const_aps.register(F32, _v, cpool[:, _i:_i + 1])
        idbf = nc.alloc_sbuf_tensor("idbf_s", [128, 128], BF16)
        idf32 = nc.alloc_sbuf_tensor("idf32_s", [128, 128], F32)
        ones1 = nc.alloc_sbuf_tensor("ones1", [1, 128], F32)
        P.dma("sync", lambda e: e.dma_start(out=idbf[:], in_=idbf_d), "dW")
        P.dma("sync", lambda e: e.dma_start(out=idf32[:], in_=idf32_d), "dW")

        phase_base = nc.sbuf_base

        def sbA(name, shape, dt=F32):
            return nc.alloc_sbuf_tensor(name, shape, dt)

        wq_bf = sbA("wq_bf", [128, 8192], BF16)
        wk_bf = sbA("wk_bf", [128, 8192], BF16)
        wv_bf = sbA("wv_bf", [128, 8192], BF16)
        wo_bf = sbA("wo_bf", [128, 8192], BF16)
        wg_bf = sbA("wg_bf", [128, 256], BF16)
        x_sb = sbA("x_sb", [128, 4096])
        xn_fm = sbA("xn_fm", [128, 4096], BF16)
        xq_fm = sbA("xq_fm", [128, 4096], BF16)
        qfm = sbA("qfm", [128, 4096], BF16)
        kfm = sbA("kfm", [128, 4096], BF16)
        vtm = [sbA(f"vtm{i}", [128, 1040], BF16) for i in range(4)]
        ktm = sbA("ktm", [128, 1024], BF16)
        khat = sbA("khat", [128, 1024], BF16)
        rtm = sbA("rtm", [128, 1024], BF16)
        rfm = sbA("rfm", [128, 1024], BF16)
        xnb = sbA("xnb", [128, 1024], BF16)
        qa = sbA("qa", [128, 512])
        qti = sbA("qti", [128, 512], I32)
        xdq = sbA("xdq", [128, 1024], BF16)
        ebuf = sbA("ebuf", [128, 512])
        stat6 = sbA("stat6", [128, 12])
        mv = sbA("mv", [128, 2])
        colsA = sbA("colsA", [128, 32])
        g_lg = sbA("g_lg", [16, 512])
        g_D = sbA("g_D", [16, 512])
        g_A = sbA("g_A", [16, 512])
        g_t1 = sbA("g_t1", [16, 512])
        g_t2 = sbA("g_t2", [16, 512])
        g_lam = sbA("g_lam", [16, 128])
        d3a = sbA("d3a", [16, 512], BF16)
        d3b = sbA("d3b", [16, 512], BF16)
        d3c = sbA("d3c", [16, 512], BF16)
        d3e = sbA("d3e", [16, 512])
        dfl = sbA("dfl", [3, 2048], BF16)
        atok = sbA("atok", [128, 16])
        editok = sbA("editok", [128, 16])
        wdtok = sbA("wdtok", [128, 16])
        lamsb = sbA("lamsb", [128, 16])
        gegs = sbA("gegs", [16, 128])
        gwgs = sbA("gwgs", [16, 128])
        ptb = sbA("ptb", [128, 1024])
        efb = [sbA(f"efb{i}", [128, 1024], BF16) for i in range(2)]
        tcb = sbA("tcb", [128, 520])
        acb = sbA("acb", [128, 520])
        denb = sbA("denb", [128, 8])
        S_f = sbA("S_f", [64, 2048])
        S_bf = sbA("S_bf", [128, 2048], BF16)
        out_sb = [sbA(f"outsb{i}", [128, 1024]) for i in range(2)]
        mask4 = sbA("mask4_s", [128, 512], BF16)
        ones3 = sbA("ones3", [3, 128], BF16)
        bgc = sbA("bgc_s", [16, 2])
        nbw = sbA("nbw", [16, 1])

        P.dma("sync", lambda e: e.dma_start(out=mask4[:], in_=mask4_d), "dW")
        P.dma("sync", lambda e: e.dma_start(out=bgc[:], in_=bg_d), "dW")
        for dc in range(8):
            for dst, src in ((wq_bf, wqT_d), (wk_bf, wkT_d), (wv_bf, wvT_d), (wo_bf, woT_d)):
                P.dma("gpsimd", lambda e, dst=dst, src=src, dc=dc: e.dma_start(
                    out=dst[:, 1024 * dc:1024 * dc + 1024],
                    in_=src[128 * dc:128 * dc + 128, :]), "dG")
            P.dma("gpsimd", lambda e, dc=dc: e.dma_start(
                out=wg_bf[:, 32 * dc:32 * dc + 32],
                in_=wgT_d[128 * dc:128 * dc + 128, :]), "dG")
        wA = P.val("dW")
        P.wait("vector", "dW", wA)
        P.wait("vector", "dG")
        for _i, _v in enumerate((1e-5, 1e-8, 0.5, 0.0)):
            P.emit("vector", lambda e, _i=_i, _v=_v: e.memset(cpool[:, _i:_i + 1], _v))
        P.emit("vector", lambda e: e.memset(flshb[:], 0.0))
        P.emit("vector", lambda e: e.memset(ones3[:], 1.0))
        P.emit("vector", lambda e: e.memset(ones1[:], 1.0))
        P.emit("vector", lambda e: e.memset(S_f[:], 0.0))
        P.emit("vector", lambda e: e.memset(S_bf[:], 0.0))
        for i in range(4):
            P.emit("vector", lambda e, i=i: e.memset(
                vtm[i][:].rearrange("p (h o) -> p h o", o=65)[:, :, 64:65], 1.0))
        P.emit("vector", lambda e: e.tensor_scalar_mul(out=nbw[:], in0=bgc[:, 0:1], scalar1=-1.0),
               inc="sv")
        initV = P.val("sv")
        laststore = {0: 0, 1: 0}
        vPrevTileDone = 0
        tPrevVcons = 0
        ssrfmLast = 0
        dqStoreLast = 0
        vsupPrev = 0
        vcomb1Prev = 0
        stdflFree = 0

        for t in range(nt):
            P.wait("sync", "sv", vPrevTileDone)
            P.dma("sync", lambda e, t=t: e.dma_start(
                out=x_sb[:].rearrange("p (c d) -> p c d", d=DM),
                in_=x_d[TILE * t:TILE * (t + 1), :].rearrange("(c p) d -> p c d", p=128)), "dX")
            xw = P.val("dX")

            P.wait("vector", "dX", xw)
            P.wait("vector", "st", tPrevVcons)
            P.wait("scalar", "dG", dqStoreLast)
            for ch in range(4):
                xc = x_sb[:, 1024 * ch:1024 * ch + 1024]
                cc = colsA[:, 8 * ch:8 * ch + 8]
                P.emit("vector", lambda e, xc=xc: e.tensor_reduce(
                    out=stat6[:, 0:1], in_=xc[:], axis=AX.X, op=ALU.add), inc="sv")
                P.vflush(flshb)
                P.emit("vector", lambda e, cc=cc: e.tensor_single_scalar(
                    out=cc[:, 0:1], in_=stat6[:, 0:1], scalar=1.0 / 1024.0, op=ALU.mult),
                    inc="sv")
                P.wait("scalar", "sv")
                P.emit("scalar", lambda e, xc=xc: e.activation(
                    out=pMi[:, 0:512], in_=xc[:, 0:512], func=AF.Square,
                    accum_out=stat6[:, 1:2]), inc="ss")
                P.wait("scalar", "ss")
                P.emit("scalar", lambda e, xc=xc: e.activation(
                    out=pMi[:, 0:512], in_=xc[:, 512:1024], func=AF.Square,
                    accum_out=stat6[:, 2:3]))
                P.sflush(flshb, inc="ss")
                P.wait("vector", "ss")
                P.emit("vector", lambda e: e.tensor_add(out=stat6[:, 1:2], in0=stat6[:, 1:2],
                                                        in1=stat6[:, 2:3]))
                P.emit("vector", lambda e, cc=cc: e.tensor_mul(out=stat6[:, 3:4],
                                                               in0=cc[:, 0:1], in1=cc[:, 0:1]))
                P.vflush(flshb)
                P.emit("vector", lambda e: e.scalar_tensor_tensor(
                    out=mv[:, 1:2], in0=stat6[:, 1:2], scalar=1.0 / 1024.0,
                    in1=stat6[:, 3:4], op0=ALU.mult, op1=ALU.subtract),
                       inc="sv")
                P.wait("scalar", "sv")
                P.emit("scalar", lambda e: e.activation(out=mv[:, 1:2], in_=mv[:, 1:2],
                                                        func=AF.Ln, bias=1e-5))
                P.emit("scalar", lambda e, cc=cc: e.activation(out=cc[:, 1:2], in_=mv[:, 1:2],
                                                               func=AF.Exp, scale=-0.5),
                       inc="ss")
                P.emit("vector", lambda e, xc=xc, cc=cc: e.tensor_scalar(
                    out=xc[:], in0=xc[:], scalar1=cc[:, 0:1], scalar2=0.0,
                    op0=ALU.subtract, op1=ALU.max, accum_out=cc[:, 2:3]), inc="sv")
                P.emit("vector", lambda e, xc=xc, cc=cc: e.tensor_reduce(
                    out=cc[:, 7:8], in_=xc[:], axis=AX.X, op=ALU.min), inc="sv")
                P.vflush(flshb)
                P.emit("vector", lambda e, cc=cc: e.tensor_scalar_mul(
                    out=cc[:, 7:8], in0=cc[:, 7:8], scalar1=-1.0))
                P.vflush(flshb)
                P.emit("vector", lambda e, cc=cc: e.tensor_max(
                    out=cc[:, 2:3], in0=cc[:, 2:3], in1=cc[:, 7:8]))
                P.wait("vector", "ss")
                P.emit("vector", lambda e, cc=cc: e.tensor_scalar_mul(
                    out=cc[:, 6:7], in0=cc[:, 1:2], scalar1=127.0))
                P.emit("vector", lambda e, cc=cc: e.tensor_mul(out=cc[:, 3:4], in0=cc[:, 2:3],
                                                               in1=cc[:, 1:2]))
                P.vflush(flshb)
                P.emit("vector", lambda e, cc=cc: e.tensor_scalar_max(
                    out=cc[:, 3:4], in0=cc[:, 3:4], scalar1=1e-5))
                P.vflush(flshb)
                P.emit("vector", lambda e, cc=cc: e.tensor_single_scalar(
                    out=cc[:, 4:5], in_=cc[:, 3:4], scalar=1.0 / 127.0, op=ALU.mult))
                P.emit("vector", lambda e, cc=cc: e.reciprocal(out=cc[:, 7:8], in_=cc[:, 3:4]))
                P.vflush(flshb)
                P.emit("vector", lambda e, cc=cc: e.tensor_mul(
                    out=cc[:, 5:6], in0=cc[:, 6:7], in1=cc[:, 7:8]))
                P.emit("vector", lambda e, xc=xc, cc=cc: e.tensor_scalar_mul(
                    out=xnb[:], in0=xc[:], scalar1=cc[:, 1:2]))
                for qh in range(2):
                    qsl = slice(512 * qh, 512 * qh + 512)
                    P.emit("vector", lambda e, xc=xc, cc=cc, qsl=qsl: e.tensor_scalar(
                        out=qa[:], in0=xc[:, qsl], scalar1=cc[:, 5:6], scalar2=0.5,
                        op0=ALU.mult, op1=ALU.add))
                    P.emit("vector", lambda e: e.tensor_copy(out=qti[:], in_=qa[:]))
                    P.emit("vector", lambda e: e.tensor_tensor(out=ebuf[:], in0=qti[:],
                                                               in1=qa[:], op=ALU.is_gt))
                    P.emit("vector", lambda e: e.tensor_tensor(out=qa[:], in0=qti[:],
                                                               in1=ebuf[:], op=ALU.subtract))
                    P.emit("vector", lambda e, cc=cc, qsl=qsl: e.tensor_scalar_mul(
                        out=xdq[:, qsl], in0=qa[:], scalar1=cc[:, 4:5]), inc="sv")
                P.wait("tensor", "sv")
                if t > 0 and ch == 0:
                    P.wait("tensor", "ss", ssrfmLast)
                for dc in range(8):
                    P.emit("tensor", lambda e, dc=dc: e.transpose(
                        out=pTr[:, 128 * dc:128 * dc + 128],
                        in_=xnb[:, 128 * dc:128 * dc + 128], identity=idbf[:]))
                P.inc_last("tensor", "st")
                stxnb = P.val("st")
                P.wait("scalar", "st", stxnb)
                P.emit("scalar", lambda e, ch=ch: e.copy(
                    out=xn_fm[:].rearrange("p (d c) -> p d c", c=512)[:, :, 128 * ch:128 * ch + 128],
                    in_=pTr[:].rearrange("p (d c) -> p d c", c=128)), inc="ss")
                P.wait("tensor", "ss", P.val("ss"))
                for dc in range(8):
                    P.emit("tensor", lambda e, dc=dc: e.transpose(
                        out=pTr[:, 128 * dc:128 * dc + 128],
                        in_=xdq[:, 128 * dc:128 * dc + 128], identity=idbf[:]))
                P.inc_last("tensor", "st")
                stxdq = P.val("st")
                P.wait("scalar", "st", stxdq)
                P.emit("scalar", lambda e, ch=ch: e.copy(
                    out=xq_fm[:].rearrange("p (d c) -> p d c", c=512)[:, :, 128 * ch:128 * ch + 128],
                    in_=pTr[:].rearrange("p (d c) -> p d c", c=128)), inc="ss")
            vxn_ss = P.val("ss")
            P.wait("gpsimd", "ss", vxn_ss)
            P.dma("gpsimd", lambda e, t=t: e.dma_start(out=xqs_d[t], in_=xq_fm[:]), "dG")
            dqStoreLast = P.val("dG")
            if debug and t == 0:
                P.dma("sync", lambda e: e.dma_start(out=dbg["dbg_xnfm"], in_=xn_fm[:]), "dF")
                P.dma("sync", lambda e: e.dma_start(out=dbg["dbg_xq"], in_=xq_fm[:]), "dF")

            P.wait("tensor", "ss", vxn_ss)
            for dc in range(8):
                P.emit("tensor", lambda e, dc=dc: e.matmul(
                    out=pMi[0:16, 0:512], lhsT=wg_bf[:, 32 * dc:32 * dc + 16],
                    rhs=xn_fm[:, 512 * dc:512 * dc + 512], start=(dc == 0), stop=(dc == 7)))
            for dc in range(8):
                P.emit("tensor", lambda e, dc=dc: e.matmul(
                    out=pAcc[0:16, 0:512], lhsT=wg_bf[:, 32 * dc + 16:32 * dc + 32],
                    rhs=xn_fm[:, 512 * dc:512 * dc + 512], start=(dc == 0), stop=(dc == 7)))
            P.inc_last("tensor", "st")
            P.wait("scalar", "st")
            P.wait("scalar", "sv", initV)
            P.emit("scalar", lambda e: e.activation(out=g_t1[:], in_=pMi[0:16, 0:512],
                                                    func=AF.Exp, scale=-1.0, bias=nbw[:, 0:1]))
            P.emit("scalar", lambda e: e.activation(out=g_t2[:], in_=pAcc[0:16, 0:512],
                                                    func=AF.Exp, scale=1.0, bias=bgc[:, 1:2]),
                   inc="ss")
            gatePsumFree = P.val("ss")
            P.wait("vector", "ss")
            P.emit("vector", lambda e: e.tensor_scalar_add(out=g_t1[:], in0=g_t1[:], scalar1=1.0))
            P.emit("vector", lambda e: e.tensor_scalar_add(out=g_t2[:], in0=g_t2[:], scalar1=1.0))
            P.vflush(flshb)
            P.emit("vector", lambda e: e.reciprocal(out=g_t2[:], in_=g_t2[:]))
            P.vflush(flshb, inc="sv")
            P.wait("scalar", "sv")
            P.emit("scalar", lambda e: e.activation(out=g_t1[:], in_=g_t1[:], func=AF.Ln))
            P.emit("scalar", lambda e: e.activation(out=g_lg[:], in_=g_t2[:], func=AF.Ln,
                                                    scale=0.95, bias=1e-8), inc="ss")
            P.wait("vector", "ss")
            for ch in range(4):
                P.emit("vector", lambda e, ch=ch: e.tensor_tensor_scan(
                    out=g_D[:, 128 * ch:128 * ch + 128], data0=g_lg[:, 128 * ch:128 * ch + 128],
                    data1=g_lg[:, 128 * ch:128 * ch + 128], initial=0.0,
                    op0=ALU.add, op1=ALU.bypass), inc="sv")
            P.vflush(flshb)
            P.emit("vector", lambda e: e.scalar_tensor_tensor(
                out=g_A[:], in0=g_t1[:], scalar=-1.0, in1=g_D[:],
                op0=ALU.mult, op1=ALU.subtract))
            P.emit("vector", lambda e: e.tensor_copy(out=d3a[:], in_=g_D[:]))
            P.emit("vector", lambda e: e.tensor_tensor(out=d3e[:], in0=g_D[:], in1=d3a[:],
                                                       op=ALU.subtract))
            P.emit("vector", lambda e: e.tensor_copy(out=d3b[:], in_=d3e[:]))
            P.emit("vector", lambda e: e.tensor_tensor(out=d3e[:], in0=d3e[:], in1=d3b[:],
                                                       op=ALU.subtract))
            P.emit("vector", lambda e: e.tensor_copy(out=d3c[:], in_=d3e[:]), inc="sv")
            vgate = P.val("sv")
            if debug and t == 0:
                P.wait("sync", "sv", vgate)
                P.dma("sync", lambda e: e.dma_start(out=dbg["dbg_gD"], in_=g_D[:]), "dF")
                P.dma("sync", lambda e: e.dma_start(out=dbg["dbg_gA"], in_=g_A[:]), "dF")

            P.wait("tensor", "ss", gatePsumFree)
            banklast = {id(pMi): 0, id(pSc): 0}
            qkit = 0
            for wmat, dst in ((wq_bf, qfm), (wk_bf, kfm)):
                for oc in range(8):
                    pb = pMi if (qkit % 2 == 0) else pSc
                    eb = ebuf if (qkit % 2 == 0) else qa
                    P.wait("tensor", "sv", banklast[id(pb)])
                    for dc in range(8):
                        P.emit("tensor", lambda e, oc=oc, dc=dc, wmat=wmat, pb=pb: e.matmul(
                            out=pb[:, 0:512],
                            lhsT=wmat[:, 1024 * dc + 128 * oc:1024 * dc + 128 * oc + 128],
                            rhs=xn_fm[:, 512 * dc:512 * dc + 512],
                            start=(dc == 0), stop=(dc == 7)))
                    P.inc_last("tensor", "st")
                    stqk = P.val("st")
                    P.wait("scalar", "st", stqk)
                    P.wait("scalar", "sv", banklast[id(pb)])
                    P.emit("scalar", lambda e, pb=pb, eb=eb: e.activation(
                        out=eb[:], in_=pb[:, 0:512], func=AF.Exp), inc="ss")
                    ssqk = P.val("ss")
                    P.wait("vector", "ss", ssqk)
                    P.emit("vector", lambda e, eb=eb: e.tensor_scalar_min(
                        out=eb[:], in0=eb[:], scalar1=1.0))
                    P.emit("vector", lambda e, oc=oc, dst=dst, pb=pb, eb=eb:
                           e.scalar_tensor_tensor(
                        out=dst[:, 512 * oc:512 * oc + 512], in0=pb[:, 0:512], scalar=0.0,
                        in1=eb[:], op0=ALU.max, op1=ALU.add), inc="sv")
                    banklast[id(pb)] = P.val("sv")
                    qkit += 1
            if debug and t == 0:
                P.wait("sync", "sv")
                P.dma("sync", lambda e: e.dma_start(out=dbg["dbg_q"], in_=qfm[:]), "dF")
                P.dma("sync", lambda e: e.dma_start(out=dbg["dbg_k"], in_=kfm[:]), "dF")

            vcop = {}
            for ch in range(4):
                if ch > 0:
                    P.wait("tensor", "ss", vcop[ch - 1])
                else:
                    P.wait("tensor", "sv", vPrevTileDone)
                for dc in range(8):
                    for hf in range(2):
                        P.emit("tensor", lambda e, dc=dc, ch=ch, hf=hf: e.matmul(
                            out=pAcc[:, 512 * hf:512 * hf + 512],
                            lhsT=xn_fm[:, 512 * dc + 128 * ch:512 * dc + 128 * ch + 128],
                            rhs=wv_bf[:, 1024 * dc + 512 * hf:1024 * dc + 512 * hf + 512],
                            start=(dc == 0), stop=(dc == 7)))
                P.inc_last("tensor", "st")
                stv = P.val("st")
                P.wait("scalar", "st", stv)
                P.emit("scalar", lambda e, ch=ch: e.copy(
                    out=vtm[ch][:].rearrange("p (h o) -> p h o", o=65)[:, :, 0:64],
                    in_=pAcc[:].rearrange("p (h o) -> p h o", o=64)), inc="ss")
                vcop[ch] = P.val("ss")
            if debug and t == 0:
                P.wait("sync", "sv")
                P.dma("sync", lambda e: e.dma_start(out=dbg["dbg_v"], in_=vtm[0][:]), "dF")

            for ch in range(4):
                gsl = slice(128 * ch, 128 * ch + 128)
                dend = g_D[:, 128 * ch + 127:128 * ch + 128]
                P.wait("scalar", "sv", vgate)
                P.wait("scalar", "st", stdflFree)
                for i3, s3 in enumerate((d3a, d3b, d3c)):
                    P.dma("scalar", lambda e, i3=i3, s3=s3, gsl=gsl: e.dma_start(
                        out=dfl[i3:i3 + 1, :], in_=s3[:, gsl]), "dF")
                fD = P.val("dF")
                P.emit("scalar", lambda e, gsl=gsl: e.activation(
                    out=gegs[:], in_=g_D[:, gsl], func=AF.Exp))
                P.emit("scalar", lambda e, gsl=gsl, dend=dend: e.activation(
                    out=gwgs[:], in_=g_A[:, gsl], func=AF.Exp, bias=dend), inc="ss")
                sgt = P.val("ss")
                P.wait("vector", "ss", sgt)
                P.emit("vector", lambda e: e.tensor_copy(
                    out=g_lam[:], in_=gegs[:, 127:128].broadcast_to([16, 128])), inc="sv")
                vlam = P.val("sv")
                P.wait("tensor", "sv", vlam)
                P.emit("tensor", lambda e, gsl=gsl: e.transpose(
                    out=pMi[:, 0:16], in_=g_A[:, gsl], identity=idf32[0:16, 0:16]))
                P.emit("tensor", lambda e: e.transpose(out=pMi[:, 16:32], in_=gegs[:],
                                                       identity=idf32[0:16, 0:16]))
                P.emit("tensor", lambda e: e.transpose(out=pMi[:, 32:48], in_=gwgs[:],
                                                       identity=idf32[0:16, 0:16]))
                P.emit("tensor", lambda e: e.transpose(out=pMi[:, 48:64], in_=g_lam[:],
                                                       identity=idf32[0:16, 0:16]), inc="st")
                P.wait("vector", "st")
                P.emit("vector", lambda e: e.tensor_copy(out=atok[:], in_=pMi[:, 0:16]))
                P.emit("vector", lambda e: e.tensor_copy(out=editok[:], in_=pMi[:, 16:32]))
                P.emit("vector", lambda e: e.tensor_copy(out=wdtok[:], in_=pMi[:, 32:48]))
                P.emit("vector", lambda e: e.tensor_copy(out=lamsb[:], in_=pMi[:, 48:64]),
                       inc="sv")
                vtok = P.val("sv")
                for c2 in range(8):
                    P.emit("tensor", lambda e, c2=c2, ch=ch: e.transpose(
                        out=pTr[:, 128 * c2:128 * c2 + 128],
                        in_=kfm[:, 512 * c2 + 128 * ch:512 * c2 + 128 * ch + 128],
                        identity=idbf[:]))
                P.inc_last("tensor", "st")
                stktm = P.val("st")
                P.wait("scalar", "st", stktm)
                P.emit("scalar", lambda e: e.copy(out=ktm[:], in_=pTr[:]), inc="ss")
                ssktm = P.val("ss")
                P.wait("vector", "ss", ssktm)
                P.emit("vector", lambda e: e.tensor_tensor(
                    out=khat[:].rearrange("p (h o) -> p h o", o=64),
                    in0=ktm[:].rearrange("p (h o) -> p h o", o=64),
                    in1=wdtok[:].unsqueeze(2).broadcast_to([128, 16, 64]),
                    op=ALU.mult), inc="sv")
                vkhat = P.val("sv")

                def sc_blk(w, jj, width):
                    if w == 0:
                        return pSc[:, 128 * jj:128 * jj + width]
                    pb_ = pMi if jj < 4 else pTrF
                    return pb_[:, 128 * (jj % 4):128 * (jj % 4) + width]

                veff_w = {}
                vcomb_w = {}
                for w in range(2):
                    dkb = pDk if w == 0 else pAcc
                    if w == 0:
                        P.wait("tensor", "sv", vsupPrev)
                    else:
                        P.wait("tensor", "sv", max(vtok, vcomb1Prev))
                        P.wait("tensor", "ss", ssktm)
                    P.wait("tensor", "dF", fD)
                    for cp in range(4):
                        c = 4 * w + cp
                        for half in range(2):
                            b0, j = 64 * half, cp + 4 * half
                            P.emit("tensor", lambda e, c=c, b0=b0, j=j, ch=ch, w=w: e.matmul(
                                out=sc_blk(w, j, 128),
                                lhsT=kfm[b0:b0 + 64, 512 * c + 128 * ch:512 * c + 128 * ch + 128],
                                rhs=qfm[b0:b0 + 64, 512 * c + 128 * ch:512 * c + 128 * ch + 128],
                                start=True, stop=True))
                    dflv = dfl[:, :].rearrange("k (hw hm par i) -> k hw par hm i",
                                               hw=2, hm=4, par=2, i=128)
                    for half in range(2):
                        P.emit("tensor", lambda e, half=half, dkb=dkb: e.matmul(
                            out=dkb[:, 512 * half:512 * half + 512], lhsT=idbf[:],
                            rhs=mask4[:], start=True, stop=False))
                        P.emit("tensor", lambda e, w=w, half=half, dflv=dflv, dkb=dkb: e.matmul(
                            out=dkb[:, 512 * half:512 * half + 512], lhsT=ones3[:],
                            rhs=dflv[:, w, half, :, :],
                            start=False, stop=True))
                    P.inc_last("tensor", "st")
                    if w == 1:
                        stdflFree = P.val("st")
                    P.wait("scalar", "st")
                    P.wait("scalar", "sv", vtok)
                    if w == 1:
                        P.wait("scalar", "sv", veff_w[0])
                    for j in range(8):
                        sg_ = BLK[8 * w + j]
                        P.emit("scalar", lambda e, j=j, sg_=sg_, dkb=dkb: e.activation(
                            out=ptb[:, 128 * j:128 * j + 128], in_=dkb[:, 128 * j:128 * j + 128],
                            func=AF.Exp, bias=atok[:, sg_:sg_ + 1]))
                    P.emit("scalar", lambda e: e.copy(out=ptb[0:1, 0:1], in_=ptb[0:1, 0:1]),
                           inc="ss")
                    P.wait("vector", "ss")
                    if w == 0:
                        P.emit("vector", lambda e, w=w: e.tensor_tensor(
                            out=efb[w][:], in0=pSc[:], in1=ptb[:], op=ALU.mult), inc="sv")
                    else:
                        P.emit("vector", lambda e, w=w: e.tensor_tensor(
                            out=efb[w][:, 0:512], in0=pMi[:, 0:512], in1=ptb[:, 0:512],
                            op=ALU.mult))
                        P.emit("vector", lambda e, w=w: e.tensor_tensor(
                            out=efb[w][:, 512:1024], in0=pTrF[:, 0:512],
                            in1=ptb[:, 512:1024], op=ALU.mult), inc="sv")
                    veff = P.val("sv")
                    veff_w[w] = veff
                    if debug and t == 0 and ch == 0 and w == 0:
                        P.wait("sync", "sv", veff)
                        P.dma("sync", lambda e: e.dma_start(out=dbg["dbg_eff"], in_=efb[0][:]),
                              "dF")
                    P.wait("tensor", "sv", veff)
                    for j in range(8):
                        b = 8 * w + j
                        P.emit("tensor", lambda e, w=w, j=j, b=b, ch=ch: e.matmul(
                            out=sc_blk(w, j, 65),
                            lhsT=efb[w][:, 128 * j:128 * j + 128],
                            rhs=vtm[ch][:, 65 * b:65 * b + 65], start=True, stop=True))
                    for cp in range(4):
                        c = 4 * w + cp
                        for half in range(2):
                            b0, j = 64 * half, cp + 4 * half
                            b = 8 * w + j
                            P.emit("tensor", lambda e, c=c, b0=b0, j=j, b=b, ch=ch, dkb=dkb:
                                   e.matmul(
                                out=dkb[:, 128 * j:128 * j + 65],
                                lhsT=qfm[b0:b0 + 64, 512 * c + 128 * ch:512 * c + 128 * ch + 128],
                                rhs=S_bf[b0:b0 + 64, 128 * b:128 * b + 65],
                                start=True, stop=True))
                    P.inc_last("tensor", "st")
                    P.wait("vector", "st")
                    P.emit("vector", lambda e, w=w, dkb=dkb: e.tensor_tensor(
                        out=tcb[:].rearrange("p (hi lo o) -> p hi lo o", hi=2, lo=4),
                        in0=dkb[:].rearrange("p (hi lo o) -> p hi lo o",
                                             hi=2, lo=4)[:, :, :, 0:65],
                        in1=_slotgrid(editok, w, 65), op=ALU.mult))
                    if w == 0:
                        P.emit("vector", lambda e: e.tensor_add(
                            out=acb[:].rearrange("p (j o) -> p j o", o=65),
                            in0=tcb[:].rearrange("p (j o) -> p j o", o=65),
                            in1=pSc[:].rearrange("p (j o) -> p j o", o=128)[:, :, 0:65]))
                    else:
                        P.emit("vector", lambda e: e.tensor_add(
                            out=acb[:, 0:260].rearrange("p (j o) -> p j o", o=65),
                            in0=tcb[:, 0:260].rearrange("p (j o) -> p j o", o=65),
                            in1=pMi[:].rearrange("p (j o) -> p j o", o=128)[:, :, 0:65]))
                        P.emit("vector", lambda e: e.tensor_add(
                            out=acb[:, 260:520].rearrange("p (j o) -> p j o", o=65),
                            in0=tcb[:, 260:520].rearrange("p (j o) -> p j o", o=65),
                            in1=pTrF[:].rearrange("p (j o) -> p j o", o=128)[:, :, 0:65]))
                    P.vflush(flshb)
                    P.emit("vector", lambda e: e.tensor_single_scalar(
                        out=denb[:], in_=acb[:].rearrange("p (j o) -> p j o", o=65)[:, :, 64:65],
                        scalar=1e-6, op=ALU.add))
                    P.vflush(flshb)
                    P.emit("vector", lambda e: e.reciprocal(out=denb[:], in_=denb[:]))
                    P.vflush(flshb)
                    P.emit("vector", lambda e, w=w: e.tensor_tensor(
                        out=rtm[:, 512 * w:512 * w + 512].rearrange("p (j o) -> p j o", o=64),
                        in0=acb[:].rearrange("p (j o) -> p j o", o=65)[:, :, 0:64],
                        in1=denb[:].unsqueeze(2).broadcast_to([128, 8, 64]),
                        op=ALU.mult), inc="sv")
                    vcomb_w[w] = P.val("sv")
                vcomb1Prev = vcomb_w[1]
                P.wait("tensor", "sv", vcomb_w[0])
                for b in range(16):
                    hh = BLK[b]
                    pout, j = (pSc, b) if b < 8 else (pDk, b - 8)
                    P.emit("tensor", lambda e, hh=hh, pout=pout, j=j, b=b, ch=ch: e.matmul(
                        out=pout[0:64, 128 * j:128 * j + 65],
                        lhsT=khat[:, 64 * hh:64 * hh + 64],
                        rhs=vtm[ch][:, 65 * b:65 * b + 65], start=True, stop=True))
                P.inc_last("tensor", "st")
                P.wait("vector", "st")
                for w in range(2):
                    pin = pSc if w == 0 else pDk
                    sv_ = S_f[:, 1024 * w:1024 * w + 1024].rearrange(
                        "p (hi lo o) -> p hi lo o", hi=2, lo=4)[:, :, :, 0:65]
                    P.emit("vector", lambda e, sv_=sv_, w=w: e.tensor_tensor(
                        out=sv_, in0=sv_, in1=_slotgrid(lamsb, w, 65, 64), op=ALU.mult))
                    P.emit("vector", lambda e, sv_=sv_, pin=pin: e.tensor_add(
                        out=sv_, in0=sv_,
                        in1=pin[0:64, :].rearrange("p (hi lo o) -> p hi lo o",
                                                   hi=2, lo=4)[:, :, :, 0:65]))
                P.emit("vector", lambda e: e.tensor_copy(out=S_bf[0:64, :], in_=S_f[:]),
                       inc="sv")
                vsup = P.val("sv")
                vsupPrev = vsup
                P.wait("scalar", "sv", vsup)
                P.dma("scalar", lambda e: e.dma_start(out=S_bf[64:128, :], in_=S_bf[0:64, :]),
                      "dF")
                fS = P.val("dF")
                if debug and t == 0 and ch == 0:
                    P.wait("sync", "sv", vsup)
                    P.dma("sync", lambda e: e.dma_start(out=dbg["dbg_S"], in_=S_f[:]), "dF")
                    P.dma("sync", lambda e: e.dma_start(out=dbg["dbg_R"], in_=rtm[:]), "dF")

                P.wait("tensor", "sv", vcomb_w[1])
                for fc in range(8):
                    P.emit("tensor", lambda e, fc=fc: e.transpose(
                        out=pTr[:, 128 * fc:128 * fc + 128], in_=rtm[:, 128 * fc:128 * fc + 128],
                        identity=idbf[:]))
                P.inc_last("tensor", "st")
                strt = P.val("st")
                P.wait("scalar", "st", strt)
                P.emit("scalar", lambda e: e.copy(out=rfm[:], in_=pTr[:]), inc="ss")
                ssrfmLast = P.val("ss")
                P.wait("tensor", "ss", ssrfmLast)
                for fc in range(8):
                    for hf in range(2):
                        P.emit("tensor", lambda e, fc=fc, hf=hf: e.matmul(
                            out=pAcc[:, 512 * hf:512 * hf + 512],
                            lhsT=rfm[:, 128 * fc:128 * fc + 128],
                            rhs=wo_bf[:, 1024 * fc + 512 * hf:1024 * fc + 512 * hf + 512],
                            start=(fc == 0), stop=(fc == 7)))
                P.inc_last("tensor", "st")
                P.wait("vector", "st")
                ob = out_sb[ch % 2]
                P.wait("vector", "dO", laststore[ch % 2])
                P.emit("vector", lambda e, ob=ob, ch=ch: e.scalar_tensor_tensor(
                    out=ob[:], in0=x_sb[:, 1024 * ch:1024 * ch + 1024],
                    scalar=colsA[:, 8 * ch:8 * ch + 1], op0=ALU.add,
                    in1=pAcc[:], op1=ALU.add), inc="sv")
                vout = P.val("sv")
                P.wait("sync", "sv", vout)
                P.dma("sync", lambda e, t=t, ch=ch, ob=ob: e.dma_start(
                    out=y_d[TILE * t + 128 * ch:TILE * t + 128 * ch + 128, :], in_=ob[:]), "dO")
                laststore[ch % 2] = P.val("dO")
            vPrevTileDone = P.val("sv")
            tPrevVcons = P.val("st")

        endA_sv, endA_ss, endA_st = P.val("sv"), P.val("ss"), P.val("st")

        _bcur = [phase_base]
        _boff = {}

        def sbB(name, shape, dt=F32, at=None):
            esz = {F32: 4, BF16: 2, I32: 4}[dt]
            nb = shape[1] * esz
            off = _bcur[0] if at is None else at
            h = nc.alloc_sbuf_tensor_at(name, shape, dt, offset=off)
            _boff[name] = off
            if at is None:
                _bcur[0] = off + ((nb + 31) // 32) * 32
            return h

        w1q = sbB("w1q", [128, 32768], BF16)
        w2q = sbB("w2q", [128, 32768], BF16)
        xqB = [sbB(f"xqB{i}", [128, 4096], BF16) for i in range(2)]
        opB2 = [sbB(f"opB{i}", [128, 1024]) for i in range(2)]
        hsq = sbB("hsq", [128, 16384], BF16)
        hb = [sbB(f"hb{i}", [128, 514]) for i in range(3)]
        hsave = sbB("hsave", [128, 64])
        maccD = sbB("maccD", [128, 512])
        maccN = sbB("maccN", [128, 512])
        s2b = sbB("s2b", [128, 512], F32, at=_boff["maccD"])
        aB = [sbB("aB0", [128, 512])] * 2
        tiB = [sbB("tiB0", [128, 512], I32)] * 2
        convW = sbB("convW", [128, 96])
        convB = sbB("convB", [128, 32])
        colsB = sbB("colsB", [128, 16])
        rowB = sbB("rowB", [1, 512])
        yrow = sbB("yrow", [1, 1024])
        tfB = [sbB("tfB0", [128, 512], F32, at=_boff["rowB"])]
        cB = [sbB("cB0", [128, 512], F32, at=_boff["yrow"])]
        scl = sbB("scl", [1, 8])
        gfb = sbB("gfb", [128, 2])
        epf = sbB("epf", [128, 128], F32, at=_boff["aB0"])
        epb = sbB("epb", [128, 32], BF16, at=_boff["aB0"] + 512)
        epi32 = sbB("epi32", [128, 32], I32, at=_boff["aB0"] + 576)
        wstr = [sbB("wstr0", [128, 4096], F32, at=_boff["hsq"]),
                sbB("wstr1", [128, 4096], F32, at=_boff["hsq"] + 16384)]

        for e in ENGS:
            P.wait(e, "sv", endA_sv)
            P.wait(e, "ss", endA_ss)
            P.wait(e, "st", endA_st)
            P.wait(e, "dO")
            P.wait(e, "dQ")
            P.wait(e, "dF")
            P.wait(e, "dG")
        P.dma("sync", lambda e: e.dma_start(out=convW[:], in_=cw_d), "dW")
        P.dma("sync", lambda e: e.dma_start(out=convB[:], in_=cb_d), "dW")
        wB0 = P.val("dW")
        P.wait("vector", "dW", wB0)
        P.emit("vector", lambda e: e.memset(colsB[:, 3:4], 1.0))
        P.emit("vector", lambda e: e.memset(opB2[0][0:1, :], 0.0))
        P.emit("vector", lambda e: e.memset(hsave[:], 0.0), inc="sv")
        P.emit("gpsimd", lambda e: e.memset(gfb[:], 0.0))

        WQS = (("sync", "dW"), ("scalar", "dQ"), ("gpsimd", "dG"))

        def wquant(src_d, nch, chw, dstq, colix, bufs):
            nb = len(bufs)
            use_done = [0] * nb
            for i in range(nch):
                eng, sem = WQS[i % len(WQS)]
                P.wait(eng, "sv", use_done[i % nb])
                P.dma(eng, lambda e, i=i, src_d=src_d, chw=chw, b=bufs[i % nb]:
                      e.dma_start(out=b[:, 0:chw],
                                  in_=src_d[128 * i:128 * i + 128, :]), sem)
                P.wait("vector", sem)
                P.emit("vector", lambda e, i=i, chw=chw, b=bufs[i % nb]: e.tensor_reduce(
                    out=maccD[:, i % 4:i % 4 + 1], in_=b[:, 0:chw],
                    axis=AX.X, op=ALU.add, apply_absolute_value=True), inc="sv")
                use_done[i % nb] = P.val("sv")
                P.vflush(flshb)
                if i == 0:
                    P.emit("vector", lambda e: e.tensor_copy(
                        out=maccD[:, 4:5], in_=maccD[:, 0:1]), inc="sv")
                else:
                    P.emit("vector", lambda e, i=i: e.tensor_add(
                        out=maccD[:, 4:5], in0=maccD[:, 4:5],
                        in1=maccD[:, i % 4:i % 4 + 1]), inc="sv")
            P.wait("tensor", "sv")
            P.emit("tensor", lambda e: e.matmul(out=pMi[0:1, 0:1], lhsT=maccD[:, 4:5],
                                                rhs=colsB[:, 3:4], start=True, stop=True),
                   inc="st")
            P.wait("vector", "st")
            P.emit("vector", lambda e: e.tensor_single_scalar(
                out=scl[0:1, 0:1], in_=pMi[0:1, 0:1], scalar=1.0 / (4096.0 * 1024.0),
                op=ALU.mult))
            P.vflush(flshb)
            P.emit("vector", lambda e: e.tensor_scalar_max(out=scl[0:1, 0:1],
                                                           in0=scl[0:1, 0:1], scalar1=1e-5))
            P.vflush(flshb)
            P.emit("vector", lambda e: e.reciprocal(out=scl[0:1, 1:2], in_=scl[0:1, 0:1]))
            P.vflush(flshb, inc="sv")
            P.wait("tensor", "sv")
            P.emit("tensor", lambda e: e.matmul(out=pMi[:, 4:5], lhsT=ones1[0:1, :],
                                                rhs=scl[0:1, 1:2], start=True, stop=True),
                   inc="st")
            P.wait("vector", "st")
            P.emit("vector", lambda e, colix=colix: e.tensor_copy(
                out=colsB[:, colix:colix + 1], in_=pMi[:, 4:5]), inc="sv")
            P.vflush(flshb)
            for i in range(nch):
                eng, sem = WQS[i % len(WQS)]
                P.wait(eng, "sv", use_done[i % nb])
                P.dma(eng, lambda e, i=i, src_d=src_d, chw=chw, b=bufs[i % nb]:
                      e.dma_start(out=b[:, 0:chw],
                                  in_=src_d[128 * i:128 * i + 128, :]), sem)
                P.wait("vector", sem)
                wb = bufs[i % nb]
                P.emit("vector", lambda e, wb=wb, chw=chw, colix=colix: e.tensor_scalar(
                    out=wb[:, 0:chw], in0=wb[:, 0:chw], scalar1=colsB[:, colix:colix + 1],
                    scalar2=4.5, op0=ALU.mult, op1=ALU.add))
                for s0 in range(0, chw, 512):
                    P.emit("vector", lambda e, wb=wb, s0=s0: e.tensor_copy(
                        out=tiB[0][:], in_=wb[:, s0:s0 + 512]))
                    P.emit("vector", lambda e, wb=wb, s0=s0: e.tensor_tensor(
                        out=cB[0][:], in0=tiB[0][:], in1=wb[:, s0:s0 + 512], op=ALU.is_gt))
                    P.emit("vector", lambda e, wb=wb, s0=s0: e.tensor_tensor(
                        out=wb[:, s0:s0 + 512], in0=tiB[0][:], in1=cB[0][:], op=ALU.subtract))
                P.emit("vector", lambda e, wb=wb, chw=chw: e.tensor_scalar(
                    out=wb[:, 0:chw], in0=wb[:, 0:chw], scalar1=-4.0, scalar2=1.0,
                    op0=ALU.add, op1=ALU.min))
                P.emit("vector", lambda e, wb=wb, chw=chw, i=i, dstq=dstq: e.tensor_single_scalar(
                    out=dstq[:, chw * i:chw * i + chw], in_=wb[:, 0:chw], scalar=-1.0,
                    op=ALU.max), inc="sv")
                use_done[i % nb] = P.val("sv")

        import os as _os0
        _qv = int(_os0.environ.get("QV", "2"))
        wstr4 = [sbB(f"wst4_{k}", [128, 4096], F32, at=_boff["w2q"] + 16384 * k)
                 for k in range(4)]
        wstr8 = [sbB(f"wst8_{k}", [128, 1024], F32, at=_boff["hsq"] + 4096 * k)
                 for k in range(8)]
        if do_b and _qv >= 1:
            wquant(w1T_d, 8, 4096, w1q, 0, wstr4)
        if do_b and _qv >= 2:
            wquant(w2T_d, 32, 1024, w2q, 1, wstr8)
        P.wait("vector", "sv")
        if not do_b:
            P.emit("vector", lambda e: e.memset(colsB[:], 1.0), inc="sv")
        P.vflush(flshb)
        P.emit("vector", lambda e: e.reciprocal(out=colsB[:, 2:3], in_=colsB[:, 1:2]))
        P.vflush(flshb)
        P.emit("vector", lambda e: e.reciprocal(out=scl[0:1, 3:4], in_=colsB[0:1, 0:1]))
        P.vflush(flshb, inc="sv")
        P.wait("tensor", "sv")
        P.emit("tensor", lambda e: e.matmul(out=pMi[:, 6:7], lhsT=ones1[0:1, :],
                                            rhs=scl[0:1, 3:4], start=True, stop=True), inc="st")
        P.wait("vector", "st")
        P.emit("vector", lambda e: e.tensor_copy(out=colsB[:, 15:16], in_=pMi[:, 6:7]))
        P.vflush(flshb)
        P.emit("vector", lambda e: e.tensor_scalar_mul(out=convW[:], in0=convW[:],
                                                       scalar1=colsB[:, 15:16]), inc="sv")
        if debug:
            P.wait("sync", "sv")
            P.dma("sync", lambda e: e.dma_start(out=dbg["dbg_w1q"], in_=w1q[:, 0:4096]), "dF")

        import os as _osB
        lastout = {0: 0, 1: 0}
        lastopcons = {0: 0, 1: 0}
        lastxqT = {0: 0, 1: 0}

        P.wait("tensor", "sv")
        ssevac = {}
        svconv = {}
        svround = {}
        svstt = {}
        stmacc = {}
        w2banks = [(pAcc, 0), (pAcc, 512), (pSc, 0), (pSc, 512),
                   (pDk, 0), (pDk, 512), (pTrF, 0), (pMi, 0)]
        for t in range(nt if do_b else 0):
            w0 = TILE * t - 1
            xb = xqB[t % 2]
            P.wait("gpsimd", "st", lastxqT[t % 2])
            P.dma("gpsimd", lambda e, t=t, xb=xb: e.dma_start(out=xb[:], in_=xqs_d[t]), "dG")
            ldx = P.val("dG")
            P.wait("tensor", "dG", ldx)
            if t > 0:
                P.wait("tensor", "sv", svstt[(t - 1, 3)])
                P.wait("tensor", "sv", svstt[(t - 1, 1)])

            for hc in range(32):
                slot = hc % 3
                pb = pMi if hc % 2 == 0 else pSc
                if hc >= 2:
                    P.wait("tensor", "ss", ssevac[(t, hc - 2)])
                elif t > 0:
                    P.wait("tensor", "ss", ssevac[(t - 1, 30 + hc)])
                for dc in range(8):
                    P.emit("tensor", lambda e, dc=dc, hc=hc, pb=pb, xb=xb: e.matmul(
                        out=pb[:, 0:512],
                        lhsT=w1q[:, 4096 * dc + 128 * hc:4096 * dc + 128 * hc + 128],
                        rhs=xb[:, 512 * dc:512 * dc + 512], start=(dc == 0), stop=(dc == 7)))
                P.inc_last("tensor", "st")
                stw1 = P.val("st")
                P.wait("scalar", "st", stw1)
                if hc >= 3:
                    P.wait("scalar", "sv", svconv[(t, hc - 3)])
                elif t > 0:
                    P.wait("scalar", "sv", svconv[(t - 1, 29 + hc)])
                P.emit("scalar", lambda e, slot=slot, pb=pb: e.copy(
                    out=hb[slot][:, 2:514], in_=pb[:, 0:512]), inc="ss")
                ssevac[(t, hc)] = P.val("ss")
                P.wait("vector", "ss", ssevac[(t, hc)])
                P.emit("vector", lambda e, slot=slot, hc=hc: e.tensor_copy(
                    out=hb[slot][:, 0:2], in_=hsave[:, 2 * hc:2 * hc + 2]))
                P.emit("vector", lambda e, slot=slot, hc=hc: e.tensor_copy(
                    out=hsave[:, 2 * hc:2 * hc + 2], in_=hb[slot][:, 512:514]))
                P.emit("vector", lambda e, slot=slot, hc=hc: e.tensor_scalar_mul(
                    out=aB[0][:], in0=hb[slot][:, 1:513],
                    scalar1=convW[:, 3 * hc + 1:3 * hc + 2]))
                P.emit("vector", lambda e, slot=slot, hc=hc: e.scalar_tensor_tensor(
                    out=aB[0][:], in0=hb[slot][:, 0:512], scalar=convW[:, 3 * hc:3 * hc + 1],
                    in1=aB[0][:], op0=ALU.mult, op1=ALU.add))
                P.emit("vector", lambda e, slot=slot, hc=hc: e.scalar_tensor_tensor(
                    out=aB[0][:], in0=hb[slot][:, 2:514],
                    scalar=convW[:, 3 * hc + 2:3 * hc + 3],
                    in1=aB[0][:], op0=ALU.mult, op1=ALU.add), inc="sv")
                svconv[(t, hc)] = P.val("sv")
                P.wait("scalar", "sv", svconv[(t, hc)])
                P.emit("scalar", lambda e, hc=hc: e.activation(
                    out=hsq[:, 512 * hc:512 * hc + 512], in_=aB[0][:], func=AF.Silu,
                    bias=convB[:, hc:hc + 1]), inc="ss")
                sssilu = P.val("ss")
                P.wait("vector", "ss", sssilu)
                if hc == 0:
                    if t > 0:
                        P.wait("vector", "st", stmacc[t - 1])
                    P.emit("vector", lambda e, hc=hc: e.tensor_copy(
                        out=maccD[:], in_=hsq[:, 512 * hc:512 * hc + 512]))
                    P.emit("vector", lambda e, hc=hc: e.tensor_copy(
                        out=maccN[:], in_=hsq[:, 512 * hc:512 * hc + 512]), inc="sv")
                else:
                    P.emit("vector", lambda e, hc=hc: e.tensor_tensor(
                        out=maccD[:], in0=maccD[:], in1=hsq[:, 512 * hc:512 * hc + 512],
                        op=ALU.max))
                    P.emit("vector", lambda e, hc=hc: e.tensor_tensor(
                        out=maccN[:], in0=maccN[:], in1=hsq[:, 512 * hc:512 * hc + 512],
                        op=ALU.min), inc="sv")
            lastxqT[t % 2] = P.val("st")
            P.emit("vector", lambda e: e.tensor_scalar_mul(out=maccN[:], in0=maccN[:],
                                                           scalar1=-1.0))
            P.vflush(flshb)
            P.emit("vector", lambda e: e.tensor_max(out=maccD[:], in0=maccD[:],
                                                    in1=maccN[:]), inc="sv")
            svmacc = P.val("sv")

            P.wait("tensor", "sv", svmacc)
            for su in range(4):
                P.emit("tensor", lambda e, su=su: e.transpose(
                    out=pMi[:, 128 * su:128 * su + 128],
                    in_=maccD[:, 128 * su:128 * su + 128], identity=idf32[:]))
            P.inc_last("tensor", "st")
            stmacc[t] = P.val("st")
            P.wait("vector", "st", stmacc[t])
            for su in range(4):
                P.emit("vector", lambda e, su=su: e.reduce_max(
                    out=colsB[:, 4 + su:5 + su], in_=pMi[:, 128 * su:128 * su + 128],
                    axis=AX.X), inc="sv")
            P.vflush(flshb)
            P.emit("vector", lambda e: e.tensor_scalar_max(
                out=colsB[:, 4:8], in0=colsB[:, 4:8], scalar1=1e-5))
            for su in range(4):
                P.vflush(flshb)
                P.emit("vector", lambda e, su=su: e.reciprocal(
                    out=colsB[:, 12 + su:13 + su], in_=colsB[:, 4 + su:5 + su]))
                P.vflush(flshb)
            P.emit("vector", lambda e: e.tensor_scalar_mul(
                out=colsB[:, 12:16], in0=colsB[:, 12:16], scalar1=127.0), inc="sv")
            P.wait("tensor", "sv")
            for su in range(4):
                P.emit("tensor", lambda e, su=su: e.transpose(
                    out=pSc[0:1, 128 * su:128 * su + 128], in_=colsB[:, 12 + su:13 + su],
                    identity=idf32[:]))
            P.inc_last("tensor", "st")
            P.wait("vector", "st")
            P.emit("vector", lambda e: e.tensor_copy(out=rowB[0:1, :], in_=pSc[0:1, 0:512]),
                   inc="sv")
            P.wait("tensor", "sv")
            P.emit("tensor", lambda e: e.matmul(out=pSc[:, 512:1024], lhsT=ones1[0:1, :],
                                                rhs=rowB[0:1, :], start=True, stop=True),
                   inc="st")
            P.wait("vector", "st")
            P.emit("vector", lambda e: e.tensor_copy(out=s2b[:], in_=pSc[:, 512:1024]))
            P.emit("vector", lambda e: e.tensor_scalar(
                out=colsB[:, 8:12], in0=colsB[:, 4:8], scalar1=colsB[:, 2:3],
                scalar2=1.0 / 127.0, op0=ALU.mult, op1=ALU.mult), inc="sv")
            vs2b = P.val("sv")

            P.wait("vector", "sv", vs2b)
            for hc in range(32):
                hsl = hsq[:, 512 * hc:512 * hc + 512]
                P.emit("vector", lambda e, hsl=hsl: e.tensor_tensor(
                    out=tfB[0][:], in0=hsl, in1=s2b[:], op=ALU.mult))
                P.emit("vector", lambda e: e.tensor_single_scalar(
                    out=cB[0][:], in_=tfB[0][:], scalar=0.5, op=ALU.add))
                P.emit("vector", lambda e: e.tensor_copy(
                    out=tiB[0][:], in_=cB[0][:]))
                P.emit("vector", lambda e: e.tensor_tensor(
                    out=tfB[0][:], in0=tiB[0][:], in1=cB[0][:], op=ALU.is_gt))
                P.emit("vector", lambda e, hsl=hsl: e.tensor_tensor(
                    out=hsl, in0=tiB[0][:], in1=tfB[0][:], op=ALU.subtract), inc="sv")
                svround[(t, hc)] = P.val("sv")
                P.wait("tensor", "sv", svround[(t, hc)])
                for su in range(4):
                    for hf in range(2):
                        pbank, off = w2banks[2 * su + hf]
                        P.emit("tensor", lambda e, su=su, hc=hc, hf=hf, pbank=pbank,
                               off=off: e.matmul(
                            out=pbank[:, off:off + 512],
                            lhsT=hsq[:, 512 * hc + 128 * su:512 * hc + 128 * su + 128],
                            rhs=w2q[:, 1024 * hc + 512 * hf:1024 * hc + 512 * hf + 512],
                            start=(hc == 0), stop=(hc == 31)))
            P.inc_last("tensor", "st")
            stw2 = P.val("st")
            for su in range(4):
                obi = su % 2
                r0 = w0 + 128 * su
                P.wait("sync", "sv", lastopcons[obi])
                if r0 < 0:
                    P.dma("sync", lambda e, obi=obi: e.dma_start(
                        out=opB2[obi][1:128, :], in_=y_d[0:127, :]), "dO")
                else:
                    P.dma("sync", lambda e, r0=r0, obi=obi: e.dma_start(
                        out=opB2[obi][:], in_=y_d[r0:r0 + 128, :]), "dO")
                ldop = P.val("dO")
                P.wait("vector", "st", stw2)
                P.wait("vector", "dO", max(ldop, lastout[obi]))
                b0, o0 = w2banks[2 * su]
                b1, o1 = w2banks[2 * su + 1]
                P.emit("vector", lambda e, su=su, obi=obi, b0=b0, o0=o0: e.scalar_tensor_tensor(
                    out=opB2[obi][:, 0:512], in0=b0[:, o0:o0 + 512],
                    scalar=colsB[:, 8 + su:9 + su],
                    op0=ALU.mult, in1=opB2[obi][:, 0:512], op1=ALU.add))
                P.emit("vector", lambda e, su=su, obi=obi, b1=b1, o1=o1: e.scalar_tensor_tensor(
                    out=opB2[obi][:, 512:1024], in0=b1[:, o1:o1 + 512],
                    scalar=colsB[:, 8 + su:9 + su],
                    op0=ALU.mult, in1=opB2[obi][:, 512:1024], op1=ALU.add),
                    inc="sv")
                svstt[(t, su)] = P.val("sv")
                lastopcons[obi] = P.val("sv")
                vob = P.val("sv")
                P.wait("sync", "sv", vob)
                if r0 < 0:
                    P.dma("sync", lambda e, obi=obi: e.dma_start(
                        out=y_d[0:127, :], in_=opB2[obi][1:128, :]), "dO")
                else:
                    P.dma("sync", lambda e, r0=r0, obi=obi: e.dma_start(
                        out=y_d[r0:r0 + 128, :], in_=opB2[obi][:]), "dO")
                lastout[obi] = P.val("dO")

        for _pn in range(int(_osB.environ.get("PADN", "0"))):
            P.emit("tensor", lambda e: e.nop())
        import os as _os2
        do_epi = _os2.environ.get("EPI", "1") == "1" and do_b
        cw3 = convW[:].rearrange("p (c k) -> p c k", k=3)
        if do_epi:
            hs2 = hsave[:].rearrange("p (c k) -> p c k", k=2)
            P.wait("vector", "sv")
            P.emit("vector", lambda e: e.tensor_tensor(
                out=epf[:, 0:32].unsqueeze(2), in0=hs2[:, :, 0:1],
                in1=cw3[:, :, 0:1], op=ALU.mult))
            P.emit("vector", lambda e: e.tensor_tensor(
                out=epf[:, 32:64].unsqueeze(2), in0=hs2[:, :, 1:2],
                in1=cw3[:, :, 1:2], op=ALU.mult))
            P.vflush(flshb)
            P.emit("vector", lambda e: e.tensor_add(out=epf[:, 0:32], in0=epf[:, 0:32],
                                                    in1=epf[:, 32:64]))
            P.vflush(flshb)
            P.emit("vector", lambda e: e.tensor_add(out=epf[:, 0:32], in0=epf[:, 0:32],
                                                    in1=convB[:]), inc="sv")
            P.wait("scalar", "sv")
            P.emit("scalar", lambda e: e.activation(out=epf[:, 32:64], in_=epf[:, 0:32],
                                                    func=AF.Silu), inc="ss")
            P.wait("vector", "ss")
            P.emit("vector", lambda e: e.reduce_max(out=epf[:, 64:65], in_=epf[:, 32:64],
                                                    axis=AX.X, apply_absolute_value=True),
                   inc="sv")
            P.vflush(flshb)
            P.wait("tensor", "sv")
            P.emit("tensor", lambda e: e.transpose(out=pMi[0:1, 0:128], in_=epf[:, 64:65],
                                                   identity=idf32[:]), inc="st")
            P.wait("vector", "st")
            P.emit("vector", lambda e: e.reduce_max(out=scl[0:1, 4:5], in_=pMi[0:1, 0:128],
                                                    axis=AX.X), inc="sv")
            P.vflush(flshb)
            P.emit("vector", lambda e: e.tensor_scalar_max(out=scl[0:1, 4:5], in0=scl[0:1, 4:5],
                                                           scalar1=1e-5))
            P.vflush(flshb)
            P.emit("vector", lambda e: e.reciprocal(out=scl[0:1, 5:6], in_=scl[0:1, 4:5]))
            P.vflush(flshb)
            P.emit("vector", lambda e: e.tensor_single_scalar(out=scl[0:1, 5:6], in_=scl[0:1, 5:6],
                                                              scalar=127.0, op=ALU.mult))
            P.emit("vector", lambda e: e.tensor_single_scalar(out=scl[0:1, 6:7], in_=scl[0:1, 4:5],
                                                              scalar=1.0 / 127.0, op=ALU.mult))
            P.vflush(flshb)
            P.emit("vector", lambda e: e.tensor_mul(out=scl[0:1, 6:7], in0=scl[0:1, 6:7],
                                                    in1=colsB[0:1, 2:3]), inc="sv")
            P.wait("tensor", "sv")
            P.emit("tensor", lambda e: e.matmul(out=pMi[:, 2:3], lhsT=ones1[0:1, :],
                                                rhs=scl[0:1, 5:6], start=True, stop=True),
                   inc="st")
            P.wait("vector", "st")
            P.emit("vector", lambda e: e.tensor_copy(out=colsB[:, 12:13], in_=pMi[:, 2:3]))
            P.vflush(flshb)
            P.emit("vector", lambda e: e.tensor_scalar(
                out=epf[:, 64:96], in0=epf[:, 32:64], scalar1=colsB[:, 12:13], scalar2=0.5,
                op0=ALU.mult, op1=ALU.add))
            P.vflush(flshb)
            P.emit("vector", lambda e: e.tensor_copy(out=epi32[:], in_=epf[:, 64:96]))
            P.vflush(flshb)
            P.emit("vector", lambda e: e.tensor_tensor(out=epf[:, 96:128], in0=epi32[:],
                                                       in1=epf[:, 64:96], op=ALU.is_gt))
            P.vflush(flshb)
            P.emit("vector", lambda e: e.tensor_tensor(out=epb[:], in0=epi32[:],
                                                       in1=epf[:, 96:128], op=ALU.subtract),
                   inc="sv")
            vepq = P.val("sv")
            P.dma("sync", lambda e: e.dma_start(out=yrow[0:1, :], in_=y_d[4095:4096, :]), "dO")
            ldyr = P.val("dO")
            P.wait("tensor", "sv", vepq)
            for oc in range(8):
                for hc in range(32):
                    P.emit("tensor", lambda e, hc=hc, oc=oc: e.matmul(
                        out=pAcc[:, oc:oc + 1],
                        lhsT=w2q[:, 1024 * hc + 128 * oc:1024 * hc + 128 * oc + 128],
                        rhs=epb[:, hc:hc + 1], start=(hc == 0), stop=(hc == 31)))
            P.inc_last("tensor", "st")
            P.wait("vector", "st")
            P.emit("vector", lambda e: e.tensor_copy(out=epf[:, 0:8], in_=pAcc[:, 0:8]),
                   inc="sv")
            P.wait("tensor", "sv")
            P.emit("tensor", lambda e: e.transpose(out=pMi[0:8, 0:128], in_=epf[:, 0:8],
                                                   identity=idf32[:]), inc="st")
            P.wait("vector", "st")
            P.emit("vector", lambda e: e.tensor_copy(out=epf[0:8, 0:128],
                                                     in_=pMi[0:8, 0:128]), inc="sv")
            P.wait("scalar", "sv")
            P.dma("scalar", lambda e: e.dma_start(out=opB2[1][0:1, 0:1024],
                                                  in_=epf[0:8, 0:128]), "dF")
            ldfl = P.val("dF")
            P.wait("vector", "dF", ldfl)
            P.wait("vector", "dO", ldyr)
            P.emit("vector", lambda e: e.scalar_tensor_tensor(
                out=yrow[0:1, :], in0=opB2[1][0:1, 0:1024], scalar=scl[0:1, 6:7],
                op0=ALU.mult, in1=yrow[0:1, :], op1=ALU.add), inc="sv")
            P.wait("sync", "sv")
            P.dma("sync", lambda e: e.dma_start(out=y_d[4095:4096, :], in_=yrow[0:1, :]), "dO")
        P.wait("sync", "dO")
        P.wait("sync", "dX")
        P.wait("sync", "dW")
        P.wait("sync", "dQ")
        P.wait("sync", "dF")

        with nc.Block() as block:
            @block.sync
            def _(e):
                for fn in P.ev["sync"]:
                    fn(e)

            @block.vector
            def _(e):
                for fn in P.ev["vector"]:
                    fn(e)

            @block.scalar
            def _(e):
                for fn in P.ev["scalar"]:
                    fn(e)

            @block.tensor
            def _(e):
                for fn in P.ev["tensor"]:
                    fn(e)

            @block.gpsimd
            def _(e):
                for fn in P.ev["gpsimd"]:
                    fn(e)

    return nc


def _host_inputs(inputs):
    f32 = np.float32
    x = np.ascontiguousarray(inputs["L0_attn_x"], dtype=f32)
    perm = np.concatenate([np.arange(64 * b, 64 * b + 64) for b in BLK])
    base = {
        "wqT": np.ascontiguousarray(inputs["L0_attn_Wq"].T, dtype=f32),
        "wkT": np.ascontiguousarray(inputs["L0_attn_Wk"].T, dtype=f32),
        "wvT": np.ascontiguousarray(inputs["L0_attn_Wv"].T[:, perm], dtype=f32),
        "woT": np.ascontiguousarray(inputs["L0_attn_Wo"].T[perm, :], dtype=f32),
        "wgT": np.ascontiguousarray(
            np.concatenate([inputs["L0_attn_Ww"].T, inputs["L0_attn_We"].T], axis=1), dtype=f32),
        "bg": np.ascontiguousarray(
            np.stack([inputs["L0_attn_bw"], inputs["L0_attn_be"]], axis=1), dtype=f32),
        "w1T": np.ascontiguousarray(inputs["L0_ffn_w1"].T, dtype=f32),
        "w2T": np.ascontiguousarray(inputs["L0_ffn_w2"].T, dtype=f32),
        "cw": np.ascontiguousarray(
            inputs["L0_ffn_conv_w"].reshape(32, 128, 3).transpose(1, 0, 2).reshape(128, 96),
            dtype=f32),
        "cb": np.ascontiguousarray(
            inputs["L0_ffn_conv_b"].reshape(32, 128).T, dtype=f32),
        "idbf": np.eye(128, dtype=ml_dtypes.bfloat16),
        "idf32": np.eye(128, dtype=f32),
        "mask4": np.ascontiguousarray(
            np.tile(np.tril(np.full((128, 128), -1.0e30, dtype=f32), -1), (1, 4))
            .astype(ml_dtypes.bfloat16)),
    }
    return x, base


def kernel(**inputs):
    x, base = _host_inputs(inputs)
    nc = build_nc(debug=False)
    in_maps = [dict(base, x=np.ascontiguousarray(x[c])) for c in range(8)]
    res = run_bass_kernel_spmd(nc, in_maps, core_ids=list(range(8)))
    out = np.stack([np.asarray(r["y"]) for r in res.results], axis=0)
    return out.astype(np.float32)


# version 13 (final)
# speedup vs baseline: 1.0622x; 1.0201x over previous
import numpy as np
import ml_dtypes
import concourse.bass as bass
import concourse.mybir as mybir
from concourse.bass_utils import run_bass_kernel_spmd
from contextlib import ExitStack

F32 = mybir.dt.float32
BF16 = mybir.dt.bfloat16
I32 = mybir.dt.int32
AF = mybir.ActivationFunctionType
ALU = mybir.AluOpType
AX = mybir.AxisListType

L, DM, H, DH, DHID = 4096, 1024, 16, 64, 4096
TILE, NT, C, NCH = 512, 8, 128, 4
BLK = [8 * w + 2 * (j % 4) + j // 4 for w in range(2) for j in range(8)]

ENGS = ("sync", "vector", "scalar", "tensor", "gpsimd")


class Plan:
    def __init__(self):
        self.ev = {e: [] for e in ENGS}
        self.cnt = {}
        self.semh = {}

    def sem(self, name, stack, nc):
        self.semh[name] = stack.enter_context(nc.semaphore(name))
        self.cnt[name] = 0

    def emit(self, eng, fn, inc=None, k=1):
        if inc is not None:
            self.cnt[inc] += k
            sem = self.semh[inc]
            self.ev[eng].append(lambda e, fn=fn, sem=sem, k=k: fn(e).then_inc(sem, k))
        else:
            self.ev[eng].append(fn)

    def dma(self, eng, fn, inc):
        self.emit(eng, fn, inc, 16)

    def vflush(self, fb, inc=None):
        self.emit("vector", lambda e, fb=fb: e.tensor_copy(out=fb[:, 1:2],
                                                           in_=fb[:, 0:1]), inc=inc)

    def sflush(self, fb, inc=None):
        self.emit("scalar", lambda e, fb=fb: e.copy(out=fb[:, 1:2],
                                                    in_=fb[:, 0:1]), inc=inc)

    def inc_last(self, eng, sname, k=1):
        self.cnt[sname] += k
        sem = self.semh[sname]
        prev = self.ev[eng].pop()
        self.ev[eng].append(lambda e, prev=prev, sem=sem, k=k: prev(e).then_inc(sem, k))

    def wait(self, eng, sname, val=None):
        v = self.cnt[sname] if val is None else val
        if v <= 0:
            return
        sem = self.semh[sname]
        self.ev[eng].append(lambda e, sem=sem, v=v: e.wait_ge(sem, v))

    def val(self, sname):
        return self.cnt[sname]


def _slotgrid(ap2d, w, rep, parts=128):
    v = ap2d[0:parts, 8 * w:8 * w + 8]
    v = v.rearrange("p (lo hi) -> p hi lo", hi=2)
    return v.unsqueeze(3).broadcast_to([parts, 2, 4, rep])


def build_nc(debug=False, nt=NT, do_b=True, stage_b=5):
    nc = bass.Bass("TRN2", target_bir_lowering=False, detect_race_conditions=False)
    x_d = nc.dram_tensor("x", [L, DM], F32, kind="ExternalInput").ap()
    y_d = nc.dram_tensor("y", [L, DM], F32, kind="ExternalOutput").ap()
    wqT_d = nc.dram_tensor("wqT", [DM, DM], F32, kind="ExternalInput").ap()
    wkT_d = nc.dram_tensor("wkT", [DM, DM], F32, kind="ExternalInput").ap()
    wvT_d = nc.dram_tensor("wvT", [DM, DM], F32, kind="ExternalInput").ap()
    woT_d = nc.dram_tensor("woT", [DM, DM], F32, kind="ExternalInput").ap()
    wgT_d = nc.dram_tensor("wgT", [DM, 32], F32, kind="ExternalInput").ap()
    bg_d = nc.dram_tensor("bg", [16, 2], F32, kind="ExternalInput").ap()
    w1T_d = nc.dram_tensor("w1T", [DM, DHID], F32, kind="ExternalInput").ap()
    w2T_d = nc.dram_tensor("w2T", [DHID, DM], F32, kind="ExternalInput").ap()
    cw_d = nc.dram_tensor("cw", [128, 96], F32, kind="ExternalInput").ap()
    cb_d = nc.dram_tensor("cb", [128, 32], F32, kind="ExternalInput").ap()
    idbf_d = nc.dram_tensor("idbf", [128, 128], BF16, kind="ExternalInput").ap()
    idf32_d = nc.dram_tensor("idf32", [128, 128], F32, kind="ExternalInput").ap()
    mask4_d = nc.dram_tensor("mask4", [128, 512], BF16, kind="ExternalInput").ap()
    xqs_d = nc.dram_tensor("xqs", [NT, 128, 4096], BF16, kind="Internal").ap()

    dbg = {}
    if debug:
        for nm, shp, dt in [("dbg_xnfm", [128, 4096], BF16), ("dbg_q", [128, 4096], BF16),
                            ("dbg_k", [128, 4096], BF16), ("dbg_v", [128, 1040], BF16),
                            ("dbg_gD", [16, 512], F32), ("dbg_gA", [16, 512], F32),
                            ("dbg_R", [128, 1024], BF16), ("dbg_S", [64, 2048], F32),
                            ("dbg_eff", [128, 1024], BF16), ("dbg_xq", [128, 4096], BF16),
                            ("dbg_w1q", [128, 4096], BF16), ("dbg_hq", [128, 512], BF16),
                            ("dbg_hs", [128, 514], F32)]:
            dbg[nm] = nc.dram_tensor(nm, shp, dt, kind="ExternalOutput").ap()

    P = Plan()
    root = ExitStack()
    with root:
        def psum(name, shape, dt=F32):
            return nc.alloc_psum_tensor(name, shape, dt)

        for s in ("dX", "dW", "dG", "dO", "dQ", "dF", "sv", "ss", "st", "sg", "sy"):
            P.sem(s, root, nc)

        pAcc = psum("pAcc", [128, 1024])
        pSc = psum("pSc", [128, 1024])
        pDk = psum("pDk", [128, 1024])
        pTrF = nc.alloc_psum_tensor("pTr", [128, 512], F32)
        pTr = pTrF[:].bitcast(BF16)
        pMi = psum("pMi", [128, 512])
        cpool = nc.alloc_sbuf_tensor("cpool", [128, 4], F32)
        flshb = nc.alloc_sbuf_tensor("flshb", [128, 2], F32)
        for _i, _v in enumerate((1e-5, 1e-8, 0.5, 0.0)):
            nc.const_aps.register(F32, _v, cpool[:, _i:_i + 1])
        idbf = nc.alloc_sbuf_tensor("idbf_s", [128, 128], BF16)
        idf32 = nc.alloc_sbuf_tensor("idf32_s", [128, 128], F32)
        ones1 = nc.alloc_sbuf_tensor("ones1", [1, 128], F32)
        P.dma("sync", lambda e: e.dma_start(out=idbf[:], in_=idbf_d), "dW")
        P.dma("sync", lambda e: e.dma_start(out=idf32[:], in_=idf32_d), "dW")

        phase_base = nc.sbuf_base

        def sbA(name, shape, dt=F32):
            return nc.alloc_sbuf_tensor(name, shape, dt)

        wq_bf = sbA("wq_bf", [128, 8192], BF16)
        wk_bf = sbA("wk_bf", [128, 8192], BF16)
        wv_bf = sbA("wv_bf", [128, 8192], BF16)
        wo_bf = sbA("wo_bf", [128, 8192], BF16)
        wg_bf = sbA("wg_bf", [128, 256], BF16)
        x_sb = sbA("x_sb", [128, 4096])
        xn_fm = sbA("xn_fm", [128, 4096], BF16)
        xq_fm = sbA("xq_fm", [128, 4096], BF16)
        qfm = sbA("qfm", [128, 4096], BF16)
        kfm = sbA("kfm", [128, 4096], BF16)
        vtm = [sbA(f"vtm{i}", [128, 1040], BF16) for i in range(4)]
        ktm = sbA("ktm", [128, 1024], BF16)
        khat = sbA("khat", [128, 1024], BF16)
        rtm = sbA("rtm", [128, 1024], BF16)
        rfm = sbA("rfm", [128, 1024], BF16)
        xnb = sbA("xnb", [128, 1024], BF16)
        qa = sbA("qa", [128, 512])
        qti = sbA("qti", [128, 512], I32)
        xdq = sbA("xdq", [128, 1024], BF16)
        ebuf = sbA("ebuf", [128, 512])
        stat6 = sbA("stat6", [128, 12])
        mv = sbA("mv", [128, 2])
        colsA = sbA("colsA", [128, 32])
        g_lg = sbA("g_lg", [16, 512])
        g_D = sbA("g_D", [16, 512])
        g_A = sbA("g_A", [16, 512])
        g_t1 = sbA("g_t1", [16, 512])
        g_t2 = sbA("g_t2", [16, 512])
        g_lam = sbA("g_lam", [16, 128])
        d3a = sbA("d3a", [16, 512], BF16)
        d3b = sbA("d3b", [16, 512], BF16)
        d3c = sbA("d3c", [16, 512], BF16)
        d3e = sbA("d3e", [16, 512])
        dfl = sbA("dfl", [3, 2048], BF16)
        atok = sbA("atok", [128, 16])
        editok = sbA("editok", [128, 16])
        wdtok = sbA("wdtok", [128, 16])
        lamsb = sbA("lamsb", [128, 16])
        gegs = sbA("gegs", [16, 128])
        gwgs = sbA("gwgs", [16, 128])
        ptb = sbA("ptb", [128, 1024])
        efb = [sbA(f"efb{i}", [128, 1024], BF16) for i in range(2)]
        tcb = sbA("tcb", [128, 520])
        acb = sbA("acb", [128, 520])
        denb = sbA("denb", [128, 8])
        S_f = sbA("S_f", [64, 2048])
        S_bf = sbA("S_bf", [128, 2048], BF16)
        out_sb = [sbA(f"outsb{i}", [128, 1024]) for i in range(2)]
        mask4 = sbA("mask4_s", [128, 512], BF16)
        ones3 = sbA("ones3", [3, 128], BF16)
        bgc = sbA("bgc_s", [16, 2])
        nbw = sbA("nbw", [16, 1])

        P.dma("sync", lambda e: e.dma_start(out=mask4[:], in_=mask4_d), "dW")
        P.dma("sync", lambda e: e.dma_start(out=bgc[:], in_=bg_d), "dW")
        for dc in range(8):
            for dst, src in ((wq_bf, wqT_d), (wk_bf, wkT_d), (wv_bf, wvT_d), (wo_bf, woT_d)):
                P.dma("gpsimd", lambda e, dst=dst, src=src, dc=dc: e.dma_start(
                    out=dst[:, 1024 * dc:1024 * dc + 1024],
                    in_=src[128 * dc:128 * dc + 128, :]), "dG")
            P.dma("gpsimd", lambda e, dc=dc: e.dma_start(
                out=wg_bf[:, 32 * dc:32 * dc + 32],
                in_=wgT_d[128 * dc:128 * dc + 128, :]), "dG")
        wA = P.val("dW")
        P.wait("vector", "dW", wA)
        P.wait("vector", "dG")
        for _i, _v in enumerate((1e-5, 1e-8, 0.5, 0.0)):
            P.emit("vector", lambda e, _i=_i, _v=_v: e.memset(cpool[:, _i:_i + 1], _v))
        P.emit("vector", lambda e: e.memset(flshb[:], 0.0))
        P.emit("vector", lambda e: e.memset(ones3[:], 1.0))
        P.emit("vector", lambda e: e.memset(ones1[:], 1.0))
        P.emit("vector", lambda e: e.memset(S_f[:], 0.0))
        P.emit("vector", lambda e: e.memset(S_bf[:], 0.0))
        for i in range(4):
            P.emit("vector", lambda e, i=i: e.memset(
                vtm[i][:].rearrange("p (h o) -> p h o", o=65)[:, :, 64:65], 1.0))
        P.emit("vector", lambda e: e.tensor_scalar_mul(out=nbw[:], in0=bgc[:, 0:1], scalar1=-1.0),
               inc="sv")
        initV = P.val("sv")
        laststore = {0: 0, 1: 0}
        vPrevTileDone = 0
        tPrevVcons = 0
        ssrfmLast = 0
        dqStoreLast = 0
        vsupPrev = 0
        vcomb1Prev = 0
        stdflFree = 0
        ssxqPrev = 0

        for t in range(nt):
            P.wait("sync", "sv", vPrevTileDone)
            P.dma("sync", lambda e, t=t: e.dma_start(
                out=x_sb[:].rearrange("p (c d) -> p c d", d=DM),
                in_=x_d[TILE * t:TILE * (t + 1), :].rearrange("(c p) d -> p c d", p=128)), "dX")
            xw = P.val("dX")

            P.wait("vector", "dX", xw)
            P.wait("vector", "st", tPrevVcons)
            P.wait("scalar", "dG", dqStoreLast)
            for ch in range(4):
                xc = x_sb[:, 1024 * ch:1024 * ch + 1024]
                cc = colsA[:, 8 * ch:8 * ch + 8]
                P.emit("vector", lambda e, xc=xc: e.tensor_reduce(
                    out=stat6[:, 0:1], in_=xc[:], axis=AX.X, op=ALU.add), inc="sv")
                P.vflush(flshb)
                P.emit("vector", lambda e, cc=cc: e.tensor_single_scalar(
                    out=cc[:, 0:1], in_=stat6[:, 0:1], scalar=1.0 / 1024.0, op=ALU.mult),
                    inc="sv")
                P.wait("scalar", "sv")
                P.emit("scalar", lambda e, xc=xc: e.activation(
                    out=pMi[:, 0:512], in_=xc[:, 0:512], func=AF.Square,
                    accum_out=stat6[:, 1:2]), inc="ss")
                P.wait("scalar", "ss")
                P.emit("scalar", lambda e, xc=xc: e.activation(
                    out=pMi[:, 0:512], in_=xc[:, 512:1024], func=AF.Square,
                    accum_out=stat6[:, 2:3]))
                P.sflush(flshb, inc="ss")
                P.wait("vector", "ss")
                P.emit("vector", lambda e: e.tensor_add(out=stat6[:, 1:2], in0=stat6[:, 1:2],
                                                        in1=stat6[:, 2:3]))
                P.emit("vector", lambda e, cc=cc: e.tensor_mul(out=stat6[:, 3:4],
                                                               in0=cc[:, 0:1], in1=cc[:, 0:1]))
                P.vflush(flshb)
                P.emit("vector", lambda e: e.scalar_tensor_tensor(
                    out=mv[:, 1:2], in0=stat6[:, 1:2], scalar=1.0 / 1024.0,
                    in1=stat6[:, 3:4], op0=ALU.mult, op1=ALU.subtract),
                       inc="sv")
                P.wait("scalar", "sv")
                P.emit("scalar", lambda e: e.activation(out=mv[:, 1:2], in_=mv[:, 1:2],
                                                        func=AF.Ln, bias=1e-5))
                P.emit("scalar", lambda e, cc=cc: e.activation(out=cc[:, 1:2], in_=mv[:, 1:2],
                                                               func=AF.Exp, scale=-0.5),
                       inc="ss")
                P.emit("vector", lambda e, xc=xc, cc=cc: e.tensor_scalar(
                    out=xc[:], in0=xc[:], scalar1=cc[:, 0:1], scalar2=0.0,
                    op0=ALU.subtract, op1=ALU.max, accum_out=cc[:, 2:3]), inc="sv")
                P.emit("vector", lambda e, xc=xc, cc=cc: e.tensor_reduce(
                    out=cc[:, 7:8], in_=xc[:], axis=AX.X, op=ALU.min), inc="sv")
                P.vflush(flshb)
                P.emit("vector", lambda e, cc=cc: e.tensor_scalar_mul(
                    out=cc[:, 7:8], in0=cc[:, 7:8], scalar1=-1.0))
                P.vflush(flshb)
                P.emit("vector", lambda e, cc=cc: e.tensor_max(
                    out=cc[:, 2:3], in0=cc[:, 2:3], in1=cc[:, 7:8]))
                P.wait("vector", "ss")
                P.emit("vector", lambda e, cc=cc: e.tensor_scalar_mul(
                    out=cc[:, 6:7], in0=cc[:, 1:2], scalar1=127.0))
                P.emit("vector", lambda e, cc=cc: e.tensor_mul(out=cc[:, 3:4], in0=cc[:, 2:3],
                                                               in1=cc[:, 1:2]))
                P.vflush(flshb)
                P.emit("vector", lambda e, cc=cc: e.tensor_scalar_max(
                    out=cc[:, 3:4], in0=cc[:, 3:4], scalar1=1e-5))
                P.vflush(flshb)
                P.emit("vector", lambda e, cc=cc: e.tensor_single_scalar(
                    out=cc[:, 4:5], in_=cc[:, 3:4], scalar=1.0 / 127.0, op=ALU.mult))
                P.emit("vector", lambda e, cc=cc: e.reciprocal(out=cc[:, 7:8], in_=cc[:, 3:4]))
                P.vflush(flshb)
                P.emit("vector", lambda e, cc=cc: e.tensor_mul(
                    out=cc[:, 5:6], in0=cc[:, 6:7], in1=cc[:, 7:8]))
                P.emit("vector", lambda e, xc=xc, cc=cc: e.tensor_scalar_mul(
                    out=xnb[:], in0=xc[:], scalar1=cc[:, 1:2]), inc="sv")
                svxnb = P.val("sv")
                for qh in range(2):
                    qsl = slice(512 * qh, 512 * qh + 512)
                    P.emit("vector", lambda e, xc=xc, cc=cc, qsl=qsl: e.tensor_scalar(
                        out=qa[:], in0=xc[:, qsl], scalar1=cc[:, 5:6], scalar2=0.5,
                        op0=ALU.mult, op1=ALU.add))
                    P.emit("vector", lambda e: e.tensor_copy(out=qti[:], in_=qa[:]))
                    P.emit("vector", lambda e: e.tensor_tensor(out=ebuf[:], in0=qti[:],
                                                               in1=qa[:], op=ALU.is_gt))
                    P.emit("vector", lambda e: e.tensor_tensor(out=qa[:], in0=qti[:],
                                                               in1=ebuf[:], op=ALU.subtract))
                    P.emit("vector", lambda e, cc=cc, qsl=qsl: e.tensor_scalar_mul(
                        out=xdq[:, qsl], in0=qa[:], scalar1=cc[:, 4:5]), inc="sv")
                svxdq = P.val("sv")
                P.wait("tensor", "sv", svxnb)
                P.wait("tensor", "ss", ssxqPrev)
                if t > 0 and ch == 0:
                    P.wait("tensor", "ss", ssrfmLast)
                for dc in range(8):
                    P.emit("tensor", lambda e, dc=dc: e.transpose(
                        out=pTr[:, 128 * dc:128 * dc + 128],
                        in_=xnb[:, 128 * dc:128 * dc + 128], identity=idbf[:]))
                P.inc_last("tensor", "st")
                stxnb = P.val("st")
                P.wait("scalar", "st", stxnb)
                P.emit("scalar", lambda e, ch=ch: e.copy(
                    out=xn_fm[:].rearrange("p (d c) -> p d c", c=512)[:, :, 128 * ch:128 * ch + 128],
                    in_=pTr[:].rearrange("p (d c) -> p d c", c=128)), inc="ss")
                P.wait("tensor", "ss", P.val("ss"))
                P.wait("tensor", "sv", svxdq)
                for dc in range(8):
                    P.emit("tensor", lambda e, dc=dc: e.transpose(
                        out=pTr[:, 128 * dc:128 * dc + 128],
                        in_=xdq[:, 128 * dc:128 * dc + 128], identity=idbf[:]))
                P.inc_last("tensor", "st")
                stxdq = P.val("st")
                P.wait("scalar", "st", stxdq)
                P.emit("scalar", lambda e, ch=ch: e.copy(
                    out=xq_fm[:].rearrange("p (d c) -> p d c", c=512)[:, :, 128 * ch:128 * ch + 128],
                    in_=pTr[:].rearrange("p (d c) -> p d c", c=128)), inc="ss")
                ssxqPrev = P.val("ss")
            vxn_ss = P.val("ss")
            P.wait("gpsimd", "ss", vxn_ss)
            P.dma("gpsimd", lambda e, t=t: e.dma_start(out=xqs_d[t], in_=xq_fm[:]), "dG")
            dqStoreLast = P.val("dG")
            if debug and t == 0:
                P.dma("sync", lambda e: e.dma_start(out=dbg["dbg_xnfm"], in_=xn_fm[:]), "dF")
                P.dma("sync", lambda e: e.dma_start(out=dbg["dbg_xq"], in_=xq_fm[:]), "dF")

            P.wait("tensor", "ss", vxn_ss)
            for dc in range(8):
                P.emit("tensor", lambda e, dc=dc: e.matmul(
                    out=pMi[0:16, 0:512], lhsT=wg_bf[:, 32 * dc:32 * dc + 16],
                    rhs=xn_fm[:, 512 * dc:512 * dc + 512], start=(dc == 0), stop=(dc == 7)))
            for dc in range(8):
                P.emit("tensor", lambda e, dc=dc: e.matmul(
                    out=pAcc[0:16, 0:512], lhsT=wg_bf[:, 32 * dc + 16:32 * dc + 32],
                    rhs=xn_fm[:, 512 * dc:512 * dc + 512], start=(dc == 0), stop=(dc == 7)))
            P.inc_last("tensor", "st")
            P.wait("scalar", "st")
            P.wait("scalar", "sv", initV)
            P.emit("scalar", lambda e: e.activation(out=g_t1[:], in_=pMi[0:16, 0:512],
                                                    func=AF.Exp, scale=-1.0, bias=nbw[:, 0:1]))
            P.emit("scalar", lambda e: e.activation(out=g_t2[:], in_=pAcc[0:16, 0:512],
                                                    func=AF.Exp, scale=1.0, bias=bgc[:, 1:2]),
                   inc="ss")
            gatePsumFree = P.val("ss")
            P.wait("vector", "ss")
            P.emit("vector", lambda e: e.tensor_scalar_add(out=g_t1[:], in0=g_t1[:], scalar1=1.0))
            P.emit("vector", lambda e: e.tensor_scalar_add(out=g_t2[:], in0=g_t2[:], scalar1=1.0))
            P.vflush(flshb)
            P.emit("vector", lambda e: e.reciprocal(out=g_t2[:], in_=g_t2[:]))
            P.vflush(flshb, inc="sv")
            P.wait("scalar", "sv")
            P.emit("scalar", lambda e: e.activation(out=g_t1[:], in_=g_t1[:], func=AF.Ln))
            P.emit("scalar", lambda e: e.activation(out=g_lg[:], in_=g_t2[:], func=AF.Ln,
                                                    scale=0.95, bias=1e-8), inc="ss")
            P.wait("vector", "ss")
            for ch in range(4):
                P.emit("vector", lambda e, ch=ch: e.tensor_tensor_scan(
                    out=g_D[:, 128 * ch:128 * ch + 128], data0=g_lg[:, 128 * ch:128 * ch + 128],
                    data1=g_lg[:, 128 * ch:128 * ch + 128], initial=0.0,
                    op0=ALU.add, op1=ALU.bypass), inc="sv")
            P.vflush(flshb)
            P.emit("vector", lambda e: e.scalar_tensor_tensor(
                out=g_A[:], in0=g_t1[:], scalar=-1.0, in1=g_D[:],
                op0=ALU.mult, op1=ALU.subtract))
            P.emit("vector", lambda e: e.tensor_copy(out=d3a[:], in_=g_D[:]))
            P.emit("vector", lambda e: e.tensor_tensor(out=d3e[:], in0=g_D[:], in1=d3a[:],
                                                       op=ALU.subtract))
            P.emit("vector", lambda e: e.tensor_copy(out=d3b[:], in_=d3e[:]))
            P.emit("vector", lambda e: e.tensor_tensor(out=d3e[:], in0=d3e[:], in1=d3b[:],
                                                       op=ALU.subtract))
            P.emit("vector", lambda e: e.tensor_copy(out=d3c[:], in_=d3e[:]), inc="sv")
            vgate = P.val("sv")
            if debug and t == 0:
                P.wait("sync", "sv", vgate)
                P.dma("sync", lambda e: e.dma_start(out=dbg["dbg_gD"], in_=g_D[:]), "dF")
                P.dma("sync", lambda e: e.dma_start(out=dbg["dbg_gA"], in_=g_A[:]), "dF")

            P.wait("tensor", "ss", gatePsumFree)
            banklast = {id(pMi): 0, id(pSc): 0}
            qkit = 0
            for wmat, dst in ((wq_bf, qfm), (wk_bf, kfm)):
                for oc in range(8):
                    pb = pMi if (qkit % 2 == 0) else pSc
                    eb = ebuf if (qkit % 2 == 0) else qa
                    P.wait("tensor", "sv", banklast[id(pb)])
                    for dc in range(8):
                        P.emit("tensor", lambda e, oc=oc, dc=dc, wmat=wmat, pb=pb: e.matmul(
                            out=pb[:, 0:512],
                            lhsT=wmat[:, 1024 * dc + 128 * oc:1024 * dc + 128 * oc + 128],
                            rhs=xn_fm[:, 512 * dc:512 * dc + 512],
                            start=(dc == 0), stop=(dc == 7)))
                    P.inc_last("tensor", "st")
                    stqk = P.val("st")
                    P.wait("scalar", "st", stqk)
                    P.wait("scalar", "sv", banklast[id(pb)])
                    P.emit("scalar", lambda e, pb=pb, eb=eb: e.activation(
                        out=eb[:], in_=pb[:, 0:512], func=AF.Exp), inc="ss")
                    ssqk = P.val("ss")
                    P.wait("vector", "ss", ssqk)
                    P.emit("vector", lambda e, eb=eb: e.tensor_scalar_min(
                        out=eb[:], in0=eb[:], scalar1=1.0))
                    P.emit("vector", lambda e, oc=oc, dst=dst, pb=pb, eb=eb:
                           e.scalar_tensor_tensor(
                        out=dst[:, 512 * oc:512 * oc + 512], in0=pb[:, 0:512], scalar=0.0,
                        in1=eb[:], op0=ALU.max, op1=ALU.add), inc="sv")
                    banklast[id(pb)] = P.val("sv")
                    qkit += 1
            if debug and t == 0:
                P.wait("sync", "sv")
                P.dma("sync", lambda e: e.dma_start(out=dbg["dbg_q"], in_=qfm[:]), "dF")
                P.dma("sync", lambda e: e.dma_start(out=dbg["dbg_k"], in_=kfm[:]), "dF")

            vcop = {}
            for ch in range(4):
                if ch > 0:
                    P.wait("tensor", "ss", vcop[ch - 1])
                else:
                    P.wait("tensor", "sv", vPrevTileDone)
                for dc in range(8):
                    for hf in range(2):
                        P.emit("tensor", lambda e, dc=dc, ch=ch, hf=hf: e.matmul(
                            out=pAcc[:, 512 * hf:512 * hf + 512],
                            lhsT=xn_fm[:, 512 * dc + 128 * ch:512 * dc + 128 * ch + 128],
                            rhs=wv_bf[:, 1024 * dc + 512 * hf:1024 * dc + 512 * hf + 512],
                            start=(dc == 0), stop=(dc == 7)))
                P.inc_last("tensor", "st")
                stv = P.val("st")
                P.wait("scalar", "st", stv)
                P.emit("scalar", lambda e, ch=ch: e.copy(
                    out=vtm[ch][:].rearrange("p (h o) -> p h o", o=65)[:, :, 0:64],
                    in_=pAcc[:].rearrange("p (h o) -> p h o", o=64)), inc="ss")
                vcop[ch] = P.val("ss")
            if debug and t == 0:
                P.wait("sync", "sv")
                P.dma("sync", lambda e: e.dma_start(out=dbg["dbg_v"], in_=vtm[0][:]), "dF")

            for ch in range(4):
                gsl = slice(128 * ch, 128 * ch + 128)
                dend = g_D[:, 128 * ch + 127:128 * ch + 128]
                P.wait("scalar", "sv", vgate)
                P.wait("scalar", "st", stdflFree)
                for i3, s3 in enumerate((d3a, d3b, d3c)):
                    P.dma("scalar", lambda e, i3=i3, s3=s3, gsl=gsl: e.dma_start(
                        out=dfl[i3:i3 + 1, :], in_=s3[:, gsl]), "dF")
                fD = P.val("dF")
                P.emit("scalar", lambda e, gsl=gsl: e.activation(
                    out=gegs[:], in_=g_D[:, gsl], func=AF.Exp))
                P.emit("scalar", lambda e, gsl=gsl, dend=dend: e.activation(
                    out=gwgs[:], in_=g_A[:, gsl], func=AF.Exp, bias=dend), inc="ss")
                sgt = P.val("ss")
                P.wait("vector", "ss", sgt)
                P.emit("vector", lambda e: e.tensor_copy(
                    out=g_lam[:], in_=gegs[:, 127:128].broadcast_to([16, 128])), inc="sv")
                vlam = P.val("sv")
                P.wait("tensor", "sv", vlam)
                P.emit("tensor", lambda e, gsl=gsl: e.transpose(
                    out=pMi[:, 0:16], in_=g_A[:, gsl], identity=idf32[0:16, 0:16]))
                P.emit("tensor", lambda e: e.transpose(out=pMi[:, 16:32], in_=gegs[:],
                                                       identity=idf32[0:16, 0:16]))
                P.emit("tensor", lambda e: e.transpose(out=pMi[:, 32:48], in_=gwgs[:],
                                                       identity=idf32[0:16, 0:16]))
                P.emit("tensor", lambda e: e.transpose(out=pMi[:, 48:64], in_=g_lam[:],
                                                       identity=idf32[0:16, 0:16]), inc="st")
                P.wait("vector", "st")
                P.emit("vector", lambda e: e.tensor_copy(out=atok[:], in_=pMi[:, 0:16]))
                P.emit("vector", lambda e: e.tensor_copy(out=editok[:], in_=pMi[:, 16:32]))
                P.emit("vector", lambda e: e.tensor_copy(out=wdtok[:], in_=pMi[:, 32:48]))
                P.emit("vector", lambda e: e.tensor_copy(out=lamsb[:], in_=pMi[:, 48:64]),
                       inc="sv")
                vtok = P.val("sv")
                for c2 in range(8):
                    P.emit("tensor", lambda e, c2=c2, ch=ch: e.transpose(
                        out=pTr[:, 128 * c2:128 * c2 + 128],
                        in_=kfm[:, 512 * c2 + 128 * ch:512 * c2 + 128 * ch + 128],
                        identity=idbf[:]))
                P.inc_last("tensor", "st")
                stktm = P.val("st")
                P.wait("scalar", "st", stktm)
                P.emit("scalar", lambda e: e.copy(out=ktm[:], in_=pTr[:]), inc="ss")
                ssktm = P.val("ss")
                P.wait("vector", "ss", ssktm)
                P.emit("vector", lambda e: e.tensor_tensor(
                    out=khat[:].rearrange("p (h o) -> p h o", o=64),
                    in0=ktm[:].rearrange("p (h o) -> p h o", o=64),
                    in1=wdtok[:].unsqueeze(2).broadcast_to([128, 16, 64]),
                    op=ALU.mult), inc="sv")
                vkhat = P.val("sv")

                def sc_blk(w, jj, width):
                    if w == 0:
                        return pSc[:, 128 * jj:128 * jj + width]
                    pb_ = pMi if jj < 4 else pTrF
                    return pb_[:, 128 * (jj % 4):128 * (jj % 4) + width]

                veff_w = {}
                vcomb_w = {}
                for w in range(2):
                    dkb = pDk if w == 0 else pAcc
                    if w == 0:
                        P.wait("tensor", "sv", vsupPrev)
                    else:
                        P.wait("tensor", "sv", max(vtok, vcomb1Prev))
                        P.wait("tensor", "ss", ssktm)
                    P.wait("tensor", "dF", fD)
                    for cp in range(4):
                        c = 4 * w + cp
                        for half in range(2):
                            b0, j = 64 * half, cp + 4 * half
                            P.emit("tensor", lambda e, c=c, b0=b0, j=j, ch=ch, w=w: e.matmul(
                                out=sc_blk(w, j, 128),
                                lhsT=kfm[b0:b0 + 64, 512 * c + 128 * ch:512 * c + 128 * ch + 128],
                                rhs=qfm[b0:b0 + 64, 512 * c + 128 * ch:512 * c + 128 * ch + 128],
                                start=True, stop=True))
                    dflv = dfl[:, :].rearrange("k (hw hm par i) -> k hw par hm i",
                                               hw=2, hm=4, par=2, i=128)
                    for half in range(2):
                        P.emit("tensor", lambda e, half=half, dkb=dkb: e.matmul(
                            out=dkb[:, 512 * half:512 * half + 512], lhsT=idbf[:],
                            rhs=mask4[:], start=True, stop=False))
                        P.emit("tensor", lambda e, w=w, half=half, dflv=dflv, dkb=dkb: e.matmul(
                            out=dkb[:, 512 * half:512 * half + 512], lhsT=ones3[:],
                            rhs=dflv[:, w, half, :, :],
                            start=False, stop=True))
                    P.inc_last("tensor", "st")
                    if w == 1:
                        stdflFree = P.val("st")
                    P.wait("scalar", "st")
                    P.wait("scalar", "sv", vtok)
                    if w == 1:
                        P.wait("scalar", "sv", veff_w[0])
                    for j in range(8):
                        sg_ = BLK[8 * w + j]
                        P.emit("scalar", lambda e, j=j, sg_=sg_, dkb=dkb: e.activation(
                            out=ptb[:, 128 * j:128 * j + 128], in_=dkb[:, 128 * j:128 * j + 128],
                            func=AF.Exp, bias=atok[:, sg_:sg_ + 1]))
                    P.emit("scalar", lambda e: e.copy(out=ptb[0:1, 0:1], in_=ptb[0:1, 0:1]),
                           inc="ss")
                    P.wait("vector", "ss")
                    if w == 0:
                        P.emit("vector", lambda e, w=w: e.tensor_tensor(
                            out=efb[w][:], in0=pSc[:], in1=ptb[:], op=ALU.mult), inc="sv")
                    else:
                        P.emit("vector", lambda e, w=w: e.tensor_tensor(
                            out=efb[w][:, 0:512], in0=pMi[:, 0:512], in1=ptb[:, 0:512],
                            op=ALU.mult))
                        P.emit("vector", lambda e, w=w: e.tensor_tensor(
                            out=efb[w][:, 512:1024], in0=pTrF[:, 0:512],
                            in1=ptb[:, 512:1024], op=ALU.mult), inc="sv")
                    veff = P.val("sv")
                    veff_w[w] = veff
                    if debug and t == 0 and ch == 0 and w == 0:
                        P.wait("sync", "sv", veff)
                        P.dma("sync", lambda e: e.dma_start(out=dbg["dbg_eff"], in_=efb[0][:]),
                              "dF")
                    P.wait("tensor", "sv", veff)
                    for j in range(8):
                        b = 8 * w + j
                        P.emit("tensor", lambda e, w=w, j=j, b=b, ch=ch: e.matmul(
                            out=sc_blk(w, j, 65),
                            lhsT=efb[w][:, 128 * j:128 * j + 128],
                            rhs=vtm[ch][:, 65 * b:65 * b + 65], start=True, stop=True))
                    for cp in range(4):
                        c = 4 * w + cp
                        for half in range(2):
                            b0, j = 64 * half, cp + 4 * half
                            b = 8 * w + j
                            P.emit("tensor", lambda e, c=c, b0=b0, j=j, b=b, ch=ch, dkb=dkb:
                                   e.matmul(
                                out=dkb[:, 128 * j:128 * j + 65],
                                lhsT=qfm[b0:b0 + 64, 512 * c + 128 * ch:512 * c + 128 * ch + 128],
                                rhs=S_bf[b0:b0 + 64, 128 * b:128 * b + 65],
                                start=True, stop=True))
                    P.inc_last("tensor", "st")
                    P.wait("vector", "st")
                    P.emit("vector", lambda e, w=w, dkb=dkb: e.tensor_tensor(
                        out=tcb[:].rearrange("p (hi lo o) -> p hi lo o", hi=2, lo=4),
                        in0=dkb[:].rearrange("p (hi lo o) -> p hi lo o",
                                             hi=2, lo=4)[:, :, :, 0:65],
                        in1=_slotgrid(editok, w, 65), op=ALU.mult))
                    if w == 0:
                        P.emit("vector", lambda e: e.tensor_add(
                            out=acb[:].rearrange("p (j o) -> p j o", o=65),
                            in0=tcb[:].rearrange("p (j o) -> p j o", o=65),
                            in1=pSc[:].rearrange("p (j o) -> p j o", o=128)[:, :, 0:65]))
                    else:
                        P.emit("vector", lambda e: e.tensor_add(
                            out=acb[:, 0:260].rearrange("p (j o) -> p j o", o=65),
                            in0=tcb[:, 0:260].rearrange("p (j o) -> p j o", o=65),
                            in1=pMi[:].rearrange("p (j o) -> p j o", o=128)[:, :, 0:65]))
                        P.emit("vector", lambda e: e.tensor_add(
                            out=acb[:, 260:520].rearrange("p (j o) -> p j o", o=65),
                            in0=tcb[:, 260:520].rearrange("p (j o) -> p j o", o=65),
                            in1=pTrF[:].rearrange("p (j o) -> p j o", o=128)[:, :, 0:65]))
                    P.vflush(flshb)
                    P.emit("vector", lambda e: e.tensor_single_scalar(
                        out=denb[:], in_=acb[:].rearrange("p (j o) -> p j o", o=65)[:, :, 64:65],
                        scalar=1e-6, op=ALU.add))
                    P.vflush(flshb)
                    P.emit("vector", lambda e: e.reciprocal(out=denb[:], in_=denb[:]))
                    P.vflush(flshb)
                    P.emit("vector", lambda e, w=w: e.tensor_tensor(
                        out=rtm[:, 512 * w:512 * w + 512].rearrange("p (j o) -> p j o", o=64),
                        in0=acb[:].rearrange("p (j o) -> p j o", o=65)[:, :, 0:64],
                        in1=denb[:].unsqueeze(2).broadcast_to([128, 8, 64]),
                        op=ALU.mult), inc="sv")
                    vcomb_w[w] = P.val("sv")
                vcomb1Prev = vcomb_w[1]
                P.wait("tensor", "sv", vcomb_w[0])
                for b in range(16):
                    hh = BLK[b]
                    pout, j = (pSc, b) if b < 8 else (pDk, b - 8)
                    P.emit("tensor", lambda e, hh=hh, pout=pout, j=j, b=b, ch=ch: e.matmul(
                        out=pout[0:64, 128 * j:128 * j + 65],
                        lhsT=khat[:, 64 * hh:64 * hh + 64],
                        rhs=vtm[ch][:, 65 * b:65 * b + 65], start=True, stop=True))
                P.inc_last("tensor", "st")
                P.wait("vector", "st")
                for w in range(2):
                    pin = pSc if w == 0 else pDk
                    sv_ = S_f[:, 1024 * w:1024 * w + 1024].rearrange(
                        "p (hi lo o) -> p hi lo o", hi=2, lo=4)[:, :, :, 0:65]
                    P.emit("vector", lambda e, sv_=sv_, w=w: e.tensor_tensor(
                        out=sv_, in0=sv_, in1=_slotgrid(lamsb, w, 65, 64), op=ALU.mult))
                    P.emit("vector", lambda e, sv_=sv_, pin=pin: e.tensor_add(
                        out=sv_, in0=sv_,
                        in1=pin[0:64, :].rearrange("p (hi lo o) -> p hi lo o",
                                                   hi=2, lo=4)[:, :, :, 0:65]))
                P.emit("vector", lambda e: e.tensor_copy(out=S_bf[0:64, :], in_=S_f[:]),
                       inc="sv")
                vsup = P.val("sv")
                vsupPrev = vsup
                P.wait("scalar", "sv", vsup)
                P.dma("scalar", lambda e: e.dma_start(out=S_bf[64:128, :], in_=S_bf[0:64, :]),
                      "dF")
                fS = P.val("dF")
                if debug and t == 0 and ch == 0:
                    P.wait("sync", "sv", vsup)
                    P.dma("sync", lambda e: e.dma_start(out=dbg["dbg_S"], in_=S_f[:]), "dF")
                    P.dma("sync", lambda e: e.dma_start(out=dbg["dbg_R"], in_=rtm[:]), "dF")

                P.wait("tensor", "sv", vcomb_w[1])
                for fc in range(8):
                    P.emit("tensor", lambda e, fc=fc: e.transpose(
                        out=pTr[:, 128 * fc:128 * fc + 128], in_=rtm[:, 128 * fc:128 * fc + 128],
                        identity=idbf[:]))
                P.inc_last("tensor", "st")
                strt = P.val("st")
                P.wait("scalar", "st", strt)
                P.emit("scalar", lambda e: e.copy(out=rfm[:], in_=pTr[:]), inc="ss")
                ssrfmLast = P.val("ss")
                P.wait("tensor", "ss", ssrfmLast)
                for fc in range(8):
                    for hf in range(2):
                        P.emit("tensor", lambda e, fc=fc, hf=hf: e.matmul(
                            out=pAcc[:, 512 * hf:512 * hf + 512],
                            lhsT=rfm[:, 128 * fc:128 * fc + 128],
                            rhs=wo_bf[:, 1024 * fc + 512 * hf:1024 * fc + 512 * hf + 512],
                            start=(fc == 0), stop=(fc == 7)))
                P.inc_last("tensor", "st")
                P.wait("vector", "st")
                ob = out_sb[ch % 2]
                P.wait("vector", "dO", laststore[ch % 2])
                P.emit("vector", lambda e, ob=ob, ch=ch: e.scalar_tensor_tensor(
                    out=ob[:], in0=x_sb[:, 1024 * ch:1024 * ch + 1024],
                    scalar=colsA[:, 8 * ch:8 * ch + 1], op0=ALU.add,
                    in1=pAcc[:], op1=ALU.add), inc="sv")
                vout = P.val("sv")
                P.wait("sync", "sv", vout)
                P.dma("sync", lambda e, t=t, ch=ch, ob=ob: e.dma_start(
                    out=y_d[TILE * t + 128 * ch:TILE * t + 128 * ch + 128, :], in_=ob[:]), "dO")
                laststore[ch % 2] = P.val("dO")
            vPrevTileDone = P.val("sv")
            tPrevVcons = P.val("st")

        endA_sv, endA_ss, endA_st = P.val("sv"), P.val("ss"), P.val("st")

        _bcur = [phase_base]
        _boff = {}

        def sbB(name, shape, dt=F32, at=None):
            esz = {F32: 4, BF16: 2, I32: 4}[dt]
            nb = shape[1] * esz
            off = _bcur[0] if at is None else at
            h = nc.alloc_sbuf_tensor_at(name, shape, dt, offset=off)
            _boff[name] = off
            if at is None:
                _bcur[0] = off + ((nb + 31) // 32) * 32
            return h

        w1q = sbB("w1q", [128, 32768], BF16)
        w2q = sbB("w2q", [128, 32768], BF16)
        xqB = [sbB(f"xqB{i}", [128, 4096], BF16) for i in range(2)]
        opB2 = [sbB(f"opB{i}", [128, 1024]) for i in range(2)]
        hsq = sbB("hsq", [128, 16384], BF16)
        hb = [sbB(f"hb{i}", [128, 514]) for i in range(3)]
        hsave = sbB("hsave", [128, 64])
        maccD = sbB("maccD", [128, 512])
        maccN = sbB("maccN", [128, 512])
        s2b = sbB("s2b", [128, 512], F32, at=_boff["maccD"])
        aB = [sbB("aB0", [128, 512])] * 2
        tiB = [sbB("tiB0", [128, 512], I32)] * 2
        convW = sbB("convW", [128, 96])
        convB = sbB("convB", [128, 32])
        colsB = sbB("colsB", [128, 16])
        rowB = sbB("rowB", [1, 512])
        yrow = sbB("yrow", [1, 1024])
        tfB = [sbB("tfB0", [128, 512], F32, at=_boff["rowB"])]
        cB = [sbB("cB0", [128, 512], F32, at=_boff["yrow"])]
        scl = sbB("scl", [1, 8])
        gfb = sbB("gfb", [128, 2])
        epf = sbB("epf", [128, 128], F32, at=_boff["aB0"])
        epb = sbB("epb", [128, 32], BF16, at=_boff["aB0"] + 512)
        epi32 = sbB("epi32", [128, 32], I32, at=_boff["aB0"] + 576)
        wstr = [sbB("wstr0", [128, 4096], F32, at=_boff["hsq"]),
                sbB("wstr1", [128, 4096], F32, at=_boff["hsq"] + 16384)]

        for e in ENGS:
            P.wait(e, "sv", endA_sv)
            P.wait(e, "ss", endA_ss)
            P.wait(e, "st", endA_st)
            P.wait(e, "dO")
            P.wait(e, "dQ")
            P.wait(e, "dF")
            P.wait(e, "dG")
        P.dma("sync", lambda e: e.dma_start(out=convW[:], in_=cw_d), "dW")
        P.dma("sync", lambda e: e.dma_start(out=convB[:], in_=cb_d), "dW")
        wB0 = P.val("dW")
        P.wait("vector", "dW", wB0)
        P.emit("vector", lambda e: e.memset(colsB[:, 3:4], 1.0))
        P.emit("vector", lambda e: e.memset(opB2[0][0:1, :], 0.0))
        P.emit("vector", lambda e: e.memset(hsave[:], 0.0), inc="sv")
        P.emit("gpsimd", lambda e: e.memset(gfb[:], 0.0))

        WQS = (("sync", "dW"), ("scalar", "dQ"), ("gpsimd", "dG"))

        def wquant(src_d, nch, chw, dstq, colix, bufs):
            nb = len(bufs)
            use_done = [0] * nb
            for i in range(nch):
                eng, sem = WQS[i % len(WQS)]
                P.wait(eng, "sv", use_done[i % nb])
                P.dma(eng, lambda e, i=i, src_d=src_d, chw=chw, b=bufs[i % nb]:
                      e.dma_start(out=b[:, 0:chw],
                                  in_=src_d[128 * i:128 * i + 128, :]), sem)
                P.wait("vector", sem)
                P.emit("vector", lambda e, i=i, chw=chw, b=bufs[i % nb]: e.tensor_reduce(
                    out=maccD[:, i % 4:i % 4 + 1], in_=b[:, 0:chw],
                    axis=AX.X, op=ALU.add, apply_absolute_value=True), inc="sv")
                use_done[i % nb] = P.val("sv")
                P.vflush(flshb)
                if i == 0:
                    P.emit("vector", lambda e: e.tensor_copy(
                        out=maccD[:, 4:5], in_=maccD[:, 0:1]), inc="sv")
                else:
                    P.emit("vector", lambda e, i=i: e.tensor_add(
                        out=maccD[:, 4:5], in0=maccD[:, 4:5],
                        in1=maccD[:, i % 4:i % 4 + 1]), inc="sv")
            P.wait("tensor", "sv")
            P.emit("tensor", lambda e: e.matmul(out=pMi[0:1, 0:1], lhsT=maccD[:, 4:5],
                                                rhs=colsB[:, 3:4], start=True, stop=True),
                   inc="st")
            P.wait("vector", "st")
            P.emit("vector", lambda e: e.tensor_single_scalar(
                out=scl[0:1, 0:1], in_=pMi[0:1, 0:1], scalar=1.0 / (4096.0 * 1024.0),
                op=ALU.mult))
            P.vflush(flshb)
            P.emit("vector", lambda e: e.tensor_scalar_max(out=scl[0:1, 0:1],
                                                           in0=scl[0:1, 0:1], scalar1=1e-5))
            P.vflush(flshb)
            P.emit("vector", lambda e: e.reciprocal(out=scl[0:1, 1:2], in_=scl[0:1, 0:1]))
            P.vflush(flshb, inc="sv")
            P.wait("tensor", "sv")
            P.emit("tensor", lambda e: e.matmul(out=pMi[:, 4:5], lhsT=ones1[0:1, :],
                                                rhs=scl[0:1, 1:2], start=True, stop=True),
                   inc="st")
            P.wait("vector", "st")
            P.emit("vector", lambda e, colix=colix: e.tensor_copy(
                out=colsB[:, colix:colix + 1], in_=pMi[:, 4:5]), inc="sv")
            P.vflush(flshb)
            for i in range(nch):
                eng, sem = WQS[i % len(WQS)]
                P.wait(eng, "sv", use_done[i % nb])
                P.dma(eng, lambda e, i=i, src_d=src_d, chw=chw, b=bufs[i % nb]:
                      e.dma_start(out=b[:, 0:chw],
                                  in_=src_d[128 * i:128 * i + 128, :]), sem)
                P.wait("vector", sem)
                wb = bufs[i % nb]
                P.emit("vector", lambda e, wb=wb, chw=chw, colix=colix: e.tensor_scalar(
                    out=wb[:, 0:chw], in0=wb[:, 0:chw], scalar1=colsB[:, colix:colix + 1],
                    scalar2=4.5, op0=ALU.mult, op1=ALU.add))
                for s0 in range(0, chw, 512):
                    P.emit("vector", lambda e, wb=wb, s0=s0: e.tensor_copy(
                        out=tiB[0][:], in_=wb[:, s0:s0 + 512]))
                    P.emit("vector", lambda e, wb=wb, s0=s0: e.tensor_tensor(
                        out=cB[0][:], in0=tiB[0][:], in1=wb[:, s0:s0 + 512], op=ALU.is_gt))
                    P.emit("vector", lambda e, wb=wb, s0=s0: e.tensor_tensor(
                        out=wb[:, s0:s0 + 512], in0=tiB[0][:], in1=cB[0][:], op=ALU.subtract))
                P.emit("vector", lambda e, wb=wb, chw=chw: e.tensor_scalar(
                    out=wb[:, 0:chw], in0=wb[:, 0:chw], scalar1=-4.0, scalar2=1.0,
                    op0=ALU.add, op1=ALU.min))
                P.emit("vector", lambda e, wb=wb, chw=chw, i=i, dstq=dstq: e.tensor_single_scalar(
                    out=dstq[:, chw * i:chw * i + chw], in_=wb[:, 0:chw], scalar=-1.0,
                    op=ALU.max), inc="sv")
                use_done[i % nb] = P.val("sv")

        import os as _os0
        _qv = int(_os0.environ.get("QV", "2"))
        wstr4 = [sbB(f"wst4_{k}", [128, 4096], F32, at=_boff["w2q"] + 16384 * k)
                 for k in range(4)]
        wstr8 = [sbB(f"wst8_{k}", [128, 1024], F32, at=_boff["hsq"] + 4096 * k)
                 for k in range(8)]
        if do_b and _qv >= 1:
            wquant(w1T_d, 8, 4096, w1q, 0, wstr4)
        if do_b and _qv >= 2:
            wquant(w2T_d, 32, 1024, w2q, 1, wstr8)
        P.wait("vector", "sv")
        if not do_b:
            P.emit("vector", lambda e: e.memset(colsB[:], 1.0), inc="sv")
        P.vflush(flshb)
        P.emit("vector", lambda e: e.reciprocal(out=colsB[:, 2:3], in_=colsB[:, 1:2]))
        P.vflush(flshb)
        P.emit("vector", lambda e: e.reciprocal(out=scl[0:1, 3:4], in_=colsB[0:1, 0:1]))
        P.vflush(flshb, inc="sv")
        P.wait("tensor", "sv")
        P.emit("tensor", lambda e: e.matmul(out=pMi[:, 6:7], lhsT=ones1[0:1, :],
                                            rhs=scl[0:1, 3:4], start=True, stop=True), inc="st")
        P.wait("vector", "st")
        P.emit("vector", lambda e: e.tensor_copy(out=colsB[:, 15:16], in_=pMi[:, 6:7]))
        P.vflush(flshb)
        P.emit("vector", lambda e: e.tensor_scalar_mul(out=convW[:], in0=convW[:],
                                                       scalar1=colsB[:, 15:16]), inc="sv")
        if debug:
            P.wait("sync", "sv")
            P.dma("sync", lambda e: e.dma_start(out=dbg["dbg_w1q"], in_=w1q[:, 0:4096]), "dF")

        import os as _osB
        lastout = {0: 0, 1: 0}
        lastopcons = {0: 0, 1: 0}
        lastxqT = {0: 0, 1: 0}

        P.wait("tensor", "sv")
        ssevac = {}
        svconv = {}
        svround = {}
        svstt = {}
        stmacc = {}
        w2banks = [(pAcc, 0), (pAcc, 512), (pSc, 0), (pSc, 512),
                   (pDk, 0), (pDk, 512), (pTrF, 0), (pMi, 0)]
        for t in range(nt if do_b else 0):
            w0 = TILE * t - 1
            xb = xqB[t % 2]
            P.wait("gpsimd", "st", lastxqT[t % 2])
            P.dma("gpsimd", lambda e, t=t, xb=xb: e.dma_start(out=xb[:], in_=xqs_d[t]), "dG")
            ldx = P.val("dG")
            P.wait("tensor", "dG", ldx)
            if t > 0:
                P.wait("tensor", "sv", svstt[(t - 1, 3)])
                P.wait("tensor", "sv", svstt[(t - 1, 1)])

            for hc in range(32):
                slot = hc % 3
                pb = pMi if hc % 2 == 0 else pSc
                if hc >= 2:
                    P.wait("tensor", "ss", ssevac[(t, hc - 2)])
                elif t > 0:
                    P.wait("tensor", "ss", ssevac[(t - 1, 30 + hc)])
                for dc in range(8):
                    P.emit("tensor", lambda e, dc=dc, hc=hc, pb=pb, xb=xb: e.matmul(
                        out=pb[:, 0:512],
                        lhsT=w1q[:, 4096 * dc + 128 * hc:4096 * dc + 128 * hc + 128],
                        rhs=xb[:, 512 * dc:512 * dc + 512], start=(dc == 0), stop=(dc == 7)))
                P.inc_last("tensor", "st")
                stw1 = P.val("st")
                P.wait("scalar", "st", stw1)
                if hc >= 3:
                    P.wait("scalar", "sv", svconv[(t, hc - 3)])
                elif t > 0:
                    P.wait("scalar", "sv", svconv[(t - 1, 29 + hc)])
                P.emit("scalar", lambda e, slot=slot, pb=pb: e.copy(
                    out=hb[slot][:, 2:514], in_=pb[:, 0:512]), inc="ss")
                ssevac[(t, hc)] = P.val("ss")
                P.wait("vector", "ss", ssevac[(t, hc)])
                P.emit("vector", lambda e, slot=slot, hc=hc: e.tensor_copy(
                    out=hb[slot][:, 0:2], in_=hsave[:, 2 * hc:2 * hc + 2]))
                P.emit("vector", lambda e, slot=slot, hc=hc: e.tensor_copy(
                    out=hsave[:, 2 * hc:2 * hc + 2], in_=hb[slot][:, 512:514]))
                P.emit("vector", lambda e, slot=slot, hc=hc: e.tensor_scalar_mul(
                    out=aB[0][:], in0=hb[slot][:, 1:513],
                    scalar1=convW[:, 3 * hc + 1:3 * hc + 2]))
                P.emit("vector", lambda e, slot=slot, hc=hc: e.scalar_tensor_tensor(
                    out=aB[0][:], in0=hb[slot][:, 0:512], scalar=convW[:, 3 * hc:3 * hc + 1],
                    in1=aB[0][:], op0=ALU.mult, op1=ALU.add))
                P.emit("vector", lambda e, slot=slot, hc=hc: e.scalar_tensor_tensor(
                    out=aB[0][:], in0=hb[slot][:, 2:514],
                    scalar=convW[:, 3 * hc + 2:3 * hc + 3],
                    in1=aB[0][:], op0=ALU.mult, op1=ALU.add), inc="sv")
                svconv[(t, hc)] = P.val("sv")
                P.wait("scalar", "sv", svconv[(t, hc)])
                P.emit("scalar", lambda e, hc=hc: e.activation(
                    out=hsq[:, 512 * hc:512 * hc + 512], in_=aB[0][:], func=AF.Silu,
                    bias=convB[:, hc:hc + 1]), inc="ss")
                sssilu = P.val("ss")
                P.wait("vector", "ss", sssilu)
                if hc == 0:
                    if t > 0:
                        P.wait("vector", "st", stmacc[t - 1])
                    P.emit("vector", lambda e, hc=hc: e.tensor_copy(
                        out=maccD[:], in_=hsq[:, 512 * hc:512 * hc + 512]))
                    P.emit("vector", lambda e, hc=hc: e.tensor_copy(
                        out=maccN[:], in_=hsq[:, 512 * hc:512 * hc + 512]), inc="sv")
                else:
                    P.emit("vector", lambda e, hc=hc: e.tensor_tensor(
                        out=maccD[:], in0=maccD[:], in1=hsq[:, 512 * hc:512 * hc + 512],
                        op=ALU.max))
                    P.emit("vector", lambda e, hc=hc: e.tensor_tensor(
                        out=maccN[:], in0=maccN[:], in1=hsq[:, 512 * hc:512 * hc + 512],
                        op=ALU.min), inc="sv")
            lastxqT[t % 2] = P.val("st")
            P.emit("vector", lambda e: e.tensor_scalar_mul(out=maccN[:], in0=maccN[:],
                                                           scalar1=-1.0))
            P.vflush(flshb)
            P.emit("vector", lambda e: e.tensor_max(out=maccD[:], in0=maccD[:],
                                                    in1=maccN[:]), inc="sv")
            svmacc = P.val("sv")

            P.wait("tensor", "sv", svmacc)
            for su in range(4):
                P.emit("tensor", lambda e, su=su: e.transpose(
                    out=pMi[:, 128 * su:128 * su + 128],
                    in_=maccD[:, 128 * su:128 * su + 128], identity=idf32[:]))
            P.inc_last("tensor", "st")
            stmacc[t] = P.val("st")
            P.wait("vector", "st", stmacc[t])
            for su in range(4):
                P.emit("vector", lambda e, su=su: e.reduce_max(
                    out=colsB[:, 4 + su:5 + su], in_=pMi[:, 128 * su:128 * su + 128],
                    axis=AX.X), inc="sv")
            P.vflush(flshb)
            P.emit("vector", lambda e: e.tensor_scalar_max(
                out=colsB[:, 4:8], in0=colsB[:, 4:8], scalar1=1e-5))
            for su in range(4):
                P.vflush(flshb)
                P.emit("vector", lambda e, su=su: e.reciprocal(
                    out=colsB[:, 12 + su:13 + su], in_=colsB[:, 4 + su:5 + su]))
                P.vflush(flshb)
            P.emit("vector", lambda e: e.tensor_scalar_mul(
                out=colsB[:, 12:16], in0=colsB[:, 12:16], scalar1=127.0), inc="sv")
            P.wait("tensor", "sv")
            for su in range(4):
                P.emit("tensor", lambda e, su=su: e.transpose(
                    out=pSc[0:1, 128 * su:128 * su + 128], in_=colsB[:, 12 + su:13 + su],
                    identity=idf32[:]))
            P.inc_last("tensor", "st")
            P.wait("vector", "st")
            P.emit("vector", lambda e: e.tensor_copy(out=rowB[0:1, :], in_=pSc[0:1, 0:512]),
                   inc="sv")
            P.wait("tensor", "sv")
            P.emit("tensor", lambda e: e.matmul(out=pSc[:, 512:1024], lhsT=ones1[0:1, :],
                                                rhs=rowB[0:1, :], start=True, stop=True),
                   inc="st")
            P.wait("vector", "st")
            P.emit("vector", lambda e: e.tensor_copy(out=s2b[:], in_=pSc[:, 512:1024]))
            P.emit("vector", lambda e: e.tensor_scalar(
                out=colsB[:, 8:12], in0=colsB[:, 4:8], scalar1=colsB[:, 2:3],
                scalar2=1.0 / 127.0, op0=ALU.mult, op1=ALU.mult), inc="sv")
            vs2b = P.val("sv")

            P.wait("vector", "sv", vs2b)
            for hc in range(32):
                hsl = hsq[:, 512 * hc:512 * hc + 512]
                P.emit("vector", lambda e, hsl=hsl: e.tensor_tensor(
                    out=tfB[0][:], in0=hsl, in1=s2b[:], op=ALU.mult))
                P.emit("vector", lambda e: e.tensor_single_scalar(
                    out=cB[0][:], in_=tfB[0][:], scalar=0.5, op=ALU.add))
                P.emit("vector", lambda e: e.tensor_copy(
                    out=tiB[0][:], in_=cB[0][:]))
                P.emit("vector", lambda e: e.tensor_tensor(
                    out=tfB[0][:], in0=tiB[0][:], in1=cB[0][:], op=ALU.is_gt))
                P.emit("vector", lambda e, hsl=hsl: e.tensor_tensor(
                    out=hsl, in0=tiB[0][:], in1=tfB[0][:], op=ALU.subtract), inc="sv")
                svround[(t, hc)] = P.val("sv")
                P.wait("tensor", "sv", svround[(t, hc)])
                for su in range(4):
                    for hf in range(2):
                        pbank, off = w2banks[2 * su + hf]
                        P.emit("tensor", lambda e, su=su, hc=hc, hf=hf, pbank=pbank,
                               off=off: e.matmul(
                            out=pbank[:, off:off + 512],
                            lhsT=hsq[:, 512 * hc + 128 * su:512 * hc + 128 * su + 128],
                            rhs=w2q[:, 1024 * hc + 512 * hf:1024 * hc + 512 * hf + 512],
                            start=(hc == 0), stop=(hc == 31)))
            P.inc_last("tensor", "st")
            stw2 = P.val("st")
            for su in range(4):
                obi = su % 2
                r0 = w0 + 128 * su
                P.wait("sync", "sv", lastopcons[obi])
                if r0 < 0:
                    P.dma("sync", lambda e, obi=obi: e.dma_start(
                        out=opB2[obi][1:128, :], in_=y_d[0:127, :]), "dO")
                else:
                    P.dma("sync", lambda e, r0=r0, obi=obi: e.dma_start(
                        out=opB2[obi][:], in_=y_d[r0:r0 + 128, :]), "dO")
                ldop = P.val("dO")
                P.wait("vector", "st", stw2)
                P.wait("vector", "dO", max(ldop, lastout[obi]))
                b0, o0 = w2banks[2 * su]
                b1, o1 = w2banks[2 * su + 1]
                P.emit("vector", lambda e, su=su, obi=obi, b0=b0, o0=o0: e.scalar_tensor_tensor(
                    out=opB2[obi][:, 0:512], in0=b0[:, o0:o0 + 512],
                    scalar=colsB[:, 8 + su:9 + su],
                    op0=ALU.mult, in1=opB2[obi][:, 0:512], op1=ALU.add))
                P.emit("vector", lambda e, su=su, obi=obi, b1=b1, o1=o1: e.scalar_tensor_tensor(
                    out=opB2[obi][:, 512:1024], in0=b1[:, o1:o1 + 512],
                    scalar=colsB[:, 8 + su:9 + su],
                    op0=ALU.mult, in1=opB2[obi][:, 512:1024], op1=ALU.add),
                    inc="sv")
                svstt[(t, su)] = P.val("sv")
                lastopcons[obi] = P.val("sv")
                vob = P.val("sv")
                P.wait("sync", "sv", vob)
                if r0 < 0:
                    P.dma("sync", lambda e, obi=obi: e.dma_start(
                        out=y_d[0:127, :], in_=opB2[obi][1:128, :]), "dO")
                else:
                    P.dma("sync", lambda e, r0=r0, obi=obi: e.dma_start(
                        out=y_d[r0:r0 + 128, :], in_=opB2[obi][:]), "dO")
                lastout[obi] = P.val("dO")

        for _pn in range(int(_osB.environ.get("PADN", "0"))):
            P.emit("tensor", lambda e: e.nop())
        import os as _os2
        do_epi = _os2.environ.get("EPI", "1") == "1" and do_b
        cw3 = convW[:].rearrange("p (c k) -> p c k", k=3)
        if do_epi:
            hs2 = hsave[:].rearrange("p (c k) -> p c k", k=2)
            P.wait("vector", "sv")
            P.emit("vector", lambda e: e.tensor_tensor(
                out=epf[:, 0:32].unsqueeze(2), in0=hs2[:, :, 0:1],
                in1=cw3[:, :, 0:1], op=ALU.mult))
            P.emit("vector", lambda e: e.tensor_tensor(
                out=epf[:, 32:64].unsqueeze(2), in0=hs2[:, :, 1:2],
                in1=cw3[:, :, 1:2], op=ALU.mult))
            P.vflush(flshb)
            P.emit("vector", lambda e: e.tensor_add(out=epf[:, 0:32], in0=epf[:, 0:32],
                                                    in1=epf[:, 32:64]))
            P.vflush(flshb)
            P.emit("vector", lambda e: e.tensor_add(out=epf[:, 0:32], in0=epf[:, 0:32],
                                                    in1=convB[:]), inc="sv")
            P.wait("scalar", "sv")
            P.emit("scalar", lambda e: e.activation(out=epf[:, 32:64], in_=epf[:, 0:32],
                                                    func=AF.Silu), inc="ss")
            P.wait("vector", "ss")
            P.emit("vector", lambda e: e.reduce_max(out=epf[:, 64:65], in_=epf[:, 32:64],
                                                    axis=AX.X, apply_absolute_value=True),
                   inc="sv")
            P.vflush(flshb)
            P.wait("tensor", "sv")
            P.emit("tensor", lambda e: e.transpose(out=pMi[0:1, 0:128], in_=epf[:, 64:65],
                                                   identity=idf32[:]), inc="st")
            P.wait("vector", "st")
            P.emit("vector", lambda e: e.reduce_max(out=scl[0:1, 4:5], in_=pMi[0:1, 0:128],
                                                    axis=AX.X), inc="sv")
            P.vflush(flshb)
            P.emit("vector", lambda e: e.tensor_scalar_max(out=scl[0:1, 4:5], in0=scl[0:1, 4:5],
                                                           scalar1=1e-5))
            P.vflush(flshb)
            P.emit("vector", lambda e: e.reciprocal(out=scl[0:1, 5:6], in_=scl[0:1, 4:5]))
            P.vflush(flshb)
            P.emit("vector", lambda e: e.tensor_single_scalar(out=scl[0:1, 5:6], in_=scl[0:1, 5:6],
                                                              scalar=127.0, op=ALU.mult))
            P.emit("vector", lambda e: e.tensor_single_scalar(out=scl[0:1, 6:7], in_=scl[0:1, 4:5],
                                                              scalar=1.0 / 127.0, op=ALU.mult))
            P.vflush(flshb)
            P.emit("vector", lambda e: e.tensor_mul(out=scl[0:1, 6:7], in0=scl[0:1, 6:7],
                                                    in1=colsB[0:1, 2:3]), inc="sv")
            P.wait("tensor", "sv")
            P.emit("tensor", lambda e: e.matmul(out=pMi[:, 2:3], lhsT=ones1[0:1, :],
                                                rhs=scl[0:1, 5:6], start=True, stop=True),
                   inc="st")
            P.wait("vector", "st")
            P.emit("vector", lambda e: e.tensor_copy(out=colsB[:, 12:13], in_=pMi[:, 2:3]))
            P.vflush(flshb)
            P.emit("vector", lambda e: e.tensor_scalar(
                out=epf[:, 64:96], in0=epf[:, 32:64], scalar1=colsB[:, 12:13], scalar2=0.5,
                op0=ALU.mult, op1=ALU.add))
            P.vflush(flshb)
            P.emit("vector", lambda e: e.tensor_copy(out=epi32[:], in_=epf[:, 64:96]))
            P.vflush(flshb)
            P.emit("vector", lambda e: e.tensor_tensor(out=epf[:, 96:128], in0=epi32[:],
                                                       in1=epf[:, 64:96], op=ALU.is_gt))
            P.vflush(flshb)
            P.emit("vector", lambda e: e.tensor_tensor(out=epb[:], in0=epi32[:],
                                                       in1=epf[:, 96:128], op=ALU.subtract),
                   inc="sv")
            vepq = P.val("sv")
            P.dma("sync", lambda e: e.dma_start(out=yrow[0:1, :], in_=y_d[4095:4096, :]), "dO")
            ldyr = P.val("dO")
            P.wait("tensor", "sv", vepq)
            for oc in range(8):
                for hc in range(32):
                    P.emit("tensor", lambda e, hc=hc, oc=oc: e.matmul(
                        out=pAcc[:, oc:oc + 1],
                        lhsT=w2q[:, 1024 * hc + 128 * oc:1024 * hc + 128 * oc + 128],
                        rhs=epb[:, hc:hc + 1], start=(hc == 0), stop=(hc == 31)))
            P.inc_last("tensor", "st")
            P.wait("vector", "st")
            P.emit("vector", lambda e: e.tensor_copy(out=epf[:, 0:8], in_=pAcc[:, 0:8]),
                   inc="sv")
            P.wait("tensor", "sv")
            P.emit("tensor", lambda e: e.transpose(out=pMi[0:8, 0:128], in_=epf[:, 0:8],
                                                   identity=idf32[:]), inc="st")
            P.wait("vector", "st")
            P.emit("vector", lambda e: e.tensor_copy(out=epf[0:8, 0:128],
                                                     in_=pMi[0:8, 0:128]), inc="sv")
            P.wait("scalar", "sv")
            P.dma("scalar", lambda e: e.dma_start(out=opB2[1][0:1, 0:1024],
                                                  in_=epf[0:8, 0:128]), "dF")
            ldfl = P.val("dF")
            P.wait("vector", "dF", ldfl)
            P.wait("vector", "dO", ldyr)
            P.emit("vector", lambda e: e.scalar_tensor_tensor(
                out=yrow[0:1, :], in0=opB2[1][0:1, 0:1024], scalar=scl[0:1, 6:7],
                op0=ALU.mult, in1=yrow[0:1, :], op1=ALU.add), inc="sv")
            P.wait("sync", "sv")
            P.dma("sync", lambda e: e.dma_start(out=y_d[4095:4096, :], in_=yrow[0:1, :]), "dO")
        P.wait("sync", "dO")
        P.wait("sync", "dX")
        P.wait("sync", "dW")
        P.wait("sync", "dQ")
        P.wait("sync", "dF")

        with nc.Block() as block:
            @block.sync
            def _(e):
                for fn in P.ev["sync"]:
                    fn(e)

            @block.vector
            def _(e):
                for fn in P.ev["vector"]:
                    fn(e)

            @block.scalar
            def _(e):
                for fn in P.ev["scalar"]:
                    fn(e)

            @block.tensor
            def _(e):
                for fn in P.ev["tensor"]:
                    fn(e)

            @block.gpsimd
            def _(e):
                for fn in P.ev["gpsimd"]:
                    fn(e)

    return nc


def _host_inputs(inputs):
    f32 = np.float32
    x = np.ascontiguousarray(inputs["L0_attn_x"], dtype=f32)
    perm = np.concatenate([np.arange(64 * b, 64 * b + 64) for b in BLK])
    base = {
        "wqT": np.ascontiguousarray(inputs["L0_attn_Wq"].T, dtype=f32),
        "wkT": np.ascontiguousarray(inputs["L0_attn_Wk"].T, dtype=f32),
        "wvT": np.ascontiguousarray(inputs["L0_attn_Wv"].T[:, perm], dtype=f32),
        "woT": np.ascontiguousarray(inputs["L0_attn_Wo"].T[perm, :], dtype=f32),
        "wgT": np.ascontiguousarray(
            np.concatenate([inputs["L0_attn_Ww"].T, inputs["L0_attn_We"].T], axis=1), dtype=f32),
        "bg": np.ascontiguousarray(
            np.stack([inputs["L0_attn_bw"], inputs["L0_attn_be"]], axis=1), dtype=f32),
        "w1T": np.ascontiguousarray(inputs["L0_ffn_w1"].T, dtype=f32),
        "w2T": np.ascontiguousarray(inputs["L0_ffn_w2"].T, dtype=f32),
        "cw": np.ascontiguousarray(
            inputs["L0_ffn_conv_w"].reshape(32, 128, 3).transpose(1, 0, 2).reshape(128, 96),
            dtype=f32),
        "cb": np.ascontiguousarray(
            inputs["L0_ffn_conv_b"].reshape(32, 128).T, dtype=f32),
        "idbf": np.eye(128, dtype=ml_dtypes.bfloat16),
        "idf32": np.eye(128, dtype=f32),
        "mask4": np.ascontiguousarray(
            np.tile(np.tril(np.full((128, 128), -1.0e30, dtype=f32), -1), (1, 4))
            .astype(ml_dtypes.bfloat16)),
    }
    return x, base


def kernel(**inputs):
    x, base = _host_inputs(inputs)
    nc = build_nc(debug=False)
    in_maps = [dict(base, x=np.ascontiguousarray(x[c])) for c in range(8)]
    res = run_bass_kernel_spmd(nc, in_maps, core_ids=list(range(8)))
    out = np.stack([np.asarray(r["y"]) for r in res.results], axis=0)
    return out.astype(np.float32)
